# version 25
# baseline (speedup 1.0000x reference)
import numpy as np
import ml_dtypes
import concourse.bass as bass
import concourse.mybir as mybir
from concourse.bass_utils import run_bass_kernel_spmd

F32 = mybir.dt.float32
F16 = mybir.dt.float16
BF16 = mybir.dt.bfloat16
I32 = mybir.dt.int32
AF = mybir.ActivationFunctionType
OP = mybir.AluOpType
LN2 = float(np.log(2.0))
LNHALF = float(np.log(0.5))

T = 2048
CDIM = 2048
IDIM = 5632
TCH = 512
CT = CDIM // 128
IT = IDIM // 128
GU_UNITS = 2 * IT
RECIP_C = 1.0 / 2048.0

ENGS = ("pe", "dve", "act", "sync", "gp")

_PERM = np.empty(CDIM, np.int64)
for _w in range(256):
    for _j in range(8):
        _PERM[8 * _w + _j] = _j * 256 + _w
ORDER = np.argsort(_PERM)
_PERM_I = np.empty(IDIM, np.int64)
for _w in range(704):
    for _j in range(8):
        _PERM_I[8 * _w + _j] = _j * 704 + _w
ORDER_I = np.argsort(_PERM_I)


class Sem:
    __slots__ = ("name", "planned", "h")

    def __init__(self, name):
        self.name = name
        self.planned = 0
        self.h = None


class Buf:
    __slots__ = ("name", "writer", "readers")

    def __init__(self, name):
        self.name = name
        self.writer = None
        self.readers = []


class PlOp:
    __slots__ = ("fn", "inc")

    def __init__(self, fn):
        self.fn = fn
        self.inc = None


class Builder:
    def __init__(self):
        self.plan = {e: [] for e in ENGS}
        self.sems = {}
        self.hw = {e: {} for e in ENGS}
        self.epoch = {e: 0 for e in ENGS}
        self.drain_at = {e: -1 for e in ENGS}
        self.cur_group = None

    def sem(self, name):
        s = Sem(name)
        self.sems[name] = s
        return s

    def _wait(self, eng, sem, val):
        if val <= 0:
            return
        if val <= self.hw[eng].get(sem.name, 0):
            return
        assert val <= sem.planned, (
            f"wait on {sem.name}={val} but only {sem.planned} planned (eng {eng})")
        self.hw[eng][sem.name] = val
        self.plan[eng].append(PlOp(lambda E, s=sem, v=val: E.wait_ge(s.h, v)))

    def _drain(self, eng):
        self.plan[eng].append(PlOp(lambda E: E.drain()))
        self.drain_at[eng] = self.epoch[eng]

    def group(self, eng, sem, n=1):
        assert self.cur_group is None
        self.cur_group = [eng, sem, n, None]
        return _GroupCtx(self)

    def _edge(self, eng, dep, this_is_dma):
        deng, dsem, dval, dep_, disdma = dep
        if deng == eng and not disdma and not this_is_dma:
            if dep_ > self.drain_at[eng] and eng in ("dve", "act", "gp"):
                self._drain(eng)
        else:
            assert dsem is not None, f"edge from {deng} has no milestone"
            self._wait(eng, dsem, dval)

    def op(self, eng, fn, reads=(), writes=()):
        g = self.cur_group
        assert g is not None and g[0] == eng, f"op on {eng} outside its group"
        sem, n = g[1], g[2]
        is_dma = (n == 16)
        for b in reads:
            if b.writer is not None:
                self._edge(eng, b.writer, is_dma)
        for b in writes:
            for r in b.readers:
                self._edge(eng, r, is_dma)
            if b.writer is not None:
                w = b.writer
                if w[0] == eng and not w[4] and not is_dma:
                    pass
                else:
                    self._edge(eng, w, is_dma)
        self.epoch[eng] += 1
        ep = self.epoch[eng]
        o = PlOp(fn)
        self.plan[eng].append(o)
        g[3] = o
        val_after = sem.planned + n
        for b in writes:
            b.writer = (eng, sem, val_after, ep, is_dma)
            b.readers = []
        for b in reads:
            b.readers.append((eng, sem, val_after, ep, is_dma))

    def close_group(self):
        eng, sem, n, last = self.cur_group
        assert last is not None, "empty group"
        sem.planned += n
        last.inc = (sem, n)
        self.cur_group = None

    def raw_wait(self, eng, sem, val):
        assert self.cur_group is None
        self._wait(eng, sem, val)

    def bare(self, eng, fn):
        assert self.cur_group is None
        self.plan[eng].append(PlOp(fn))


class _GroupCtx:
    def __init__(self, b):
        self.b = b

    def __enter__(self):
        return self.b

    def __exit__(self, *a):
        if a[0] is None:
            self.b.close_group()
        return False


def _run_plan(b, eng_name, E):
    for o in b.plan[eng_name]:
        inst = o.fn(E)
        if o.inc is not None:
            sem, n = o.inc
            inst.then_inc(sem.h, n)



def _chunks_fm(x2d):
    nq = x2d.shape[0] // TCH
    out = np.empty((nq, 128, CT * TCH), dtype=x2d.dtype)
    for q in range(nq):
        blk = x2d[q * TCH:(q + 1) * TCH, :]
        bt = np.ascontiguousarray(blk.T)
        out[q] = bt.reshape(CT, 128, TCH).transpose(1, 0, 2).reshape(128, CT * TCH)
    return out


def _unchunk_fm(ych):
    nq = ych.shape[0]
    out = np.empty((nq * TCH, CDIM), dtype=ych.dtype)
    for q in range(nq):
        blk = ych[q].reshape(128, CT, TCH).transpose(1, 0, 2).reshape(CDIM, TCH)
        out[q * TCH:(q + 1) * TCH, :] = blk.T
    return out


def _tile_lhsT(arr):
    K, M = arr.shape
    kt, mt = K // 128, M // 128
    return np.ascontiguousarray(
        arr.reshape(kt, 128, mt, 128).transpose(2, 1, 0, 3).reshape(mt, 128, kt * 128))


def _vec_tile(v):
    return np.ascontiguousarray(v.reshape(CT, 128).T)



def build_nc(T_=T, phases=("f1", "at", "f2"), dbg=None, shard=True):
    NCH = T_ // TCH
    GU_SH = GU_UNITS // 8 if shard else GU_UNITS
    DN_SH = CT // 8 if shard else CT
    nc = bass.Bass("TRN2", target_bir_lowering=False, debug=False)

    x_in = nc.dram_tensor("x_in", [NCH, 128, CT * TCH], F32, kind="ExternalInput").ap()
    gu_pk = nc.dram_tensor("gu_pk", [GU_SH * 128, 256], I32, kind="ExternalInput").ap()
    gu_sc = nc.dram_tensor("gu_sc", [128, GU_SH * 16], F16, kind="ExternalInput").ap()
    dn_pk = nc.dram_tensor("dn_pk", [DN_SH * 128, 704], I32, kind="ExternalInput").ap()
    dn_sc = nc.dram_tensor("dn_sc", [128, DN_SH * 44], F16, kind="ExternalInput").ap()
    wk_t = nc.dram_tensor("wk_t", [CT, 128, CDIM], F32, kind="ExternalInput").ap()
    wv_t = nc.dram_tensor("wv_t", [CT, 128, CDIM], F32, kind="ExternalInput").ap()
    wr_t = nc.dram_tensor("wr_t", [CT, 128, CDIM], F32, kind="ExternalInput").ap()
    wo_t = nc.dram_tensor("wo_t", [CT, 128, CDIM], F32, kind="ExternalInput").ap()
    v_td = nc.dram_tensor("v_td", [128, CT], F32, kind="ExternalInput").ap()
    v_tf = nc.dram_tensor("v_tf", [128, CT], F32, kind="ExternalInput").ap()
    v_tmk = nc.dram_tensor("v_tmk", [128, CT], F32, kind="ExternalInput").ap()
    v_tmv = nc.dram_tensor("v_tmv", [128, CT], F32, kind="ExternalInput").ap()
    v_tmr = nc.dram_tensor("v_tmr", [128, CT], F32, kind="ExternalInput").ap()
    ident_in = nc.dram_tensor("ident_in", [128, 128], BF16, kind="ExternalInput").ap()
    y_out = nc.dram_tensor("y_out", [NCH, 128, CT * TCH], F32, kind="ExternalOutput").ap()
    dbg_out = nc.dram_tensor("dbg_out", [128, CT * TCH], F32, kind="ExternalOutput").ap()
    xbuf = nc.dram_tensor("xbuf", [NCH, 128, CT * TCH], F32).ap()
    wgu_full_t = nc.dram_tensor("wgu_full", [GU_UNITS * 128, CT * 128], BF16)
    wdn_full_t = nc.dram_tensor("wdn_full", [CT * 128, IT * 128], BF16)
    wgu_full = wgu_full_t.ap()
    wdn_full = wdn_full_t.ap()
    if shard:
        cgu_t = nc.dram_tensor("cgu", [GU_SH * 128, CT * 128], BF16)
        cdn_t = nc.dram_tensor("cdn", [DN_SH * 128, IT * 128], BF16)
        cgu, cdn = cgu_t.ap(), cdn_t.ap()
    else:
        cgu, cdn = wgu_full, wdn_full

    from contextlib import ExitStack
    es = ExitStack()
    b = Builder()

    def sb(name, shape, dt):
        return es.enter_context(nc.sbuf_tensor(name, shape, dt))

    with es:
        xchA = sb("xchA", [128, CT * TCH], F32)
        xchB = sb("xchB", [128, CT * TCH], F32)
        ar2 = sb("ar2", [128, 3 * CT * TCH], BF16)
        ar3 = sb("ar3", [128, CT * TCH], F32)
        xnb = sb("xnb", [128, CT * TCH], BF16)
        wgu_s = [sb(f"wgu{i}", [128, CT * 128], BF16) for i in range(3)]
        wdn_s = [sb(f"wdn{i}", [128, IT * 128], BF16) for i in range(2)]
        stats = sb("stats", [128, 512], F32)
        srow = sb("srow", [128, 1536], F32)
        ssg = sb("ssg", [128, 512], F32)
        y2b = sb("y2b", [128, 1024], BF16)
        scf_gu = None
        scf_dn = None
        a_sb = sb("a_sb", [128, CT], F32)
        u_sb = sb("u_sb", [128, CT], F32)
        tmk_sb = sb("tmk_sb", [128, CT], F32)
        tmv_sb = sb("tmv_sb", [128, CT], F32)
        tmr_sb = sb("tmr_sb", [128, CT], F32)
        td_sb = sb("td_sb", [128, CT], F32)
        cu_sb = sb("cu_sb", [128, CT], F32)
        cd_sb = sb("cd_sb", [128, CT], F32)
        bnd_sb = sb("bnd_sb", [128, CT], BF16)
        ident = sb("ident", [128, 128], BF16)
        ones_b = sb("ones_b", [128, 1], BF16)
        ones_f = sb("ones_f", [1, 128], F32)
        bias_ln2n = sb("bias_ln2n", [128, 1], F32)
        bias_one = sb("bias_one", [128, 1], F32)
        bias_eps5 = sb("bias_eps5", [128, 1], F32)
        bias_eps6 = sb("bias_eps6", [128, 1], F32)
        bias_lnh = sb("bias_lnh", [128, 1], F32)
        bias_n6 = sb("bias_n6", [128, 1], F32)
        bias_n2 = sb("bias_n2", [128, 1], F32)
        WB = 768
        dq_pk = [xchA[:, 0:WB].bitcast(I32)[:, 0:WB],
                 xchA[:, WB:2 * WB].bitcast(I32)[:, 0:WB]]
        dq_m3 = [xchA[:, (2 + p) * WB:(3 + p) * WB].bitcast(I32)[:, 0:WB]
                 for p in range(4)]
        dq_si = [xchA[:, (6 + p) * WB:(7 + p) * WB].bitcast(I32)[:, 0:WB]
                 for p in range(4)]
        scf_gu = xchB[:, 0:GU_SH * 16]
        scf_dn = xnb[:, 2112:2112 + 2 * DN_SH * 44].bitcast(F32)[:, 0:DN_SH * 44]
        _o = GU_SH * 16
        def _bcarve(off_f32, n_bf16):
            return xchB[:, off_f32:off_f32 + (n_bf16 + 1) // 2].bitcast(BF16)[:, 0:n_bf16]
        def _bcarve3(off_f32, n_bf16):
            return ar3[:, off_f32:off_f32 + (n_bf16 + 1) // 2].bitcast(BF16)[:, 0:n_bf16]
        dq_h0, dq_r2 = [], []
        for _s in range(8):
            dq_h0.append(_bcarve(_o, WB)); _o += WB // 2
            dq_r2.append(_bcarve(_o, WB)); _o += WB // 2
        assert _o <= 8192, _o
        dq_r1 = [_bcarve3(_s * WB // 2, WB) for _s in range(8)]
        dq_s2m = [_bcarve3(8 * WB // 2, WB), _bcarve3(9 * WB // 2, WB),
                  _bcarve3(7682, WB),
                  xchA[:, 7680:7680 + WB // 2].bitcast(BF16)[:, 0:WB]] + \
                 [xchA[:, (2 + _p) * WB:(2 + _p) * WB + WB // 2]
                  .bitcast(BF16)[:, 0:WB] for _p in range(4)]
        dq_ub = [ar2[:, 0:IT * 128], ar2[:, IT * 128:2 * IT * 128]]
        dq_vb = [ar2[:, 2 * IT * 128:2 * IT * 128 + 6144],
                 ar2[:, 2 * IT * 128 + 6144:2 * IT * 128 + 12288]]

        pb = [es.enter_context(nc.psum_tensor(f"pb{i}", [128, 512], F32)) for i in range(7)]
        ptb = es.enter_context(nc.psum_tensor("ptb", [128, 1024], BF16))

        s_pe = b.sem("s_pe")
        s_dve = b.sem("s_dve")
        s_act = b.sem("s_act")
        s_init = b.sem("s_init")
        s_xch = [b.sem("s_xchA"), b.sem("s_xchB")]
        s_wgu = [b.sem(f"s_wgu{i}") for i in range(3)]
        s_wdn = [b.sem(f"s_wdn{i}") for i in range(2)]
        s_watt = [b.sem(f"s_watt{i}") for i in range(6)]
        s_dqpk = [b.sem("s_dqpkA"), b.sem("s_dqpkB")]
        s_dqst2 = [b.sem("s_dqstA"), b.sem("s_dqstB")]
        s_ptst = [b.sem("s_ptstA"), b.sem("s_ptstB")]
        s_st = [b.sem("s_stA"), b.sem("s_stB")]
        s_cc = b.sem("s_cc")

        B_ = Buf
        bxch = [B_("xchA"), B_("xchB")]
        bxnb = B_("xnb")
        bxnb4 = [B_(f"xnb{i}") for i in range(4)]
        bh = B_("h")
        bxk, bxv, bxr = B_("xk"), B_("xv"), B_("xr")
        bysb = B_("ysb")
        bsrr = B_("srr")
        bscan = B_("scan")
        b_ek, b_euk, b_sr = B_("ek"), B_("euk"), B_("sr")
        b_ekv, b_euv = B_("ekv"), B_("euv")
        b_U, b_D = B_("scU"), B_("scD")
        bstats = B_("stats")
        bsrow = B_("srow")
        bssg = B_("ssg")
        by2b = [B_("y2b0"), B_("y2b1")]
        bwgu = [B_(f"wgu{i}") for i in range(3)]
        bwdn = [B_(f"wdn{i}") for i in range(2)]
        bvecs = B_("vecs")
        bconsts = B_("consts")
        bident = B_("ident")
        bcarry = B_("carry")
        bbnd = B_("bnd")
        bpb = [B_(f"pb{i}") for i in range(7)]
        bptb = B_("ptb")
        bdq = B_("dq")
        bscf = B_("scf")
        bdram_w = {("gu", j): B_(f"dwgu{j}") for j in range(GU_UNITS)}
        bdram_w.update({("dn", j): B_(f"dwdn{j}") for j in range(CT)})
        bdram_x = [B_(f"dxbuf{q}") for q in range(NCH)]

        watt_ap = [wgu_s[0][:, :], wgu_s[1][:, :], wgu_s[2][:, :],
                   wdn_s[0][:, 0:CT * 128], wdn_s[0][:, CT * 128:2 * CT * 128],
                   wdn_s[1][:, 0:CT * 128]]
        bwatt = [B_(f"watt{i}") for i in range(6)]

        srr_v = ar3[:, 0:CT * TCH // 2].bitcast(BF16)[:, 0:CT * TCH]
        scf32 = ar3
        sc_ek = scf32[:, 4096:4608]
        sc_euk = scf32[:, 4608:5120]
        sc_sr = scf32[:, 5120:5632]
        sc_ekv = scf32[:, 5632:6144]
        sc_euv = scf32[:, 6144:6656]
        sc_U = scf32[:, 6656:7169]
        sc_D = scf32[:, 7169:7682]

        h_v = ar2[:, 0:IT * TCH]
        xk_v = ar2[:, 0:CT * TCH]
        xv_v = ar2[:, CT * TCH:2 * CT * TCH]
        xr_v = ar2[:, 2 * CT * TCH:3 * CT * TCH]

        xch_t = [xchA, xchB]

        with b.group("dve", s_dve):
            b.op("dve", lambda E: E.memset(ones_b[:], 1.0), writes=[bconsts])
            b.op("dve", lambda E: E.memset(ones_f[:], 1.0), writes=[bconsts])
            b.op("dve", lambda E: E.memset(bias_ln2n[:], -LN2), writes=[bconsts])
            b.op("dve", lambda E: E.memset(bias_one[:], 1.0), writes=[bconsts])
            b.op("dve", lambda E: E.memset(bias_eps5[:], 1e-5), writes=[bconsts])
            b.op("dve", lambda E: E.memset(bias_eps6[:], 1e-6), writes=[bconsts])
            b.op("dve", lambda E: E.memset(bias_lnh[:], LNHALF), writes=[bconsts])
            b.op("dve", lambda E: E.memset(bias_n6[:], -6.0), writes=[bconsts])
            b.op("dve", lambda E: E.memset(bias_n2[:], -2.0), writes=[bconsts])
            b.op("dve", lambda E: E.memset(cu_sb[:], 0.0), writes=[bcarry])
            b.op("dve", lambda E: E.memset(cd_sb[:], 0.0), writes=[bcarry])
            b.op("dve", lambda E: E.memset(bnd_sb[:], 0.0), writes=[bbnd])

        def dma(eng, sem, out_ap, in_ap, reads, writes):
            with b.group(eng, sem, 16):
                b.op(eng, lambda E, o=out_ap, i=in_ap: E.dma_start(out=o, in_=i),
                     reads=reads, writes=writes)

        dma("sync", s_init, ident[:], ident_in, [], [bident])
        for v_ap, tile in ((v_td, td_sb), (v_tf, u_sb), (v_tmk, tmk_sb),
                           (v_tmv, tmv_sb), (v_tmr, tmr_sb)):
            dma("sync", s_init, tile[:], v_ap, [], [bvecs])
        sc_gu16 = xnb[:, 0:GU_SH * 16].bitcast(F16)[:, 0:GU_SH * 16]
        sc_dn16 = xnb[:, GU_SH * 16:GU_SH * 16 + DN_SH * 44].bitcast(F16)[:, 0:DN_SH * 44]
        dma("sync", s_init, sc_gu16, gu_sc, [], [bxnb, bxnb4[0]])
        dma("sync", s_init, sc_dn16, dn_sc, [], [bxnb, bxnb4[0]])
        def bump_to_total(bufs, eng, sem):
            for bb_ in bufs:
                if bb_.writer is not None:
                    bb_.writer = (eng, sem, sem.planned, bb_.writer[3], True)
        bump_to_total([bident, bvecs, bxnb, bxnb4[0]], "sync", s_init)
        with b.group("act", s_act):
            b.op("act", lambda E: E.activation(out=scf_gu[:], in_=sc_gu16, func=AF.Copy),
                 reads=[bxnb], writes=[bscf])
            b.op("act", lambda E: E.activation(out=scf_dn[:], in_=sc_dn16, func=AF.Copy),
                 reads=[bxnb], writes=[bscf])
            b.op("act", lambda E: E.activation(out=a_sb[:], in_=td_sb[:], func=AF.Exp),
                 reads=[bvecs], writes=[bvecs])
            b.op("act", lambda E: E.activation(out=a_sb[:], in_=a_sb[:], func=AF.Exp,
                                               scale=-1.0),
                 reads=[bvecs], writes=[bvecs])

        def alias_handoff(dsts, srcs):
            rd, wr = [], None
            for s in srcs:
                rd.extend(s.readers)
                if s.writer is not None:
                    rd.append(s.writer)
                    wr = s.writer
            for d in dsts:
                d.writer = wr
                d.readers = list(rd)

        def emit_pk(kind, u0, nb, slot):
            W = 256 if kind == "gu" else 704
            pk_full = gu_pk if kind == "gu" else dn_pk
            pkt = dq_pk[slot]
            for k in range(nb):
                dma("sync", s_dqpk[slot], pkt[:, k * W:(k + 1) * W],
                    pk_full[(u0 + k) * 128:(u0 + k + 1) * 128, :], [], [bdq_pk[slot]])

        def emit_dq_planes(kind, u0, nb, slot, fill, barrier):
            W = 256 if kind == "gu" else 704
            NW = nb * W
            pkt = dq_pk[slot]
            vb = dq_vb[slot]
            def g1(jq):
                with b.group("dve", s_dve):
                    for p in range(4):
                        j = 4 * jq + p
                        b.op("dve", lambda E, j=j, p=p: E.tensor_scalar(
                            out=dq_m3[p][:, 0:NW], in0=pkt[:, 0:NW], scalar1=4 * j,
                            scalar2=7, op0=OP.logical_shift_right, op1=OP.bitwise_and),
                            reads=[bdq_pk[slot]], writes=[bdq_i[p]])
                        b.op("dve", lambda E, j=j, p=p: E.tensor_scalar(
                            out=dq_si[p][:, 0:NW], in0=pkt[:, 0:NW], scalar1=4 * j + 3,
                            scalar2=1, op0=OP.logical_shift_right, op1=OP.bitwise_and),
                            reads=[bdq_pk[slot]], writes=[bdq_i[p]])
                    for p in range(4):
                        s8 = 4 * jq + p
                        b.op("dve", lambda E, p=p, s8=s8: E.tensor_scalar(
                            out=dq_h0[s8][:, 0:NW], in0=dq_m3[p][:, 0:NW], scalar1=0.5,
                            scalar2=None, op0=OP.mult),
                            reads=[bdq_i[p]], writes=[b_dqh[s8]])
                    for p in range(4):
                        s8 = 4 * jq + p
                        ws = [b_dqs[s8]] + ([bdq_i[p]] if s8 >= 4 else [])
                        b.op("dve", lambda E, p=p, s8=s8: E.tensor_scalar(
                            out=dq_s2m[s8][:, 0:NW], in0=dq_si[p][:, 0:NW], scalar1=-2.0,
                            scalar2=1.0, op0=OP.mult, op1=OP.add),
                            reads=[bdq_i[p]], writes=ws)

            def gact(jq):
                with b.group("act", s_act):
                    for p in range(4):
                        s8 = 4 * jq + p
                        b.op("act", lambda E, s8=s8: E.activation(
                            out=dq_r2[s8][:, 0:NW], in_=dq_h0[s8][:, 0:NW],
                            func=AF.Relu, scale=2.0, bias=bias_n6[:]),
                            reads=[b_dqh[s8], bconsts], writes=[b_dqB[s8]])

            def g2(jq):
                with b.group("dve", s_dve):
                    for p in range(4):
                        s8 = 4 * jq + p
                        b.op("dve", lambda E, s8=s8: E.tensor_scalar(
                            out=dq_r1[s8][:, 0:NW], in0=dq_h0[s8][:, 0:NW],
                            scalar1=-2.0, scalar2=0.0, op0=OP.add, op1=OP.max),
                            reads=[b_dqh[s8]], writes=[b_dqr[s8]])
                    for p in range(4):
                        s8 = 4 * jq + p
                        b.op("dve", lambda E, s8=s8: E.tensor_tensor(
                            out=dq_r2[s8][:, 0:NW], in0=dq_h0[s8][:, 0:NW],
                            in1=dq_r2[s8][:, 0:NW], op=OP.add),
                            reads=[b_dqh[s8], b_dqB[s8]], writes=[b_dqB[s8]])
                    for p in range(4):
                        s8 = 4 * jq + p
                        b.op("dve", lambda E, s8=s8: E.tensor_tensor(
                            out=dq_r2[s8][:, 0:NW], in0=dq_r2[s8][:, 0:NW],
                            in1=dq_r1[s8][:, 0:NW], op=OP.add),
                            reads=[b_dqB[s8], b_dqr[s8]], writes=[b_dqB[s8]])
                    for p in range(4):
                        s8 = 4 * jq + p
                        j = 4 * jq + p
                        CW = W * 8
                        out_ap = vb[:, 0:nb * CW] \
                            .rearrange("p (k r) -> p k r", r=CW)[:, :, j * W:j * W + W] \
                            if nb > 1 else vb[:, j * W:j * W + W]
                        rd = [b_dqs[s8], b_dqB[s8]] + ([bdq_i[p]] if s8 >= 4 else [])
                        b.op("dve", lambda E, s8=s8, out_ap=out_ap: E.tensor_tensor(
                            out=out_ap, in0=dq_s2m[s8][:, 0:NW], in1=dq_r2[s8][:, 0:NW],
                            op=OP.mult),
                            reads=rd,
                            writes=[bdq_vbB[slot], bdq_vbS[slot]])

            g1(0)
            gact(0)
            g1(1)
            gact(1)
            fill()
            fill()
            barrier()
            g2(0)
            fill()
            g2(1)
            fill()

        def emit_dq_tail(kind, u0, nb, slot):
            W = 256 if kind == "gu" else 704
            nct = CT if kind == "gu" else IT
            scf = scf_gu if kind == "gu" else scf_dn
            ngrp = W * 8 // 128
            vb = dq_vb[slot]
            ub = dq_ub[slot]
            for k in range(nb):
                u = u0 + k
                scoff = u * (16 if kind == "gu" else 44)
                CW = W * 8
                with b.group("act", s_act):
                    for g in range(ngrp):
                        ap_g = vb[:, k * CW:(k + 1) * CW] \
                            .rearrange("p (j r) -> p j r", r=W)[:, :, 16 * g:16 * g + 16]
                        b.op("act", lambda E, ap_g=ap_g, g=g, scoff=scoff, scf=scf: E.activation(
                            out=ap_g, in_=ap_g,
                            func=AF.Copy, scale=scf[:, scoff + g:scoff + g + 1]),
                            reads=[bdq_vbB[slot], bscf], writes=[bdq_vbS[slot]])
                yield
                for t8 in range((ngrp + 7) // 8):
                    n_in = min(8, ngrp - t8 * 8)
                    with b.group("pe", s_pe):
                        for kk in range(n_in):
                            g = t8 * 8 + kk
                            c0 = k * ngrp * 128 + g * 128
                            b.op("pe", lambda E, c0=c0, kk=kk, vb=vb: E.transpose(
                                out=ptb[:, kk * 128:(kk + 1) * 128],
                                in_=vb[:, c0:c0 + 128], identity=ident[:]),
                                reads=[bdq_vbS[slot], bident], writes=[bptb])
                    with b.group("act", s_act):
                        b.op("act", lambda E, t8=t8, n_in=n_in, ub=ub: E.activation(
                            out=ub[:, t8 * 1024:t8 * 1024 + n_in * 128],
                            in_=ptb[:, 0:n_in * 128], func=AF.Copy),
                            reads=[bptb], writes=[bdq_ubB[slot]])
                    yield
                dst = (cgu if kind == "gu" else cdn)[u * 128:(u + 1) * 128, :]
                dma("gp", s_dqst2[slot], dst, ub[:, 0:nct * 128], [bdq_ubB[slot]],
                    [(b_cgu if kind == "gu" else b_cdn) if shard else bdram_w[(kind, u)]])

        b_cgu, b_cdn = Buf("cgu"), Buf("cdn")
        bdq_pk = [Buf("dqpk0"), Buf("dqpk1")]
        bdq_i = [Buf(f"dqi{p}") for p in range(4)]
        bdq_a = [Buf(f"dqa{p}") for p in range(4)]
        b_dqh = [Buf(f"dqh{p}") for p in range(8)]
        b_dqs = [Buf(f"dqs{p}") for p in range(8)]
        b_dqB = [Buf(f"dqB{p}") for p in range(8)]
        b_dqr = [Buf(f"dqr{p}") for p in range(8)]
        bdq_v = [Buf("dqv0"), Buf("dqv1")]
        bdq_vbB = [Buf("dqvb0"), Buf("dqvb1")]
        bdq_vbS = [Buf("dqvbs0"), Buf("dqvbs1")]
        bdq_ubB = [Buf("dqub0"), Buf("dqub1")]
        batches = []
        u = 0
        while u < GU_SH:
            nb = min(3, GU_SH - u)
            batches.append(("gu", u, nb))
            u += nb
        u = 0
        while u < DN_SH:
            batches.append(("dn", u, 1))
            u += 1
        pending = []
        _S = object()

        def fill():
            while pending:
                if next(pending[0], _S) is _S:
                    pending.pop(0)
                    continue
                return

        def barrier():
            while len(pending) > 1:
                for _ in pending.pop(0):
                    pass

        emit_pk(*batches[0], 0)
        for bi, (kind_, u0_, nb_) in enumerate(batches):
            if bi + 1 < len(batches):
                emit_pk(*batches[bi + 1], (bi + 1) % 2)
            if bi > 0:
                k2, u2, n2 = batches[bi - 1]
                pending.append(emit_dq_tail(k2, u2, n2, (bi - 1) % 2))
            emit_dq_planes(kind_, u0_, nb_, bi % 2, fill, barrier)
        for g in pending:
            for _ in g:
                pass
        k2, u2, n2 = batches[-1]
        for _ in emit_dq_tail(k2, u2, n2, (len(batches) - 1) % 2):
            pass
        alias_handoff([bxch[0], bxch[1], bh, bxk, bxv, bxr, bysb, bsrr, bscan,
                       b_ek, b_euk, b_sr, b_ekv, b_euv, b_U, b_D, bpb[6], bxnb,
                       bxnb4[0], bxnb4[1], bxnb4[2], bxnb4[3]],
                      [bdq, bscf, bdq_v[0], bdq_v[1], bdq_i[0], bdq_i[1],
                       bdq_i[2], bdq_i[3], bdq_a[0], bdq_a[1],
                       bdq_a[2], bdq_a[3], bdq_pk[0], bdq_pk[1],
                       b_dqh[0], b_dqh[1], b_dqh[2], b_dqh[3],
                       b_dqh[4], b_dqh[5], b_dqh[6], b_dqh[7],
                       b_dqs[0], b_dqs[1], b_dqs[2], b_dqs[3],
                       b_dqs[4], b_dqs[5], b_dqs[6], b_dqs[7],
                       b_dqB[0], b_dqB[1], b_dqB[2], b_dqB[3],
                       b_dqB[4], b_dqB[5], b_dqB[6], b_dqB[7],
                       b_dqr[0], b_dqr[1], b_dqr[2], b_dqr[3],
                       b_dqr[4], b_dqr[5], b_dqr[6], b_dqr[7],
                       bdq_vbB[0], bdq_vbB[1], bdq_vbS[0], bdq_vbS[1],
                       bdq_ubB[0], bdq_ubB[1], bptb])
        if shard:
            bump_to_total([b_cgu], "gp", s_dqst2[0])
            bump_to_total([b_cdn], "gp", s_dqst2[1])
            with b.group("gp", s_cc, 16):
                b.op("gp", lambda E: E.collective_compute(
                    "AllGather", OP.bypass, replica_groups=[list(range(8))],
                    ins=[cgu_t.ap().opt()], outs=[wgu_full_t.ap().opt()]),
                    reads=[b_cgu], writes=[bdram_w[("gu", 0)]])
            with b.group("gp", s_cc, 16):
                b.op("gp", lambda E: E.collective_compute(
                    "AllGather", OP.bypass, replica_groups=[list(range(8))],
                    ins=[cdn_t.ap().opt()], outs=[wdn_full_t.ap().opt()]),
                    reads=[b_cdn], writes=[bdram_w[("dn", 0)]])
            for (kk, uu), bb_ in bdram_w.items():
                bb_.writer = bdram_w[(kk, 0)].writer
        else:
            for (kk_, uu_), bb_ in bdram_w.items():
                pass
            done0 = ("gp", s_dqst2[0], s_dqst2[0].planned, 0, True)
            done1 = ("gp", s_dqst2[1], s_dqst2[1].planned, 0, True)
            for bb_ in bdram_w.values():
                bb_.writer = done0
                bb_.readers = [done1]

        m_row = srow[0:1, 0:512]
        rstd_row = srow[0:1, 512:1024]
        mr_row = srow[0:1, 1024:1536]

        def emit_ln_a1(xch, bx, xb16, bxb16, subbufs=None):
            with b.group("dve", s_dve):
                for ct in range(CT):
                    ws = [bxb16] + ([subbufs[ct // 4]] if subbufs else [])
                    b.op("dve", lambda E, ct=ct, xch=xch, xb16=xb16: E.tensor_scalar(
                        out=xb16[:, ct * TCH:(ct + 1) * TCH],
                        in0=xch[:, ct * TCH:(ct + 1) * TCH],
                        scalar1=1.0, scalar2=None, op0=OP.mult),
                        reads=[bx], writes=ws)

        def emit_ln_a2(xch, bx, xsq, bxsq):
            for gi in range(4):
                with b.group("act", s_act):
                    for k in range(4):
                        ct = gi * 4 + k
                        b.op("act", lambda E, ct=ct, xch=xch, xsq=xsq: E.activation(
                            out=xsq[:, ct * TCH:(ct + 1) * TCH],
                            in_=xch[:, ct * TCH:(ct + 1) * TCH], func=AF.Square),
                            reads=[bx], writes=[bxsq])

        def emit_ln(xch, bx, xb16, bxb16, xsq, bxsq, st0, st1, bc0, bc1,
                    prepped=False, subbufs=None):
            if not prepped:
                emit_ln_a1(xch, bx, xb16, bxb16, subbufs=subbufs)
                emit_ln_a2(xch, bx, xsq, bxsq)
            with b.group("pe", s_pe):
                for ct in range(CT):
                    b.op("pe", lambda E, ct=ct, xb16=xb16: E.matmul(
                        out=pb[st0][0:1, :], lhsT=ones_b[:, :],
                        rhs=xb16[:, ct * TCH:(ct + 1) * TCH],
                        start=(ct == 0), stop=(ct == CT - 1)),
                        reads=[bxb16, bconsts], writes=[bpb[st0]])
            with b.group("pe", s_pe):
                for ct in range(CT):
                    b.op("pe", lambda E, ct=ct, xsq=xsq: E.matmul(
                        out=pb[st1][0:1, :], lhsT=ones_b[:, :],
                        rhs=xsq[:, ct * TCH:(ct + 1) * TCH],
                        start=(ct == 0), stop=(ct == CT - 1)),
                        reads=[bxsq, bconsts], writes=[bpb[st1]])
            with b.group("dve", s_dve):
                b.op("dve", lambda E: E.tensor_scalar(
                    out=m_row, in0=pb[st0][0:1, :], scalar1=RECIP_C, scalar2=None,
                    op0=OP.mult), reads=[bpb[st0]], writes=[bsrow])
                b.op("dve", lambda E: E.tensor_scalar(
                    out=rstd_row, in0=pb[st1][0:1, :], scalar1=RECIP_C, scalar2=None,
                    op0=OP.mult), reads=[bpb[st1]], writes=[bsrow])
                b.op("dve", lambda E: E.scalar_tensor_tensor(
                    out=mr_row, in0=m_row, scalar=1.0, in1=m_row,
                    op0=OP.mult, op1=OP.mult), reads=[bsrow], writes=[bsrow])
                b.op("dve", lambda E: E.tensor_tensor(
                    out=rstd_row, in0=rstd_row, in1=mr_row, op=OP.subtract),
                    reads=[bsrow], writes=[bsrow])
            with b.group("act", s_act):
                b.op("act", lambda E: E.activation(
                    out=rstd_row, in_=rstd_row, func=AF.Ln, bias=bias_eps5[0:1, :]),
                    reads=[bsrow, bconsts], writes=[bsrow])
                b.op("act", lambda E: E.activation(
                    out=rstd_row, in_=rstd_row, func=AF.Exp, scale=-0.5),
                    reads=[bsrow], writes=[bsrow])
            with b.group("dve", s_dve):
                b.op("dve", lambda E: E.tensor_tensor(
                    out=mr_row, in0=m_row, in1=rstd_row, op=OP.mult),
                    reads=[bsrow], writes=[bsrow])
            with b.group("pe", s_pe):
                b.op("pe", lambda E: E.matmul(
                    out=pb[bc0][:, :], lhsT=ones_f[:, :], rhs=rstd_row,
                    start=True, stop=True), reads=[bsrow, bconsts], writes=[bpb[bc0]])
            with b.group("pe", s_pe):
                b.op("pe", lambda E: E.matmul(
                    out=pb[bc1][:, :], lhsT=ones_f[:, :], rhs=mr_row,
                    start=True, stop=True), reads=[bsrow, bconsts], writes=[bpb[bc1]])

        def emit_x_load(q, src_ap, bx):
            dma("sync", s_xch[q % 2], xch_t[q % 2][:, :], src_ap[q], [], [bx])

        def emit_ffn(q, src_is_input, dst_ap, bdst, la_done=False,
                     hook_fills=None, hook_tail=None, hook_xn=None):
            xch = xch_t[q % 2]
            bx = bxch[q % 2]
            if not la_done:
                emit_ln(xch, bx, xnb, bxnb, h_v, bh, 4, 5, 4, 5,
                        prepped=False, subbufs=bxnb4)
                for gi in range(4):
                    with b.group("dve", s_dve):
                        for k in range(4):
                            ct = gi * 4 + k
                            b.op("dve", lambda E, ct=ct: E.tensor_tensor(
                                out=stats[:, :], in0=xch[:, ct * TCH:(ct + 1) * TCH],
                                in1=pb[4][:, :], op=OP.mult),
                                reads=[bx, bpb[4]], writes=[bstats])
                            b.op("dve", lambda E, ct=ct: E.tensor_tensor(
                                out=xnb[:, ct * TCH:(ct + 1) * TCH], in0=stats[:, :],
                                in1=pb[5][:, :], op=OP.subtract),
                                reads=[bstats, bpb[5]], writes=[bxnb, bxnb4[ct // 4]])
            if dbg == "lnonly":
                dma("act", s_st[q % 2], dst_ap[q], xch[:, :], [bx], [bdst[q]])
                return
            def gu_load(j, which, slot):
                u = j if which == 0 else j + IT
                dma("sync", s_wgu[slot], wgu_s[slot][:, :],
                    wgu_full[u * 128:(u + 1) * 128, :],
                    [bdram_w[("gu", u)]], [bwgu[slot]])
            if not la_done:
                gu_load(0, 0, 0)
                gu_load(0, 1, 1)
            for j in range(IT):
                sg_, su_ = (2 * j) % 3, (2 * j + 1) % 3
                pg, pu = 2 * (j % 2), 2 * (j % 2) + 1
                with b.group("pe", s_pe):
                    for ct in range(CT):
                        b.op("pe", lambda E, j=j, ct=ct, sg_=sg_, pg=pg: E.matmul(
                            out=pb[pg][:, :], lhsT=wgu_s[sg_][:, ct * 128:(ct + 1) * 128],
                            rhs=xnb[:, ct * TCH:(ct + 1) * TCH],
                            start=(ct == 0), stop=(ct == CT - 1)),
                            reads=[bwgu[sg_], bxnb4[ct // 4]], writes=[bpb[pg]])
                with b.group("pe", s_pe):
                    for ct in range(CT):
                        b.op("pe", lambda E, j=j, ct=ct, su_=su_, pu=pu: E.matmul(
                            out=pb[pu][:, :], lhsT=wgu_s[su_][:, ct * 128:(ct + 1) * 128],
                            rhs=xnb[:, ct * TCH:(ct + 1) * TCH],
                            start=(ct == 0), stop=(ct == CT - 1)),
                            reads=[bwgu[su_], bxnb4[ct // 4]], writes=[bpb[pu]])
                if j + 1 < IT:
                    gu_load(j + 1, 0, (2 * j + 2) % 3)
                    gu_load(j + 1, 1, (2 * j + 3) % 3)
                with b.group("act", s_act):
                    b.op("act", lambda E, pg=pg: E.activation(
                        out=ssg[:, :], in_=pb[pg][:, :], func=AF.Silu),
                        reads=[bpb[pg]], writes=[bssg])
                with b.group("dve", s_dve):
                    b.op("dve", lambda E, j=j, pu=pu: E.tensor_tensor(
                        out=h_v[:, j * TCH:(j + 1) * TCH], in0=ssg[:, :],
                        in1=pb[pu][:, :], op=OP.mult),
                        reads=[bssg, bpb[pu]], writes=[bh])
            fill_a1 = list(hook_fills()) if hook_fills is not None else []
            if dbg == "xnb" and q == 0:
                dma("gp", s_dqst, dbg_out, xnb[:, :], [bxnb], [Buf("dbg")])
            if dbg == "srow" and q == 0:
                dma("gp", s_dqst, dbg_out[0:1, 0:1536], srow[0:1, :],
                    [bsrow], [Buf("dbg")])
            if dbg == "h" and q == 0:
                dma("gp", s_dqst, dbg_out, h_v[:, 0:CT * TCH], [bh], [Buf("dbg")])
            if dbg == "guonly":
                dma("act", s_st[q % 2], dst_ap[q], xch[:, :], [bx], [bdst[q]])
                return
            dma("sync", s_wdn[0], wdn_s[0][:, :], wdn_full[0:128, :],
                [bdram_w[("dn", 0)]], [bwdn[0]])
            dma("sync", s_wdn[1], wdn_s[1][:, :], wdn_full[128:256, :],
                [bdram_w[("dn", 1)]], [bwdn[1]])
            pend_sq = None
            for ht in range(CT):
                sl = ht % 2
                py = 4 + (ht % 2)
                with b.group("pe", s_pe):
                    for it in range(IT):
                        b.op("pe", lambda E, ht=ht, it=it, sl=sl, py=py: E.matmul(
                            out=pb[py][:, :], lhsT=wdn_s[sl][:, it * 128:(it + 1) * 128],
                            rhs=h_v[:, it * TCH:(it + 1) * TCH],
                            start=(it == 0), stop=(it == IT - 1)),
                            reads=[bwdn[sl], bh], writes=[bpb[py]])
                if ht + 2 < CT:
                    dma("sync", s_wdn[(ht + 2) % 2], wdn_s[(ht + 2) % 2][:, :],
                        wdn_full[(ht + 2) * 128:(ht + 3) * 128, :],
                        [bdram_w[("dn", ht + 2)]], [bwdn[(ht + 2) % 2]])
                if pend_sq is not None:
                    hp, slot = pend_sq
                    with b.group("pe", s_pe):
                        b.op("pe", lambda E, hp=hp, slot=slot: E.matmul(
                            out=pb[6][0:1, :], lhsT=ones_b[:, :],
                            rhs=y2b[:, slot * 512:(slot + 1) * 512],
                            start=(hp == 0), stop=(hp == CT - 1)),
                            reads=[by2b[slot], bconsts], writes=[bpb[6]])
                with b.group("act", s_act):
                    b.op("act", lambda E, ht=ht, py=py: E.activation(
                        out=y2b[:, (ht % 2) * 512:(ht % 2 + 1) * 512],
                        in_=pb[py][:, :], func=AF.Square),
                        reads=[bpb[py]], writes=[by2b[ht % 2]])
                with b.group("dve", s_dve):
                    b.op("dve", lambda E, ht=ht, py=py: E.tensor_scalar(
                        out=ar3[:, ht * TCH:(ht + 1) * TCH], in0=pb[py][:, :],
                        scalar1=1.0, scalar2=None, op0=OP.mult),
                        reads=[bpb[py], by2b[ht % 2]], writes=[bysb])
                if fill_a1:
                    fill_a1.pop(0)()
                pend_sq = (ht, ht % 2)
            hp, slot = pend_sq
            with b.group("pe", s_pe):
                b.op("pe", lambda E, hp=hp, slot=slot: E.matmul(
                    out=pb[6][0:1, :], lhsT=ones_b[:, :],
                    rhs=y2b[:, slot * 512:(slot + 1) * 512],
                    start=(hp == 0), stop=(hp == CT - 1)),
                    reads=[by2b[slot], bconsts], writes=[bpb[6]])
            if dbg == "ysb" and q == 0:
                dma("gp", s_dqst, dbg_out, ar3[:, 0:8192], [bysb], [Buf("dbg")])
            with b.group("dve", s_dve):
                b.op("dve", lambda E: E.tensor_scalar(
                    out=m_row, in0=pb[6][0:1, :], scalar1=RECIP_C, scalar2=None,
                    op0=OP.mult), reads=[bpb[6]], writes=[bsrow])
            with b.group("act", s_act):
                b.op("act", lambda E: E.activation(
                    out=m_row, in_=m_row, func=AF.Ln, bias=bias_eps6[0:1, :]),
                    reads=[bsrow, bconsts], writes=[bsrow])
                b.op("act", lambda E: E.activation(
                    out=m_row, in_=m_row, func=AF.Exp, scale=-0.5, bias=bias_lnh[0:1, :]),
                    reads=[bsrow, bconsts], writes=[bsrow])
            with b.group("pe", s_pe):
                b.op("pe", lambda E: E.matmul(
                    out=pb[4][:, :], lhsT=ones_f[:, :], rhs=m_row,
                    start=True, stop=True), reads=[bsrow, bconsts], writes=[bpb[4]])
            if hook_tail is not None:
                hook_tail()
            for gi in range(4):
                if hook_xn is not None:
                    hook_xn(gi)
                with b.group("dve", s_dve):
                    for k in range(4):
                        ct = gi * 4 + k
                        b.op("dve", lambda E, ct=ct: E.tensor_tensor(
                            out=stats[:, :], in0=ar3[:, ct * TCH:(ct + 1) * TCH],
                            in1=pb[4][:, :], op=OP.mult),
                            reads=[bysb, bpb[4]], writes=[bstats])
                        b.op("dve", lambda E, ct=ct: E.tensor_tensor(
                            out=xch[:, ct * TCH:(ct + 1) * TCH],
                            in0=xch[:, ct * TCH:(ct + 1) * TCH],
                            in1=stats[:, :], op=OP.add),
                            reads=[bstats], writes=[bx])
            for f in fill_a1:
                f()
            dma("act", s_st[q % 2], dst_ap[q], xch[:, :], [bx], [bdst[q]])

        def watt_load(mat_ap, unit, slot):
            dma("gp", s_watt[slot], watt_ap[slot], mat_ap[unit], [], [bwatt[slot]])

        def emit_attn_ln(q):
            xch = xch_t[q % 2]
            bx = bxch[q % 2]
            emit_ln(xch, bx, xk_v, bxk, xv_v, bxv, 0, 1, 0, 1)

        def mix_ct_fns(q):
            xch = xch_t[q % 2]
            bx = bxch[q % 2]
            xn_t = xnb[:, 0:513]
            bxn_t = bxnb
            def mk(ct):
                return lambda: emit_mix_ct(q, ct)
            return [mk(ct) for ct in range(CT)]

        def emit_mix_ct(q, ct):
            xch = xch_t[q % 2]
            bx = bxch[q % 2]
            xn_t = xnb[:, 0:513]
            bxn_t = bxnb
            if True:
                with b.group("dve", s_dve):
                    b.op("dve", lambda E, ct=ct: E.tensor_copy(
                        out=xn_t[:, 0:1], in_=bnd_sb[:, ct:ct + 1]),
                        reads=[bbnd], writes=[bxn_t, bxnb4[0]])
                    b.op("dve", lambda E, ct=ct: E.tensor_tensor(
                        out=stats[:, :], in0=xch[:, ct * TCH:(ct + 1) * TCH],
                        in1=pb[0][:, :], op=OP.mult),
                        reads=[bx, bpb[0]], writes=[bstats])
                    b.op("dve", lambda E, ct=ct: E.tensor_tensor(
                        out=xn_t[:, 1:513], in0=stats[:, :], in1=pb[1][:, :],
                        op=OP.subtract), reads=[bstats, bpb[1]], writes=[bxn_t, bxnb4[0]])
                    b.op("dve", lambda E, ct=ct: E.tensor_copy(
                        out=bnd_sb[:, ct:ct + 1], in_=xn_t[:, 512:513]),
                        reads=[bxn_t], writes=[bbnd])
                    b.op("dve", lambda E, ct=ct: E.tensor_tensor(
                        out=y2b[:, 0:512], in0=xn_t[:, 1:513], in1=xn_t[:, 0:512],
                        op=OP.subtract), reads=[bxn_t], writes=[by2b[0]])
                    for tm_sb, xdst, bdst_ in ((tmk_sb, xk_v, bxk), (tmv_sb, xv_v, bxv),
                                               (tmr_sb, xr_v, bxr)):
                        b.op("dve", lambda E, ct=ct, tm_sb=tm_sb, xdst=xdst: E.scalar_tensor_tensor(
                            out=xdst[:, ct * TCH:(ct + 1) * TCH], in0=y2b[:, 0:512],
                            scalar=tm_sb[:, ct:ct + 1], in1=xn_t[:, 0:512],
                            op0=OP.mult, op1=OP.add),
                            reads=[by2b[0], bxn_t, bvecs], writes=[bdst_])
        def emit_attn_pre(q):
            emit_attn_ln(q)
            for f in mix_ct_fns(q):
                f()

        def emit_attn_main(q):
            xch = xch_t[q % 2]
            bx = bxch[q % 2]
            watt_load(wk_t, 0, 0)
            watt_load(wv_t, 0, 1)
            watt_load(wr_t, 0, 2)
            for j in range(CT):
                s0, s1, s2 = (3 * j) % 6, (3 * j + 1) % 6, (3 * j + 2) % 6
                if j + 1 < CT:
                    watt_load(wk_t, j + 1, (3 * j + 3) % 6)
                    watt_load(wv_t, j + 1, (3 * j + 4) % 6)
                    watt_load(wr_t, j + 1, (3 * j + 5) % 6)
                pk_, pv_, pr_ = 3 * (j % 2), 3 * (j % 2) + 1, 3 * (j % 2) + 2
                for (slot, src, pdst) in ((s0, bxk, pk_), (s1, bxv, pv_), (s2, bxr, pr_)):
                    xsrc = {id(bxk): xk_v, id(bxv): xv_v, id(bxr): xr_v}[id(src)]
                    with b.group("pe", s_pe):
                        for ct in range(CT):
                            b.op("pe", lambda E, slot=slot, ct=ct, xsrc=xsrc, pdst=pdst: E.matmul(
                                out=pb[pdst][:, :],
                                lhsT=watt_ap[slot][:, ct * 128:(ct + 1) * 128],
                                rhs=xsrc[:, ct * TCH:(ct + 1) * TCH],
                                start=(ct == 0), stop=(ct == CT - 1)),
                                reads=[bwatt[slot], src], writes=[bpb[pdst]])
                with b.group("act", s_act):
                    b.op("act", lambda E, pk_=pk_: E.activation(
                        out=sc_ek, in_=pb[pk_][:, :], func=AF.Exp),
                        reads=[bpb[pk_]], writes=[b_ek])
                    b.op("act", lambda E, pk_=pk_, j=j: E.activation(
                        out=sc_euk, in_=pb[pk_][:, :], func=AF.Exp, bias=u_sb[:, j:j + 1]),
                        reads=[bpb[pk_], bvecs], writes=[b_euk])
                    b.op("act", lambda E, pr_=pr_: E.activation(
                        out=sc_sr, in_=pb[pr_][:, :], func=AF.Exp, scale=-1.0),
                        reads=[bpb[pr_]], writes=[b_sr])
                    b.op("act", lambda E: E.activation(
                        out=sc_sr, in_=sc_sr, func=AF.Ln, bias=bias_one[:]),
                        reads=[b_sr, bconsts], writes=[b_sr])
                    b.op("act", lambda E: E.activation(
                        out=sc_sr, in_=sc_sr, func=AF.Exp, scale=-1.0),
                        reads=[b_sr], writes=[b_sr])
                with b.group("dve", s_dve):
                    b.op("dve", lambda E, pv_=pv_: E.tensor_tensor(
                        out=sc_ekv, in0=sc_ek, in1=pb[pv_][:, :], op=OP.mult),
                        reads=[b_ek, bpb[pv_]], writes=[b_ekv])
                    b.op("dve", lambda E, pv_=pv_: E.tensor_tensor(
                        out=sc_euv, in0=sc_euk, in1=pb[pv_][:, :], op=OP.mult),
                        reads=[b_euk, bpb[pv_]], writes=[b_euv])
                    b.op("dve", lambda E, j=j: E.tensor_tensor_scan(
                        out=sc_U[:, 1:513], data0=a_sb[:, j:j + 1].to_broadcast([128, 512]),
                        data1=sc_ekv, initial=cu_sb[:, j:j + 1],
                        op0=OP.mult, op1=OP.add),
                        reads=[b_ekv, bvecs, bcarry], writes=[b_U])
                    b.op("dve", lambda E, j=j: E.tensor_copy(
                        out=sc_U[:, 0:1], in_=cu_sb[:, j:j + 1]),
                        reads=[bcarry], writes=[b_U])
                    b.op("dve", lambda E, j=j: E.tensor_copy(
                        out=cu_sb[:, j:j + 1], in_=sc_U[:, 512:513]),
                        reads=[b_U], writes=[bcarry])
                    b.op("dve", lambda E, j=j: E.tensor_tensor_scan(
                        out=sc_D[:, 1:513], data0=a_sb[:, j:j + 1].to_broadcast([128, 512]),
                        data1=sc_ek, initial=cd_sb[:, j:j + 1],
                        op0=OP.mult, op1=OP.add),
                        reads=[b_ek, bvecs, bcarry], writes=[b_D])
                    b.op("dve", lambda E, j=j: E.tensor_copy(
                        out=sc_D[:, 0:1], in_=cd_sb[:, j:j + 1]),
                        reads=[bcarry], writes=[b_D])
                    b.op("dve", lambda E, j=j: E.tensor_copy(
                        out=cd_sb[:, j:j + 1], in_=sc_D[:, 512:513]),
                        reads=[b_D], writes=[bcarry])
                    b.op("dve", lambda E: E.tensor_tensor(
                        out=sc_euv, in0=sc_U[:, 0:512], in1=sc_euv, op=OP.add),
                        reads=[b_U, b_euv], writes=[b_euv])
                    b.op("dve", lambda E: E.tensor_tensor(
                        out=sc_euk, in0=sc_D[:, 0:512], in1=sc_euk, op=OP.add),
                        reads=[b_D, b_euk], writes=[b_euk])
                with b.group("act", s_act):
                    b.op("act", lambda E: E.activation(
                        out=sc_ek, in_=sc_euk, func=AF.Ln),
                        reads=[b_euk], writes=[b_ek])
                    b.op("act", lambda E: E.activation(
                        out=sc_ek, in_=sc_ek, func=AF.Exp, scale=-1.0),
                        reads=[b_ek], writes=[b_ek])
                with b.group("dve", s_dve):
                    b.op("dve", lambda E: E.tensor_tensor(
                        out=sc_ekv, in0=sc_euv, in1=sc_ek, op=OP.mult),
                        reads=[b_euv, b_ek], writes=[b_ekv])
                    b.op("dve", lambda E, j=j: E.tensor_tensor(
                        out=srr_v[:, j * TCH:(j + 1) * TCH], in0=sc_ekv, in1=sc_sr,
                        op=OP.mult), reads=[b_ekv, b_sr], writes=[bsrr])
        def emit_attn_wo(q, fillers=()):
            fillers = list(fillers)
            xch = xch_t[q % 2]
            bx = bxch[q % 2]
            watt_load(wo_t, 0, 0)
            watt_load(wo_t, 1, 1)
            for ct in range(CT):
                slot = ct % 6
                if ct + 2 < CT:
                    watt_load(wo_t, ct + 2, (ct + 2) % 6)
                po = 2 + ct % 2
                with b.group("pe", s_pe):
                    for dt in range(CT):
                        b.op("pe", lambda E, ct=ct, dt=dt, slot=slot, po=po: E.matmul(
                            out=pb[po][:, :], lhsT=watt_ap[slot][:, dt * 128:(dt + 1) * 128],
                            rhs=srr_v[:, dt * TCH:(dt + 1) * TCH],
                            start=(dt == 0), stop=(dt == CT - 1)),
                            reads=[bwatt[slot], bsrr], writes=[bpb[po]])
                with b.group("dve", s_dve):
                    b.op("dve", lambda E, ct=ct, po=po: E.tensor_tensor(
                        out=xch[:, ct * TCH:(ct + 1) * TCH],
                        in0=xch[:, ct * TCH:(ct + 1) * TCH],
                        in1=pb[po][:, :], op=OP.add),
                        reads=[bpb[po]], writes=[bx])
                if fillers:
                    fillers.pop(0)()
            for f in fillers:
                f()
            dma("act", s_st[q % 2], xbuf[q], xch[:, :], [bx], [bdram_x[q]])

        ssg_b = ssg.ap().bitcast(BF16)

        def gu_load01():
            dma("sync", s_wgu[0], wgu_s[0][:, :], wgu_full[0:128, :],
                [bdram_w[("gu", 0)]], [bwgu[0]])
            dma("sync", s_wgu[1], wgu_s[1][:, :], wgu_full[IT * 128:(IT + 1) * 128, :],
                [bdram_w[("gu", IT)]], [bwgu[1]])

        def ffn_hooks(q):
            if q + 1 >= NCH:
                return None, None, None
            nxt = xch_t[(q + 1) % 2]
            bnxt = bxch[(q + 1) % 2]

            def fills():
                def mk(ct):
                    def f():
                        with b.group("dve", s_dve):
                            b.op("dve", lambda E, ct=ct: E.tensor_scalar(
                                out=xnb[:, ct * TCH:(ct + 1) * TCH],
                                in0=nxt[:, ct * TCH:(ct + 1) * TCH],
                                scalar1=1.0, scalar2=None, op0=OP.mult),
                                reads=[bnxt], writes=[bxnb, bxnb4[ct // 4]])
                        with b.group("pe", s_pe):
                            b.op("pe", lambda E, ct=ct: E.matmul(
                                out=pb[0][0:1, :], lhsT=ones_b[:, :],
                                rhs=xnb[:, ct * TCH:(ct + 1) * TCH],
                                start=(ct == 0), stop=(ct == CT - 1)),
                                reads=[bxnb4[ct // 4], bconsts], writes=[bpb[0]])
                        with b.group("act", s_act):
                            b.op("act", lambda E, ct=ct: E.activation(
                                out=ssg_b[:, 0:TCH],
                                in_=nxt[:, ct * TCH:(ct + 1) * TCH], func=AF.Square),
                                reads=[bnxt], writes=[bssg])
                        with b.group("pe", s_pe):
                            b.op("pe", lambda E, ct=ct: E.matmul(
                                out=pb[1][0:1, :], lhsT=ones_b[:, :],
                                rhs=ssg_b[:, 0:TCH],
                                start=(ct == 0), stop=(ct == CT - 1)),
                                reads=[bssg, bconsts], writes=[bpb[1]])
                    return f
                return [mk(ct) for ct in range(CT)]

            def tail():
                gu_load01()
                with b.group("dve", s_dve):
                    b.op("dve", lambda E: E.tensor_scalar(
                        out=m_row, in0=pb[0][0:1, :], scalar1=RECIP_C, scalar2=None,
                        op0=OP.mult), reads=[bpb[0]], writes=[bsrow])
                    b.op("dve", lambda E: E.tensor_scalar(
                        out=rstd_row, in0=pb[1][0:1, :], scalar1=RECIP_C, scalar2=None,
                        op0=OP.mult), reads=[bpb[1]], writes=[bsrow])
                    b.op("dve", lambda E: E.scalar_tensor_tensor(
                        out=mr_row, in0=m_row, scalar=1.0, in1=m_row,
                        op0=OP.mult, op1=OP.mult), reads=[bsrow], writes=[bsrow])
                    b.op("dve", lambda E: E.tensor_tensor(
                        out=rstd_row, in0=rstd_row, in1=mr_row, op=OP.subtract),
                        reads=[bsrow], writes=[bsrow])
                with b.group("act", s_act):
                    b.op("act", lambda E: E.activation(
                        out=rstd_row, in_=rstd_row, func=AF.Ln, bias=bias_eps5[0:1, :]),
                        reads=[bsrow, bconsts], writes=[bsrow])
                    b.op("act", lambda E: E.activation(
                        out=rstd_row, in_=rstd_row, func=AF.Exp, scale=-0.5),
                        reads=[bsrow], writes=[bsrow])
                with b.group("dve", s_dve):
                    b.op("dve", lambda E: E.tensor_tensor(
                        out=mr_row, in0=m_row, in1=rstd_row, op=OP.mult),
                        reads=[bsrow], writes=[bsrow])
                with b.group("pe", s_pe):
                    b.op("pe", lambda E: E.matmul(
                        out=pb[2][:, :], lhsT=ones_f[:, :], rhs=rstd_row,
                        start=True, stop=True), reads=[bsrow, bconsts], writes=[bpb[2]])
                with b.group("pe", s_pe):
                    b.op("pe", lambda E: E.matmul(
                        out=pb[3][:, :], lhsT=ones_f[:, :], rhs=mr_row,
                        start=True, stop=True), reads=[bsrow, bconsts], writes=[bpb[3]])

            def xn(gi):
                with b.group("dve", s_dve):
                    for k in range(4):
                        ct = gi * 4 + k
                        b.op("dve", lambda E, ct=ct: E.tensor_tensor(
                            out=stats[:, :], in0=nxt[:, ct * TCH:(ct + 1) * TCH],
                            in1=pb[2][:, :], op=OP.mult),
                            reads=[bnxt, bpb[2]], writes=[bstats])
                        b.op("dve", lambda E, ct=ct: E.tensor_tensor(
                            out=xnb[:, ct * TCH:(ct + 1) * TCH], in0=stats[:, :],
                            in1=pb[3][:, :], op=OP.subtract),
                            reads=[bstats, bpb[3]], writes=[bxnb, bxnb4[ct // 4]])
            return fills, tail, xn

        emit_x_load(0, x_in, bxch[0])
        for q in range(NCH):
            if q + 1 < NCH:
                emit_x_load(q + 1, x_in, bxch[(q + 1) % 2])
            if "f1" in phases:
                hf, ht_, hx = ffn_hooks(q)
                emit_ffn(q, True, xbuf, bdram_x, la_done=(q > 0),
                         hook_fills=hf, hook_tail=ht_, hook_xn=hx)
            else:
                dma("act", s_st[q % 2], xbuf[q], xch_t[q % 2][:, :],
                    [bxch[q % 2]], [bdram_x[q]])
        alias_handoff([bxk, bxv, bxr], [bh])
        alias_handoff([bsrr, bscan, b_ek, b_euk, b_sr, b_ekv, b_euv, b_U, b_D], [bysb])
        for i in range(3):
            alias_handoff([bwatt[i]], [bwgu[i]])
        alias_handoff([bwatt[3], bwatt[4]], [bwdn[0]])
        alias_handoff([bwatt[5]], [bwdn[1]])
        if "at" in phases:
            dma("sync", s_xch[0], xch_t[0][:, :], xbuf[0], [bdram_x[0]], [bxch[0]])
            emit_attn_pre(0)
            for q in range(NCH):
                emit_attn_main(q)
                fill = []
                if q + 1 < NCH:
                    dma("sync", s_xch[(q + 1) % 2], xch_t[(q + 1) % 2][:, :],
                        xbuf[q + 1], [bdram_x[q + 1]], [bxch[(q + 1) % 2]])
                    emit_attn_ln(q + 1)
                    fill = mix_ct_fns(q + 1)
                emit_attn_wo(q, fill)
        alias_handoff([bh], [bxk, bxv, bxr])
        alias_handoff([bysb], [bsrr, bscan, b_ek, b_euk, b_sr, b_ekv, b_euv, b_U, b_D])
        for i in range(3):
            alias_handoff([bwgu[i]], [bwatt[i]])
        alias_handoff([bwdn[0]], [bwatt[3], bwatt[4]])
        alias_handoff([bwdn[1]], [bwatt[5]])
        by_out = [Buf(f"yout{i}") for i in range(NCH)]
        dma("sync", s_xch[0], xch_t[0][:, :], xbuf[0], [bdram_x[0]], [bxch[0]])
        for q in range(NCH):
            if q + 1 < NCH:
                dma("sync", s_xch[(q + 1) % 2], xch_t[(q + 1) % 2][:, :],
                    xbuf[q + 1], [bdram_x[q + 1]], [bxch[(q + 1) % 2]])
            if "f2" in phases:
                hf, ht_, hx = ffn_hooks(q)
                emit_ffn(q, False, y_out, by_out, la_done=(q > 0),
                         hook_fills=hf, hook_tail=ht_, hook_xn=hx)
            else:
                dma("act", s_st[q % 2], y_out[q], xch_t[q % 2][:, :],
                    [bxch[q % 2]], [by_out[q]])

        for s in (s_st[0], s_st[1], s_dqst2[0], s_dqst2[1], s_dqpk[0], s_dqpk[1], s_init, s_cc):
            b.raw_wait("gp", s, s.planned)

        for s in b.sems.values():
            s.h = es.enter_context(nc.semaphore(s.name))
        blk = es.enter_context(nc.Block())
        blk.tensor(lambda E: _run_plan(b, "pe", E))
        blk.vector(lambda E: _run_plan(b, "dve", E))
        blk.scalar(lambda E: _run_plan(b, "act", E))
        blk.sync(lambda E: _run_plan(b, "sync", E))
        blk.gpsimd(lambda E: _run_plan(b, "gp", E))

    return nc



def build_in_maps(inp, TT=T):
    x = np.asarray(inp["L0_attn_x"])
    B = x.shape[0]
    wk_t = _tile_lhsT(np.ascontiguousarray(np.asarray(inp["L0_attn_Wk"]).T[ORDER, :]))
    wv_t = _tile_lhsT(np.ascontiguousarray(np.asarray(inp["L0_attn_Wv"]).T[ORDER, :]))
    wr_t = _tile_lhsT(np.ascontiguousarray(np.asarray(inp["L0_attn_Wr"]).T[ORDER, :]))
    wo_t = _tile_lhsT(np.ascontiguousarray(np.asarray(inp["L0_attn_Wo"]).T[:, ORDER]))
    _gupk = np.asarray(inp["L0_ffn_gate_up_packed"])
    gu_pk = np.ascontiguousarray(
        np.concatenate([_gupk[0:IDIM][ORDER_I], _gupk[IDIM:][ORDER_I]], axis=0))
    dn_pk = np.ascontiguousarray(np.asarray(inp["L0_ffn_down_packed"])[ORDER, :])
    gsc0 = np.asarray(inp["L0_ffn_gate_up_scales"])
    gsc = np.concatenate([gsc0[:, 0:IDIM][:, ORDER_I], gsc0[:, IDIM:][:, ORDER_I]], axis=1)
    gu_sc = np.ascontiguousarray(
        gsc.T.reshape(GU_UNITS, 128, 16).transpose(1, 0, 2).reshape(128, GU_UNITS * 16))
    dsc = np.asarray(inp["L0_ffn_down_scales"])[:, ORDER]
    dn_sc = np.ascontiguousarray(
        dsc.T.reshape(CT, 128, 44).transpose(1, 0, 2).reshape(128, CT * 44))
    vt = {"v_td": _vec_tile(np.asarray(inp["L0_attn_time_decay"])),
          "v_tf": _vec_tile(np.asarray(inp["L0_attn_time_first"])),
          "v_tmk": _vec_tile(np.asarray(inp["L0_attn_time_mix_k"])[ORDER]),
          "v_tmv": _vec_tile(np.asarray(inp["L0_attn_time_mix_v"])[ORDER]),
          "v_tmr": _vec_tile(np.asarray(inp["L0_attn_time_mix_r"])[ORDER])}
    ident = np.eye(128, dtype=ml_dtypes.bfloat16)
    in_maps = []
    for c in range(B):
        im = {"x_in": _chunks_fm(np.ascontiguousarray(x[c][:TT, ORDER])),
              "gu_pk": gu_pk, "gu_sc": gu_sc, "dn_pk": dn_pk, "dn_sc": dn_sc,
              "wk_t": wk_t, "wv_t": wv_t, "wr_t": wr_t, "wo_t": wo_t,
              "ident_in": ident}
        im.update(vt)
        in_maps.append(im)
    return in_maps


def kernel(L0_attn_x, L0_attn_time_decay, L0_attn_time_first, L0_attn_time_mix_k,
           L0_attn_time_mix_v, L0_attn_time_mix_r, L0_attn_Wk, L0_attn_Wv,
           L0_attn_Wr, L0_attn_Wo, L0_ffn_gate_up_packed, L0_ffn_gate_up_scales,
           L0_ffn_down_packed, L0_ffn_down_scales):
    inp = dict(L0_attn_x=L0_attn_x, L0_attn_time_decay=L0_attn_time_decay,
               L0_attn_time_first=L0_attn_time_first, L0_attn_time_mix_k=L0_attn_time_mix_k,
               L0_attn_time_mix_v=L0_attn_time_mix_v, L0_attn_time_mix_r=L0_attn_time_mix_r,
               L0_attn_Wk=L0_attn_Wk, L0_attn_Wv=L0_attn_Wv, L0_attn_Wr=L0_attn_Wr,
               L0_attn_Wo=L0_attn_Wo, L0_ffn_gate_up_packed=L0_ffn_gate_up_packed,
               L0_ffn_gate_up_scales=L0_ffn_gate_up_scales,
               L0_ffn_down_packed=L0_ffn_down_packed,
               L0_ffn_down_scales=L0_ffn_down_scales)
    B = np.asarray(L0_attn_x).shape[0]
    nc = build_nc(T, shard=False)
    in_maps = build_in_maps(inp, T)
    res = run_bass_kernel_spmd(nc, in_maps, core_ids=list(range(B)))
    out = np.empty((B, T, CDIM), dtype=np.float32)
    for c in range(B):
        out[c][:, ORDER] = _unchunk_fm(res.results[c]["y_out"])
    return out


# version 26
# speedup vs baseline: 1.0730x; 1.0730x over previous
import numpy as np
import ml_dtypes
import concourse.bass as bass
import concourse.mybir as mybir
from concourse.bass_utils import run_bass_kernel_spmd

F32 = mybir.dt.float32
F16 = mybir.dt.float16
BF16 = mybir.dt.bfloat16
I32 = mybir.dt.int32
AF = mybir.ActivationFunctionType
OP = mybir.AluOpType
LN2 = float(np.log(2.0))
LNHALF = float(np.log(0.5))

T = 2048
CDIM = 2048
IDIM = 5632
TCH = 512
CT = CDIM // 128
IT = IDIM // 128
GU_UNITS = 2 * IT
RECIP_C = 1.0 / 2048.0

ENGS = ("pe", "dve", "act", "sync", "gp")

_PERM = np.empty(CDIM, np.int64)
for _w in range(256):
    for _j in range(8):
        _PERM[8 * _w + _j] = _j * 256 + _w
ORDER = np.argsort(_PERM)
_PERM_I = np.empty(IDIM, np.int64)
for _w in range(704):
    for _j in range(8):
        _PERM_I[8 * _w + _j] = _j * 704 + _w
ORDER_I = np.argsort(_PERM_I)


class Sem:
    __slots__ = ("name", "planned", "h")

    def __init__(self, name):
        self.name = name
        self.planned = 0
        self.h = None


class Buf:
    __slots__ = ("name", "writer", "readers")

    def __init__(self, name):
        self.name = name
        self.writer = None
        self.readers = []


class PlOp:
    __slots__ = ("fn", "inc")

    def __init__(self, fn):
        self.fn = fn
        self.inc = None


class Builder:
    def __init__(self):
        self.plan = {e: [] for e in ENGS}
        self.sems = {}
        self.hw = {e: {} for e in ENGS}
        self.epoch = {e: 0 for e in ENGS}
        self.drain_at = {e: -1 for e in ENGS}
        self.cur_group = None

    def sem(self, name):
        s = Sem(name)
        self.sems[name] = s
        return s

    def _wait(self, eng, sem, val):
        if val <= 0:
            return
        if val <= self.hw[eng].get(sem.name, 0):
            return
        assert val <= sem.planned, (
            f"wait on {sem.name}={val} but only {sem.planned} planned (eng {eng})")
        self.hw[eng][sem.name] = val
        self.plan[eng].append(PlOp(lambda E, s=sem, v=val: E.wait_ge(s.h, v)))

    def _drain(self, eng):
        self.plan[eng].append(PlOp(lambda E: E.drain()))
        self.drain_at[eng] = self.epoch[eng]

    def group(self, eng, sem, n=1):
        assert self.cur_group is None
        self.cur_group = [eng, sem, n, None]
        return _GroupCtx(self)

    def _edge(self, eng, dep, this_is_dma):
        deng, dsem, dval, dep_, disdma = dep
        if deng == eng and not disdma and not this_is_dma:
            if dep_ > self.drain_at[eng] and eng in ("dve", "act", "gp"):
                self._drain(eng)
        else:
            assert dsem is not None, f"edge from {deng} has no milestone"
            self._wait(eng, dsem, dval)

    def op(self, eng, fn, reads=(), writes=()):
        g = self.cur_group
        assert g is not None and g[0] == eng, f"op on {eng} outside its group"
        sem, n = g[1], g[2]
        is_dma = (n == 16)
        for b in reads:
            if b.writer is not None:
                self._edge(eng, b.writer, is_dma)
        for b in writes:
            for r in b.readers:
                self._edge(eng, r, is_dma)
            if b.writer is not None:
                w = b.writer
                if w[0] == eng and not w[4] and not is_dma:
                    pass
                else:
                    self._edge(eng, w, is_dma)
        self.epoch[eng] += 1
        ep = self.epoch[eng]
        o = PlOp(fn)
        self.plan[eng].append(o)
        g[3] = o
        val_after = sem.planned + n
        for b in writes:
            b.writer = (eng, sem, val_after, ep, is_dma)
            b.readers = []
        for b in reads:
            b.readers.append((eng, sem, val_after, ep, is_dma))

    def close_group(self):
        eng, sem, n, last = self.cur_group
        assert last is not None, "empty group"
        sem.planned += n
        last.inc = (sem, n)
        self.cur_group = None

    def raw_wait(self, eng, sem, val):
        assert self.cur_group is None
        self._wait(eng, sem, val)

    def bare(self, eng, fn):
        assert self.cur_group is None
        self.plan[eng].append(PlOp(fn))


class _GroupCtx:
    def __init__(self, b):
        self.b = b

    def __enter__(self):
        return self.b

    def __exit__(self, *a):
        if a[0] is None:
            self.b.close_group()
        return False


def _run_plan(b, eng_name, E):
    for o in b.plan[eng_name]:
        inst = o.fn(E)
        if o.inc is not None:
            sem, n = o.inc
            inst.then_inc(sem.h, n)



def _chunks_fm(x2d):
    nq = x2d.shape[0] // TCH
    out = np.empty((nq, 128, CT * TCH), dtype=x2d.dtype)
    for q in range(nq):
        blk = x2d[q * TCH:(q + 1) * TCH, :]
        bt = np.ascontiguousarray(blk.T)
        out[q] = bt.reshape(CT, 128, TCH).transpose(1, 0, 2).reshape(128, CT * TCH)
    return out


def _unchunk_fm(ych):
    nq = ych.shape[0]
    out = np.empty((nq * TCH, CDIM), dtype=ych.dtype)
    for q in range(nq):
        blk = ych[q].reshape(128, CT, TCH).transpose(1, 0, 2).reshape(CDIM, TCH)
        out[q * TCH:(q + 1) * TCH, :] = blk.T
    return out


def _tile_lhsT(arr):
    K, M = arr.shape
    kt, mt = K // 128, M // 128
    return np.ascontiguousarray(
        arr.reshape(kt, 128, mt, 128).transpose(2, 1, 0, 3).reshape(mt, 128, kt * 128))


def _vec_tile(v):
    return np.ascontiguousarray(v.reshape(CT, 128).T)



def build_nc(T_=T, phases=("f1", "at", "f2"), dbg=None, shard=True):
    NCH = T_ // TCH
    GU_SH = GU_UNITS // 8 if shard else GU_UNITS
    DN_SH = CT // 8 if shard else CT
    nc = bass.Bass("TRN2", target_bir_lowering=False, debug=False)

    x_in = nc.dram_tensor("x_in", [NCH, 128, CT * TCH], F32, kind="ExternalInput").ap()
    gu_pk = nc.dram_tensor("gu_pk", [GU_SH * 128, 256], I32, kind="ExternalInput").ap()
    gu_sc = nc.dram_tensor("gu_sc", [128, GU_SH * 16], F16, kind="ExternalInput").ap()
    dn_pk = nc.dram_tensor("dn_pk", [DN_SH * 128, 704], I32, kind="ExternalInput").ap()
    dn_sc = nc.dram_tensor("dn_sc", [128, DN_SH * 44], F16, kind="ExternalInput").ap()
    wk_t = nc.dram_tensor("wk_t", [CT, 128, CDIM], F32, kind="ExternalInput").ap()
    wv_t = nc.dram_tensor("wv_t", [CT, 128, CDIM], F32, kind="ExternalInput").ap()
    wr_t = nc.dram_tensor("wr_t", [CT, 128, CDIM], F32, kind="ExternalInput").ap()
    wo_t = nc.dram_tensor("wo_t", [CT, 128, CDIM], F32, kind="ExternalInput").ap()
    v_td = nc.dram_tensor("v_td", [128, CT], F32, kind="ExternalInput").ap()
    v_tf = nc.dram_tensor("v_tf", [128, CT], F32, kind="ExternalInput").ap()
    v_tmk = nc.dram_tensor("v_tmk", [128, CT], F32, kind="ExternalInput").ap()
    v_tmv = nc.dram_tensor("v_tmv", [128, CT], F32, kind="ExternalInput").ap()
    v_tmr = nc.dram_tensor("v_tmr", [128, CT], F32, kind="ExternalInput").ap()
    ident_in = nc.dram_tensor("ident_in", [128, 128], BF16, kind="ExternalInput").ap()
    y_out = nc.dram_tensor("y_out", [NCH, 128, CT * TCH], F32, kind="ExternalOutput").ap()
    dbg_out = nc.dram_tensor("dbg_out", [128, CT * TCH], F32, kind="ExternalOutput").ap()
    xbuf = nc.dram_tensor("xbuf", [NCH, 128, CT * TCH], F32).ap()
    wgu_full_t = nc.dram_tensor("wgu_full", [GU_UNITS * 128, CT * 128], BF16)
    wdn_full_t = nc.dram_tensor("wdn_full", [CT * 128, IT * 128], BF16)
    wgu_full = wgu_full_t.ap()
    wdn_full = wdn_full_t.ap()
    if shard:
        cgu_t = nc.dram_tensor("cgu", [GU_SH * 128, CT * 128], BF16)
        cdn_t = nc.dram_tensor("cdn", [DN_SH * 128, IT * 128], BF16)
        cgu, cdn = cgu_t.ap(), cdn_t.ap()
    else:
        cgu, cdn = wgu_full, wdn_full

    from contextlib import ExitStack
    es = ExitStack()
    b = Builder()

    def sb(name, shape, dt):
        return es.enter_context(nc.sbuf_tensor(name, shape, dt))

    with es:
        xchA = sb("xchA", [128, CT * TCH], F32)
        xchB = sb("xchB", [128, CT * TCH], F32)
        ar2 = sb("ar2", [128, 3 * CT * TCH], BF16)
        ar3 = sb("ar3", [128, CT * TCH], F32)
        xnb = sb("xnb", [128, CT * TCH], BF16)
        wgu_s = [sb(f"wgu{i}", [128, CT * 128], BF16) for i in range(3)]
        wdn_s = [sb(f"wdn{i}", [128, IT * 128], BF16) for i in range(2)]
        stats = sb("stats", [128, 512], F32)
        srow = sb("srow", [128, 1536], F32)
        ssg = sb("ssg", [128, 512], F32)
        y2b = sb("y2b", [128, 1024], BF16)
        scf_gu = None
        scf_dn = None
        a_sb = sb("a_sb", [128, CT], F32)
        u_sb = sb("u_sb", [128, CT], F32)
        tmk_sb = sb("tmk_sb", [128, CT], F32)
        tmv_sb = sb("tmv_sb", [128, CT], F32)
        tmr_sb = sb("tmr_sb", [128, CT], F32)
        td_sb = sb("td_sb", [128, CT], F32)
        cu_sb = sb("cu_sb", [128, CT], F32)
        cd_sb = sb("cd_sb", [128, CT], F32)
        bnd_sb = sb("bnd_sb", [128, CT], BF16)
        ident = sb("ident", [128, 128], BF16)
        ones_b = sb("ones_b", [128, 1], BF16)
        ones_f = sb("ones_f", [1, 128], F32)
        bias_ln2n = sb("bias_ln2n", [128, 1], F32)
        bias_one = sb("bias_one", [128, 1], F32)
        bias_eps5 = sb("bias_eps5", [128, 1], F32)
        bias_eps6 = sb("bias_eps6", [128, 1], F32)
        bias_lnh = sb("bias_lnh", [128, 1], F32)
        bias_n6 = sb("bias_n6", [128, 1], F32)
        bias_n2 = sb("bias_n2", [128, 1], F32)
        WB = 768
        dq_pk = [xchA[:, 0:WB].bitcast(I32)[:, 0:WB],
                 xchA[:, WB:2 * WB].bitcast(I32)[:, 0:WB]]
        dq_m3 = [xchA[:, (2 + p) * WB:(3 + p) * WB].bitcast(I32)[:, 0:WB]
                 for p in range(4)]
        dq_si = [xchA[:, (6 + p) * WB:(7 + p) * WB].bitcast(I32)[:, 0:WB]
                 for p in range(4)]
        scf_gu = xchB[:, 0:GU_SH * 16]
        scf_dn = xnb[:, 2112:2112 + 2 * DN_SH * 44].bitcast(F32)[:, 0:DN_SH * 44]
        _o = GU_SH * 16
        def _bcarve(off_f32, n_bf16):
            return xchB[:, off_f32:off_f32 + (n_bf16 + 1) // 2].bitcast(BF16)[:, 0:n_bf16]
        def _bcarve3(off_f32, n_bf16):
            return ar3[:, off_f32:off_f32 + (n_bf16 + 1) // 2].bitcast(BF16)[:, 0:n_bf16]
        dq_h0, dq_r2 = [], []
        for _s in range(8):
            dq_h0.append(_bcarve(_o, WB)); _o += WB // 2
            dq_r2.append(_bcarve(_o, WB)); _o += WB // 2
        assert _o <= 8192, _o
        dq_r1 = [_bcarve3(_s * WB // 2, WB) for _s in range(8)]
        dq_s2m = [_bcarve3(8 * WB // 2, WB), _bcarve3(9 * WB // 2, WB),
                  _bcarve3(7682, WB),
                  xchA[:, 7680:7680 + WB // 2].bitcast(BF16)[:, 0:WB]] + \
                 [xchA[:, (2 + _p) * WB:(2 + _p) * WB + WB // 2]
                  .bitcast(BF16)[:, 0:WB] for _p in range(4)]
        dq_ub = [ar2[:, 0:IT * 128], ar2[:, IT * 128:2 * IT * 128]]
        dq_vb = [ar2[:, 2 * IT * 128:2 * IT * 128 + 6144],
                 ar2[:, 2 * IT * 128 + 6144:2 * IT * 128 + 12288]]

        pb = [es.enter_context(nc.psum_tensor(f"pb{i}", [128, 512], F32)) for i in range(7)]
        ptb = es.enter_context(nc.psum_tensor("ptb", [128, 1024], BF16))

        s_pe = b.sem("s_pe")
        s_dve = b.sem("s_dve")
        s_act = b.sem("s_act")
        s_init = b.sem("s_init")
        s_xch = [b.sem("s_xchA"), b.sem("s_xchB")]
        s_wgu = [b.sem(f"s_wgu{i}") for i in range(3)]
        s_wdn = [b.sem(f"s_wdn{i}") for i in range(2)]
        s_watt = [b.sem(f"s_watt{i}") for i in range(6)]
        s_dqpk = [b.sem("s_dqpkA"), b.sem("s_dqpkB")]
        s_dqst2 = [b.sem("s_dqstA"), b.sem("s_dqstB")]
        s_ptst = [b.sem("s_ptstA"), b.sem("s_ptstB")]
        s_st = [b.sem("s_stA"), b.sem("s_stB")]
        s_cc = b.sem("s_cc")

        B_ = Buf
        bxch = [B_("xchA"), B_("xchB")]
        bxnb = B_("xnb")
        bxnb4 = [B_(f"xnb{i}") for i in range(4)]
        bh = B_("h")
        bxk, bxv, bxr = B_("xk"), B_("xv"), B_("xr")
        bysb = B_("ysb")
        bsrr = B_("srr")
        bscan = B_("scan")
        b_ek, b_euk, b_sr = B_("ek"), B_("euk"), B_("sr")
        b_ekv, b_euv = B_("ekv"), B_("euv")
        b_U, b_D = B_("scU"), B_("scD")
        bstats = B_("stats")
        bsrow = B_("srow")
        bssg = B_("ssg")
        by2b = [B_("y2b0"), B_("y2b1")]
        bwgu = [B_(f"wgu{i}") for i in range(3)]
        bwdn = [B_(f"wdn{i}") for i in range(2)]
        bvecs = B_("vecs")
        bconsts = B_("consts")
        bident = B_("ident")
        bcarry = B_("carry")
        bbnd = B_("bnd")
        bpb = [B_(f"pb{i}") for i in range(7)]
        bptb = B_("ptb")
        bdq = B_("dq")
        bscf = B_("scf")
        bdram_w = {("gu", j): B_(f"dwgu{j}") for j in range(GU_UNITS)}
        bdram_w.update({("dn", j): B_(f"dwdn{j}") for j in range(CT)})
        bdram_x = [B_(f"dxbuf{q}") for q in range(NCH)]

        watt_ap = [wgu_s[0][:, :], wgu_s[1][:, :], wgu_s[2][:, :],
                   wdn_s[0][:, 0:CT * 128], wdn_s[0][:, CT * 128:2 * CT * 128],
                   wdn_s[1][:, 0:CT * 128]]
        bwatt = [B_(f"watt{i}") for i in range(6)]

        srr_v = ar3[:, 0:CT * TCH // 2].bitcast(BF16)[:, 0:CT * TCH]
        scf32 = ar3
        sc_ek = scf32[:, 4096:4608]
        sc_euk = scf32[:, 4608:5120]
        sc_sr = scf32[:, 5120:5632]
        sc_ekv = scf32[:, 5632:6144]
        sc_euv = scf32[:, 6144:6656]
        sc_U = scf32[:, 6656:7169]
        sc_D = scf32[:, 7169:7682]

        h_v = ar2[:, 0:IT * TCH]
        xk_v = ar2[:, 0:CT * TCH]
        xv_v = ar2[:, CT * TCH:2 * CT * TCH]
        xr_v = ar2[:, 2 * CT * TCH:3 * CT * TCH]

        xch_t = [xchA, xchB]

        with b.group("dve", s_dve):
            b.op("dve", lambda E: E.memset(ones_b[:], 1.0), writes=[bconsts])
            b.op("dve", lambda E: E.memset(ones_f[:], 1.0), writes=[bconsts])
            b.op("dve", lambda E: E.memset(bias_ln2n[:], -LN2), writes=[bconsts])
            b.op("dve", lambda E: E.memset(bias_one[:], 1.0), writes=[bconsts])
            b.op("dve", lambda E: E.memset(bias_eps5[:], 1e-5), writes=[bconsts])
            b.op("dve", lambda E: E.memset(bias_eps6[:], 1e-6), writes=[bconsts])
            b.op("dve", lambda E: E.memset(bias_lnh[:], LNHALF), writes=[bconsts])
            b.op("dve", lambda E: E.memset(bias_n6[:], -6.0), writes=[bconsts])
            b.op("dve", lambda E: E.memset(bias_n2[:], -2.0), writes=[bconsts])
            b.op("dve", lambda E: E.memset(cu_sb[:], 0.0), writes=[bcarry])
            b.op("dve", lambda E: E.memset(cd_sb[:], 0.0), writes=[bcarry])
            b.op("dve", lambda E: E.memset(bnd_sb[:], 0.0), writes=[bbnd])

        def dma(eng, sem, out_ap, in_ap, reads, writes):
            with b.group(eng, sem, 16):
                b.op(eng, lambda E, o=out_ap, i=in_ap: E.dma_start(out=o, in_=i),
                     reads=reads, writes=writes)

        dma("sync", s_init, ident[:], ident_in, [], [bident])
        for v_ap, tile in ((v_td, td_sb), (v_tf, u_sb), (v_tmk, tmk_sb),
                           (v_tmv, tmv_sb), (v_tmr, tmr_sb)):
            dma("sync", s_init, tile[:], v_ap, [], [bvecs])
        sc_gu16 = xnb[:, 0:GU_SH * 16].bitcast(F16)[:, 0:GU_SH * 16]
        sc_dn16 = xnb[:, GU_SH * 16:GU_SH * 16 + DN_SH * 44].bitcast(F16)[:, 0:DN_SH * 44]
        dma("sync", s_init, sc_gu16, gu_sc, [], [bxnb, bxnb4[0]])
        dma("sync", s_init, sc_dn16, dn_sc, [], [bxnb, bxnb4[0]])
        def bump_to_total(bufs, eng, sem):
            for bb_ in bufs:
                if bb_.writer is not None:
                    bb_.writer = (eng, sem, sem.planned, bb_.writer[3], True)
        bump_to_total([bident, bvecs, bxnb, bxnb4[0]], "sync", s_init)
        with b.group("act", s_act):
            b.op("act", lambda E: E.activation(out=scf_gu[:], in_=sc_gu16, func=AF.Copy),
                 reads=[bxnb], writes=[bscf])
            b.op("act", lambda E: E.activation(out=scf_dn[:], in_=sc_dn16, func=AF.Copy),
                 reads=[bxnb], writes=[bscf])
            b.op("act", lambda E: E.activation(out=a_sb[:], in_=td_sb[:], func=AF.Exp),
                 reads=[bvecs], writes=[bvecs])
            b.op("act", lambda E: E.activation(out=a_sb[:], in_=a_sb[:], func=AF.Exp,
                                               scale=-1.0),
                 reads=[bvecs], writes=[bvecs])

        def alias_handoff(dsts, srcs):
            rd, wr = [], None
            for s in srcs:
                rd.extend(s.readers)
                if s.writer is not None:
                    rd.append(s.writer)
                    wr = s.writer
            for d in dsts:
                d.writer = wr
                d.readers = list(rd)

        def emit_pk(kind, u0, nb, slot):
            W = 256 if kind == "gu" else 704
            pk_full = gu_pk if kind == "gu" else dn_pk
            pkt = dq_pk[slot]
            for k in range(nb):
                dma("sync", s_dqpk[slot], pkt[:, k * W:(k + 1) * W],
                    pk_full[(u0 + k) * 128:(u0 + k + 1) * 128, :], [], [bdq_pk[slot]])

        def emit_dq_planes(kind, u0, nb, slot, fill, barrier):
            W = 256 if kind == "gu" else 704
            NW = nb * W
            pkt = dq_pk[slot]
            vb = dq_vb[slot]
            def g1(jq):
                with b.group("dve", s_dve):
                    for p in range(4):
                        j = 4 * jq + p
                        b.op("dve", lambda E, j=j, p=p: E.tensor_scalar(
                            out=dq_m3[p][:, 0:NW], in0=pkt[:, 0:NW], scalar1=4 * j,
                            scalar2=7, op0=OP.logical_shift_right, op1=OP.bitwise_and),
                            reads=[bdq_pk[slot]], writes=[bdq_i[p]])
                        b.op("dve", lambda E, j=j, p=p: E.tensor_scalar(
                            out=dq_si[p][:, 0:NW], in0=pkt[:, 0:NW], scalar1=4 * j + 3,
                            scalar2=1, op0=OP.logical_shift_right, op1=OP.bitwise_and),
                            reads=[bdq_pk[slot]], writes=[bdq_i[p]])
                    for p in range(4):
                        s8 = 4 * jq + p
                        b.op("dve", lambda E, p=p, s8=s8: E.tensor_scalar(
                            out=dq_h0[s8][:, 0:NW], in0=dq_m3[p][:, 0:NW], scalar1=0.5,
                            scalar2=None, op0=OP.mult),
                            reads=[bdq_i[p]], writes=[b_dqh[s8]])
                    for p in range(4):
                        s8 = 4 * jq + p
                        ws = [b_dqs[s8]] + ([bdq_i[p]] if s8 >= 4 else [])
                        b.op("dve", lambda E, p=p, s8=s8: E.tensor_scalar(
                            out=dq_s2m[s8][:, 0:NW], in0=dq_si[p][:, 0:NW], scalar1=-2.0,
                            scalar2=1.0, op0=OP.mult, op1=OP.add),
                            reads=[bdq_i[p]], writes=ws)

            def gact(jq):
                with b.group("act", s_act):
                    for p in range(4):
                        s8 = 4 * jq + p
                        b.op("act", lambda E, s8=s8: E.activation(
                            out=dq_r2[s8][:, 0:NW], in_=dq_h0[s8][:, 0:NW],
                            func=AF.Relu, scale=2.0, bias=bias_n6[:]),
                            reads=[b_dqh[s8], bconsts], writes=[b_dqB[s8]])

            def g2(jq):
                with b.group("dve", s_dve):
                    for p in range(4):
                        s8 = 4 * jq + p
                        b.op("dve", lambda E, s8=s8: E.tensor_scalar(
                            out=dq_r1[s8][:, 0:NW], in0=dq_h0[s8][:, 0:NW],
                            scalar1=-2.0, scalar2=0.0, op0=OP.add, op1=OP.max),
                            reads=[b_dqh[s8]], writes=[b_dqr[s8]])
                    for p in range(4):
                        s8 = 4 * jq + p
                        b.op("dve", lambda E, s8=s8: E.tensor_tensor(
                            out=dq_r2[s8][:, 0:NW], in0=dq_h0[s8][:, 0:NW],
                            in1=dq_r2[s8][:, 0:NW], op=OP.add),
                            reads=[b_dqh[s8], b_dqB[s8]], writes=[b_dqB[s8]])
                    for p in range(4):
                        s8 = 4 * jq + p
                        b.op("dve", lambda E, s8=s8: E.tensor_tensor(
                            out=dq_r2[s8][:, 0:NW], in0=dq_r2[s8][:, 0:NW],
                            in1=dq_r1[s8][:, 0:NW], op=OP.add),
                            reads=[b_dqB[s8], b_dqr[s8]], writes=[b_dqB[s8]])
                    for p in range(4):
                        s8 = 4 * jq + p
                        j = 4 * jq + p
                        CW = W * 8
                        out_ap = vb[:, 0:nb * CW] \
                            .rearrange("p (k r) -> p k r", r=CW)[:, :, j * W:j * W + W] \
                            if nb > 1 else vb[:, j * W:j * W + W]
                        rd = [b_dqs[s8], b_dqB[s8]] + ([bdq_i[p]] if s8 >= 4 else [])
                        b.op("dve", lambda E, s8=s8, out_ap=out_ap: E.tensor_tensor(
                            out=out_ap, in0=dq_s2m[s8][:, 0:NW], in1=dq_r2[s8][:, 0:NW],
                            op=OP.mult),
                            reads=rd,
                            writes=[bdq_vbB[slot], bdq_vbS[slot]])

            g1(0)
            gact(0)
            g1(1)
            gact(1)
            fill()
            fill()
            barrier()
            g2(0)
            fill()
            g2(1)
            fill()

        def emit_dq_tail(kind, u0, nb, slot):
            W = 256 if kind == "gu" else 704
            nct = CT if kind == "gu" else IT
            scf = scf_gu if kind == "gu" else scf_dn
            ngrp = W * 8 // 128
            vb = dq_vb[slot]
            ub = dq_ub[slot]
            for k in range(nb):
                u = u0 + k
                scoff = u * (16 if kind == "gu" else 44)
                CW = W * 8
                with b.group("act", s_act):
                    for g in range(ngrp):
                        ap_g = vb[:, k * CW:(k + 1) * CW] \
                            .rearrange("p (j r) -> p j r", r=W)[:, :, 16 * g:16 * g + 16]
                        b.op("act", lambda E, ap_g=ap_g, g=g, scoff=scoff, scf=scf: E.activation(
                            out=ap_g, in_=ap_g,
                            func=AF.Copy, scale=scf[:, scoff + g:scoff + g + 1]),
                            reads=[bdq_vbB[slot], bscf], writes=[bdq_vbS[slot]])
                yield
                for t8 in range((ngrp + 7) // 8):
                    n_in = min(8, ngrp - t8 * 8)
                    with b.group("pe", s_pe):
                        for kk in range(n_in):
                            g = t8 * 8 + kk
                            c0 = k * ngrp * 128 + g * 128
                            b.op("pe", lambda E, c0=c0, kk=kk, vb=vb: E.transpose(
                                out=ptb[:, kk * 128:(kk + 1) * 128],
                                in_=vb[:, c0:c0 + 128], identity=ident[:]),
                                reads=[bdq_vbS[slot], bident], writes=[bptb])
                    with b.group("act", s_act):
                        b.op("act", lambda E, t8=t8, n_in=n_in, ub=ub: E.activation(
                            out=ub[:, t8 * 1024:t8 * 1024 + n_in * 128],
                            in_=ptb[:, 0:n_in * 128], func=AF.Copy),
                            reads=[bptb], writes=[bdq_ubB[slot]])
                    yield
                dst = (cgu if kind == "gu" else cdn)[u * 128:(u + 1) * 128, :]
                dma("gp", s_dqst2[slot], dst, ub[:, 0:nct * 128], [bdq_ubB[slot]],
                    [(b_cgu if kind == "gu" else b_cdn) if shard else bdram_w[(kind, u)]])

        b_cgu, b_cdn = Buf("cgu"), Buf("cdn")
        bdq_pk = [Buf("dqpk0"), Buf("dqpk1")]
        bdq_i = [Buf(f"dqi{p}") for p in range(4)]
        bdq_a = [Buf(f"dqa{p}") for p in range(4)]
        b_dqh = [Buf(f"dqh{p}") for p in range(8)]
        b_dqs = [Buf(f"dqs{p}") for p in range(8)]
        b_dqB = [Buf(f"dqB{p}") for p in range(8)]
        b_dqr = [Buf(f"dqr{p}") for p in range(8)]
        bdq_v = [Buf("dqv0"), Buf("dqv1")]
        bdq_vbB = [Buf("dqvb0"), Buf("dqvb1")]
        bdq_vbS = [Buf("dqvbs0"), Buf("dqvbs1")]
        bdq_ubB = [Buf("dqub0"), Buf("dqub1")]
        batches = []
        u = 0
        while u < GU_SH:
            nb = min(3, GU_SH - u)
            batches.append(("gu", u, nb))
            u += nb
        u = 0
        while u < DN_SH:
            batches.append(("dn", u, 1))
            u += 1
        pending = []
        _S = object()

        def fill():
            while pending:
                if next(pending[0], _S) is _S:
                    pending.pop(0)
                    continue
                return

        def barrier():
            while len(pending) > 1:
                for _ in pending.pop(0):
                    pass

        emit_pk(*batches[0], 0)
        for bi, (kind_, u0_, nb_) in enumerate(batches):
            if bi + 1 < len(batches):
                emit_pk(*batches[bi + 1], (bi + 1) % 2)
            if bi > 0:
                k2, u2, n2 = batches[bi - 1]
                pending.append(emit_dq_tail(k2, u2, n2, (bi - 1) % 2))
            emit_dq_planes(kind_, u0_, nb_, bi % 2, fill, barrier)
        for g in pending:
            for _ in g:
                pass
        k2, u2, n2 = batches[-1]
        for _ in emit_dq_tail(k2, u2, n2, (len(batches) - 1) % 2):
            pass
        alias_handoff([bxch[0], bxch[1], bh, bxk, bxv, bxr, bysb, bsrr, bscan,
                       b_ek, b_euk, b_sr, b_ekv, b_euv, b_U, b_D, bpb[6], bxnb,
                       bxnb4[0], bxnb4[1], bxnb4[2], bxnb4[3]],
                      [bdq, bscf, bdq_v[0], bdq_v[1], bdq_i[0], bdq_i[1],
                       bdq_i[2], bdq_i[3], bdq_a[0], bdq_a[1],
                       bdq_a[2], bdq_a[3], bdq_pk[0], bdq_pk[1],
                       b_dqh[0], b_dqh[1], b_dqh[2], b_dqh[3],
                       b_dqh[4], b_dqh[5], b_dqh[6], b_dqh[7],
                       b_dqs[0], b_dqs[1], b_dqs[2], b_dqs[3],
                       b_dqs[4], b_dqs[5], b_dqs[6], b_dqs[7],
                       b_dqB[0], b_dqB[1], b_dqB[2], b_dqB[3],
                       b_dqB[4], b_dqB[5], b_dqB[6], b_dqB[7],
                       b_dqr[0], b_dqr[1], b_dqr[2], b_dqr[3],
                       b_dqr[4], b_dqr[5], b_dqr[6], b_dqr[7],
                       bdq_vbB[0], bdq_vbB[1], bdq_vbS[0], bdq_vbS[1],
                       bdq_ubB[0], bdq_ubB[1], bptb])
        if shard:
            bump_to_total([b_cgu], "gp", s_dqst2[0])
            bump_to_total([b_cdn], "gp", s_dqst2[1])
            with b.group("gp", s_cc, 16):
                b.op("gp", lambda E: E.collective_compute(
                    "AllGather", OP.bypass, replica_groups=[list(range(8))],
                    ins=[cgu_t.ap().opt()], outs=[wgu_full_t.ap().opt()]),
                    reads=[b_cgu], writes=[bdram_w[("gu", 0)]])
            with b.group("gp", s_cc, 16):
                b.op("gp", lambda E: E.collective_compute(
                    "AllGather", OP.bypass, replica_groups=[list(range(8))],
                    ins=[cdn_t.ap().opt()], outs=[wdn_full_t.ap().opt()]),
                    reads=[b_cdn], writes=[bdram_w[("dn", 0)]])
            for (kk, uu), bb_ in bdram_w.items():
                bb_.writer = bdram_w[(kk, 0)].writer
        else:
            for (kk_, uu_), bb_ in bdram_w.items():
                pass
            done0 = ("gp", s_dqst2[0], s_dqst2[0].planned, 0, True)
            done1 = ("gp", s_dqst2[1], s_dqst2[1].planned, 0, True)
            for bb_ in bdram_w.values():
                bb_.writer = done0
                bb_.readers = [done1]

        m_row = srow[0:1, 0:512]
        rstd_row = srow[0:1, 512:1024]
        mr_row = srow[0:1, 1024:1536]

        def emit_ln_a1(xch, bx, xb16, bxb16, subbufs=None):
            with b.group("dve", s_dve):
                for ct in range(CT):
                    ws = [bxb16] + ([subbufs[ct // 4]] if subbufs else [])
                    b.op("dve", lambda E, ct=ct, xch=xch, xb16=xb16: E.tensor_scalar(
                        out=xb16[:, ct * TCH:(ct + 1) * TCH],
                        in0=xch[:, ct * TCH:(ct + 1) * TCH],
                        scalar1=1.0, scalar2=None, op0=OP.mult),
                        reads=[bx], writes=ws)

        def emit_ln_a2(xch, bx, xsq, bxsq):
            for gi in range(4):
                with b.group("act", s_act):
                    for k in range(4):
                        ct = gi * 4 + k
                        b.op("act", lambda E, ct=ct, xch=xch, xsq=xsq: E.activation(
                            out=xsq[:, ct * TCH:(ct + 1) * TCH],
                            in_=xch[:, ct * TCH:(ct + 1) * TCH], func=AF.Square),
                            reads=[bx], writes=[bxsq])

        def emit_ln(xch, bx, xb16, bxb16, xsq, bxsq, st0, st1, bc0, bc1,
                    prepped=False, subbufs=None):
            if not prepped:
                emit_ln_a1(xch, bx, xb16, bxb16, subbufs=subbufs)
                emit_ln_a2(xch, bx, xsq, bxsq)
            with b.group("pe", s_pe):
                for ct in range(CT):
                    b.op("pe", lambda E, ct=ct, xb16=xb16: E.matmul(
                        out=pb[st0][0:1, :], lhsT=ones_b[:, :],
                        rhs=xb16[:, ct * TCH:(ct + 1) * TCH],
                        start=(ct == 0), stop=(ct == CT - 1)),
                        reads=[bxb16, bconsts], writes=[bpb[st0]])
            with b.group("pe", s_pe):
                for ct in range(CT):
                    b.op("pe", lambda E, ct=ct, xsq=xsq: E.matmul(
                        out=pb[st1][0:1, :], lhsT=ones_b[:, :],
                        rhs=xsq[:, ct * TCH:(ct + 1) * TCH],
                        start=(ct == 0), stop=(ct == CT - 1)),
                        reads=[bxsq, bconsts], writes=[bpb[st1]])
            with b.group("dve", s_dve):
                b.op("dve", lambda E: E.tensor_scalar(
                    out=m_row, in0=pb[st0][0:1, :], scalar1=RECIP_C, scalar2=None,
                    op0=OP.mult), reads=[bpb[st0]], writes=[bsrow])
                b.op("dve", lambda E: E.tensor_scalar(
                    out=rstd_row, in0=pb[st1][0:1, :], scalar1=RECIP_C, scalar2=None,
                    op0=OP.mult), reads=[bpb[st1]], writes=[bsrow])
                b.op("dve", lambda E: E.scalar_tensor_tensor(
                    out=mr_row, in0=m_row, scalar=1.0, in1=m_row,
                    op0=OP.mult, op1=OP.mult), reads=[bsrow], writes=[bsrow])
                b.op("dve", lambda E: E.tensor_tensor(
                    out=rstd_row, in0=rstd_row, in1=mr_row, op=OP.subtract),
                    reads=[bsrow], writes=[bsrow])
            with b.group("act", s_act):
                b.op("act", lambda E: E.activation(
                    out=rstd_row, in_=rstd_row, func=AF.Ln, bias=bias_eps5[0:1, :]),
                    reads=[bsrow, bconsts], writes=[bsrow])
                b.op("act", lambda E: E.activation(
                    out=rstd_row, in_=rstd_row, func=AF.Exp, scale=-0.5),
                    reads=[bsrow], writes=[bsrow])
            with b.group("dve", s_dve):
                b.op("dve", lambda E: E.tensor_tensor(
                    out=mr_row, in0=m_row, in1=rstd_row, op=OP.mult),
                    reads=[bsrow], writes=[bsrow])
            with b.group("pe", s_pe):
                b.op("pe", lambda E: E.matmul(
                    out=pb[bc0][:, :], lhsT=ones_f[:, :], rhs=rstd_row,
                    start=True, stop=True), reads=[bsrow, bconsts], writes=[bpb[bc0]])
            with b.group("pe", s_pe):
                b.op("pe", lambda E: E.matmul(
                    out=pb[bc1][:, :], lhsT=ones_f[:, :], rhs=mr_row,
                    start=True, stop=True), reads=[bsrow, bconsts], writes=[bpb[bc1]])

        def emit_x_load(q, src_ap, bx):
            dma("sync", s_xch[q % 2], xch_t[q % 2][:, :], src_ap[q], [], [bx])

        def emit_ffn(q, src_is_input, dst_ap, bdst, la_done=False,
                     hook_fills=None, hook_tail=None, hook_xn=None):
            xch = xch_t[q % 2]
            bx = bxch[q % 2]
            if not la_done:
                emit_ln(xch, bx, xnb, bxnb, h_v, bh, 4, 5, 4, 5,
                        prepped=False, subbufs=bxnb4)
                for gi in range(4):
                    with b.group("dve", s_dve):
                        for k in range(4):
                            ct = gi * 4 + k
                            b.op("dve", lambda E, ct=ct: E.tensor_tensor(
                                out=stats[:, :], in0=xch[:, ct * TCH:(ct + 1) * TCH],
                                in1=pb[4][:, :], op=OP.mult),
                                reads=[bx, bpb[4]], writes=[bstats])
                            b.op("dve", lambda E, ct=ct: E.tensor_tensor(
                                out=xnb[:, ct * TCH:(ct + 1) * TCH], in0=stats[:, :],
                                in1=pb[5][:, :], op=OP.subtract),
                                reads=[bstats, bpb[5]], writes=[bxnb, bxnb4[ct // 4]])
            if dbg == "lnonly":
                dma("act", s_st[q % 2], dst_ap[q], xch[:, :], [bx], [bdst[q]])
                return
            def gu_load(j, which, slot):
                u = j if which == 0 else j + IT
                dma("sync", s_wgu[slot], wgu_s[slot][:, :],
                    wgu_full[u * 128:(u + 1) * 128, :],
                    [bdram_w[("gu", u)]], [bwgu[slot]])
            if not la_done:
                gu_load(0, 0, 0)
                gu_load(0, 1, 1)
            for j in range(IT):
                sg_, su_ = (2 * j) % 3, (2 * j + 1) % 3
                pg, pu = 2 * (j % 2), 2 * (j % 2) + 1
                with b.group("pe", s_pe):
                    for ct in range(CT):
                        b.op("pe", lambda E, j=j, ct=ct, sg_=sg_, pg=pg: E.matmul(
                            out=pb[pg][:, :], lhsT=wgu_s[sg_][:, ct * 128:(ct + 1) * 128],
                            rhs=xnb[:, ct * TCH:(ct + 1) * TCH],
                            start=(ct == 0), stop=(ct == CT - 1)),
                            reads=[bwgu[sg_], bxnb4[ct // 4]], writes=[bpb[pg]])
                with b.group("pe", s_pe):
                    for ct in range(CT):
                        b.op("pe", lambda E, j=j, ct=ct, su_=su_, pu=pu: E.matmul(
                            out=pb[pu][:, :], lhsT=wgu_s[su_][:, ct * 128:(ct + 1) * 128],
                            rhs=xnb[:, ct * TCH:(ct + 1) * TCH],
                            start=(ct == 0), stop=(ct == CT - 1)),
                            reads=[bwgu[su_], bxnb4[ct // 4]], writes=[bpb[pu]])
                if j + 1 < IT:
                    gu_load(j + 1, 0, (2 * j + 2) % 3)
                    gu_load(j + 1, 1, (2 * j + 3) % 3)
                with b.group("act", s_act):
                    b.op("act", lambda E, pg=pg: E.activation(
                        out=ssg[:, :], in_=pb[pg][:, :], func=AF.Silu),
                        reads=[bpb[pg]], writes=[bssg])
                with b.group("dve", s_dve):
                    b.op("dve", lambda E, j=j, pu=pu: E.tensor_tensor(
                        out=h_v[:, j * TCH:(j + 1) * TCH], in0=ssg[:, :],
                        in1=pb[pu][:, :], op=OP.mult),
                        reads=[bssg, bpb[pu]], writes=[bh])
            fill_a1 = list(hook_fills()) if hook_fills is not None else []
            if dbg == "xnb" and q == 0:
                dma("gp", s_dqst, dbg_out, xnb[:, :], [bxnb], [Buf("dbg")])
            if dbg == "srow" and q == 0:
                dma("gp", s_dqst, dbg_out[0:1, 0:1536], srow[0:1, :],
                    [bsrow], [Buf("dbg")])
            if dbg == "h" and q == 0:
                dma("gp", s_dqst, dbg_out, h_v[:, 0:CT * TCH], [bh], [Buf("dbg")])
            if dbg == "guonly":
                dma("act", s_st[q % 2], dst_ap[q], xch[:, :], [bx], [bdst[q]])
                return
            dma("sync", s_wdn[0], wdn_s[0][:, :], wdn_full[0:128, :],
                [bdram_w[("dn", 0)]], [bwdn[0]])
            dma("sync", s_wdn[1], wdn_s[1][:, :], wdn_full[128:256, :],
                [bdram_w[("dn", 1)]], [bwdn[1]])
            pend_sq = None
            for ht in range(CT):
                sl = ht % 2
                py = 4 + (ht % 2)
                with b.group("pe", s_pe):
                    for it in range(IT):
                        b.op("pe", lambda E, ht=ht, it=it, sl=sl, py=py: E.matmul(
                            out=pb[py][:, :], lhsT=wdn_s[sl][:, it * 128:(it + 1) * 128],
                            rhs=h_v[:, it * TCH:(it + 1) * TCH],
                            start=(it == 0), stop=(it == IT - 1)),
                            reads=[bwdn[sl], bh], writes=[bpb[py]])
                if ht + 2 < CT:
                    dma("sync", s_wdn[(ht + 2) % 2], wdn_s[(ht + 2) % 2][:, :],
                        wdn_full[(ht + 2) * 128:(ht + 3) * 128, :],
                        [bdram_w[("dn", ht + 2)]], [bwdn[(ht + 2) % 2]])
                if pend_sq is not None:
                    hp, slot = pend_sq
                    with b.group("pe", s_pe):
                        b.op("pe", lambda E, hp=hp, slot=slot: E.matmul(
                            out=pb[6][0:1, :], lhsT=ones_b[:, :],
                            rhs=y2b[:, slot * 512:(slot + 1) * 512],
                            start=(hp == 0), stop=(hp == CT - 1)),
                            reads=[by2b[slot], bconsts], writes=[bpb[6]])
                with b.group("act", s_act):
                    b.op("act", lambda E, ht=ht, py=py: E.activation(
                        out=y2b[:, (ht % 2) * 512:(ht % 2 + 1) * 512],
                        in_=pb[py][:, :], func=AF.Square),
                        reads=[bpb[py]], writes=[by2b[ht % 2]])
                with b.group("dve", s_dve):
                    b.op("dve", lambda E, ht=ht, py=py: E.tensor_scalar(
                        out=ar3[:, ht * TCH:(ht + 1) * TCH], in0=pb[py][:, :],
                        scalar1=1.0, scalar2=None, op0=OP.mult),
                        reads=[bpb[py], by2b[ht % 2]], writes=[bysb])
                if fill_a1:
                    fill_a1.pop(0)()
                if fill_a1:
                    fill_a1.pop(0)()
                pend_sq = (ht, ht % 2)
            hp, slot = pend_sq
            with b.group("pe", s_pe):
                b.op("pe", lambda E, hp=hp, slot=slot: E.matmul(
                    out=pb[6][0:1, :], lhsT=ones_b[:, :],
                    rhs=y2b[:, slot * 512:(slot + 1) * 512],
                    start=(hp == 0), stop=(hp == CT - 1)),
                    reads=[by2b[slot], bconsts], writes=[bpb[6]])
            if dbg == "ysb" and q == 0:
                dma("gp", s_dqst, dbg_out, ar3[:, 0:8192], [bysb], [Buf("dbg")])
            with b.group("dve", s_dve):
                b.op("dve", lambda E: E.tensor_scalar(
                    out=m_row, in0=pb[6][0:1, :], scalar1=RECIP_C, scalar2=None,
                    op0=OP.mult), reads=[bpb[6]], writes=[bsrow])
            with b.group("act", s_act):
                b.op("act", lambda E: E.activation(
                    out=m_row, in_=m_row, func=AF.Ln, bias=bias_eps6[0:1, :]),
                    reads=[bsrow, bconsts], writes=[bsrow])
                b.op("act", lambda E: E.activation(
                    out=m_row, in_=m_row, func=AF.Exp, scale=-0.5, bias=bias_lnh[0:1, :]),
                    reads=[bsrow, bconsts], writes=[bsrow])
            with b.group("pe", s_pe):
                b.op("pe", lambda E: E.matmul(
                    out=pb[4][:, :], lhsT=ones_f[:, :], rhs=m_row,
                    start=True, stop=True), reads=[bsrow, bconsts], writes=[bpb[4]])
            if hook_tail is not None:
                hook_tail()
            for gi in range(4):
                if hook_xn is not None:
                    hook_xn(gi)
                with b.group("dve", s_dve):
                    for k in range(4):
                        ct = gi * 4 + k
                        b.op("dve", lambda E, ct=ct: E.tensor_tensor(
                            out=stats[:, :], in0=ar3[:, ct * TCH:(ct + 1) * TCH],
                            in1=pb[4][:, :], op=OP.mult),
                            reads=[bysb, bpb[4]], writes=[bstats])
                        b.op("dve", lambda E, ct=ct: E.tensor_tensor(
                            out=xch[:, ct * TCH:(ct + 1) * TCH],
                            in0=xch[:, ct * TCH:(ct + 1) * TCH],
                            in1=stats[:, :], op=OP.add),
                            reads=[bstats], writes=[bx])
            for f in fill_a1:
                f()
            dma("act", s_st[q % 2], dst_ap[q], xch[:, :], [bx], [bdst[q]])

        def watt_load(mat_ap, unit, slot):
            dma("gp", s_watt[slot], watt_ap[slot], mat_ap[unit], [], [bwatt[slot]])

        def emit_attn_ln(q):
            xch = xch_t[q % 2]
            bx = bxch[q % 2]
            emit_ln(xch, bx, xk_v, bxk, xv_v, bxv, 0, 1, 0, 1)

        def mix_ct_fns(q):
            xch = xch_t[q % 2]
            bx = bxch[q % 2]
            xn_t = xnb[:, 0:513]
            bxn_t = bxnb
            def mk(ct):
                return lambda: emit_mix_ct(q, ct)
            return [mk(ct) for ct in range(CT)]

        def emit_mix_ct(q, ct):
            xch = xch_t[q % 2]
            bx = bxch[q % 2]
            xn_t = xnb[:, 0:513]
            bxn_t = bxnb
            if True:
                with b.group("dve", s_dve):
                    b.op("dve", lambda E, ct=ct: E.tensor_copy(
                        out=xn_t[:, 0:1], in_=bnd_sb[:, ct:ct + 1]),
                        reads=[bbnd], writes=[bxn_t, bxnb4[0]])
                    b.op("dve", lambda E, ct=ct: E.tensor_tensor(
                        out=stats[:, :], in0=xch[:, ct * TCH:(ct + 1) * TCH],
                        in1=pb[0][:, :], op=OP.mult),
                        reads=[bx, bpb[0]], writes=[bstats])
                    b.op("dve", lambda E, ct=ct: E.tensor_tensor(
                        out=xn_t[:, 1:513], in0=stats[:, :], in1=pb[1][:, :],
                        op=OP.subtract), reads=[bstats, bpb[1]], writes=[bxn_t, bxnb4[0]])
                    b.op("dve", lambda E, ct=ct: E.tensor_copy(
                        out=bnd_sb[:, ct:ct + 1], in_=xn_t[:, 512:513]),
                        reads=[bxn_t], writes=[bbnd])
                    b.op("dve", lambda E, ct=ct: E.tensor_tensor(
                        out=y2b[:, 0:512], in0=xn_t[:, 1:513], in1=xn_t[:, 0:512],
                        op=OP.subtract), reads=[bxn_t], writes=[by2b[0]])
                    for tm_sb, xdst, bdst_ in ((tmk_sb, xk_v, bxk), (tmv_sb, xv_v, bxv),
                                               (tmr_sb, xr_v, bxr)):
                        b.op("dve", lambda E, ct=ct, tm_sb=tm_sb, xdst=xdst: E.scalar_tensor_tensor(
                            out=xdst[:, ct * TCH:(ct + 1) * TCH], in0=y2b[:, 0:512],
                            scalar=tm_sb[:, ct:ct + 1], in1=xn_t[:, 0:512],
                            op0=OP.mult, op1=OP.add),
                            reads=[by2b[0], bxn_t, bvecs], writes=[bdst_])
        def emit_attn_pre(q):
            emit_attn_ln(q)
            for f in mix_ct_fns(q):
                f()

        def emit_attn_main(q):
            xch = xch_t[q % 2]
            bx = bxch[q % 2]
            watt_load(wk_t, 0, 0)
            watt_load(wv_t, 0, 1)
            watt_load(wr_t, 0, 2)
            for j in range(CT):
                s0, s1, s2 = (3 * j) % 6, (3 * j + 1) % 6, (3 * j + 2) % 6
                if j + 1 < CT:
                    watt_load(wk_t, j + 1, (3 * j + 3) % 6)
                    watt_load(wv_t, j + 1, (3 * j + 4) % 6)
                    watt_load(wr_t, j + 1, (3 * j + 5) % 6)
                pk_, pv_, pr_ = 3 * (j % 2), 3 * (j % 2) + 1, 3 * (j % 2) + 2
                for (slot, src, pdst) in ((s0, bxk, pk_), (s1, bxv, pv_), (s2, bxr, pr_)):
                    xsrc = {id(bxk): xk_v, id(bxv): xv_v, id(bxr): xr_v}[id(src)]
                    with b.group("pe", s_pe):
                        for ct in range(CT):
                            b.op("pe", lambda E, slot=slot, ct=ct, xsrc=xsrc, pdst=pdst: E.matmul(
                                out=pb[pdst][:, :],
                                lhsT=watt_ap[slot][:, ct * 128:(ct + 1) * 128],
                                rhs=xsrc[:, ct * TCH:(ct + 1) * TCH],
                                start=(ct == 0), stop=(ct == CT - 1)),
                                reads=[bwatt[slot], src], writes=[bpb[pdst]])
                with b.group("act", s_act):
                    b.op("act", lambda E, pk_=pk_: E.activation(
                        out=sc_ek, in_=pb[pk_][:, :], func=AF.Exp),
                        reads=[bpb[pk_]], writes=[b_ek])
                    b.op("act", lambda E, pk_=pk_, j=j: E.activation(
                        out=sc_euk, in_=pb[pk_][:, :], func=AF.Exp, bias=u_sb[:, j:j + 1]),
                        reads=[bpb[pk_], bvecs], writes=[b_euk])
                    b.op("act", lambda E, pr_=pr_: E.activation(
                        out=sc_sr, in_=pb[pr_][:, :], func=AF.Exp, scale=-1.0),
                        reads=[bpb[pr_]], writes=[b_sr])
                    b.op("act", lambda E: E.activation(
                        out=sc_sr, in_=sc_sr, func=AF.Ln, bias=bias_one[:]),
                        reads=[b_sr, bconsts], writes=[b_sr])
                    b.op("act", lambda E: E.activation(
                        out=sc_sr, in_=sc_sr, func=AF.Exp, scale=-1.0),
                        reads=[b_sr], writes=[b_sr])
                with b.group("dve", s_dve):
                    b.op("dve", lambda E, pv_=pv_: E.tensor_tensor(
                        out=sc_ekv, in0=sc_ek, in1=pb[pv_][:, :], op=OP.mult),
                        reads=[b_ek, bpb[pv_]], writes=[b_ekv])
                    b.op("dve", lambda E, pv_=pv_: E.tensor_tensor(
                        out=sc_euv, in0=sc_euk, in1=pb[pv_][:, :], op=OP.mult),
                        reads=[b_euk, bpb[pv_]], writes=[b_euv])
                    b.op("dve", lambda E, j=j: E.tensor_tensor_scan(
                        out=sc_U[:, 1:513], data0=a_sb[:, j:j + 1].to_broadcast([128, 512]),
                        data1=sc_ekv, initial=cu_sb[:, j:j + 1],
                        op0=OP.mult, op1=OP.add),
                        reads=[b_ekv, bvecs, bcarry], writes=[b_U])
                    b.op("dve", lambda E, j=j: E.tensor_copy(
                        out=sc_U[:, 0:1], in_=cu_sb[:, j:j + 1]),
                        reads=[bcarry], writes=[b_U])
                    b.op("dve", lambda E, j=j: E.tensor_copy(
                        out=cu_sb[:, j:j + 1], in_=sc_U[:, 512:513]),
                        reads=[b_U], writes=[bcarry])
                    b.op("dve", lambda E, j=j: E.tensor_tensor_scan(
                        out=sc_D[:, 1:513], data0=a_sb[:, j:j + 1].to_broadcast([128, 512]),
                        data1=sc_ek, initial=cd_sb[:, j:j + 1],
                        op0=OP.mult, op1=OP.add),
                        reads=[b_ek, bvecs, bcarry], writes=[b_D])
                    b.op("dve", lambda E, j=j: E.tensor_copy(
                        out=sc_D[:, 0:1], in_=cd_sb[:, j:j + 1]),
                        reads=[bcarry], writes=[b_D])
                    b.op("dve", lambda E, j=j: E.tensor_copy(
                        out=cd_sb[:, j:j + 1], in_=sc_D[:, 512:513]),
                        reads=[b_D], writes=[bcarry])
                    b.op("dve", lambda E: E.tensor_tensor(
                        out=sc_euv, in0=sc_U[:, 0:512], in1=sc_euv, op=OP.add),
                        reads=[b_U, b_euv], writes=[b_euv])
                    b.op("dve", lambda E: E.tensor_tensor(
                        out=sc_euk, in0=sc_D[:, 0:512], in1=sc_euk, op=OP.add),
                        reads=[b_D, b_euk], writes=[b_euk])
                with b.group("act", s_act):
                    b.op("act", lambda E: E.activation(
                        out=sc_ek, in_=sc_euk, func=AF.Ln),
                        reads=[b_euk], writes=[b_ek])
                    b.op("act", lambda E: E.activation(
                        out=sc_ek, in_=sc_ek, func=AF.Exp, scale=-1.0),
                        reads=[b_ek], writes=[b_ek])
                with b.group("dve", s_dve):
                    b.op("dve", lambda E: E.tensor_tensor(
                        out=sc_ekv, in0=sc_euv, in1=sc_ek, op=OP.mult),
                        reads=[b_euv, b_ek], writes=[b_ekv])
                    b.op("dve", lambda E, j=j: E.tensor_tensor(
                        out=srr_v[:, j * TCH:(j + 1) * TCH], in0=sc_ekv, in1=sc_sr,
                        op=OP.mult), reads=[b_ekv, b_sr], writes=[bsrr])
        def emit_attn_wo(q, fillers=()):
            fillers = list(fillers)
            xch = xch_t[q % 2]
            bx = bxch[q % 2]
            watt_load(wo_t, 0, 0)
            watt_load(wo_t, 1, 1)
            for ct in range(CT):
                slot = ct % 6
                if ct + 2 < CT:
                    watt_load(wo_t, ct + 2, (ct + 2) % 6)
                po = 2 + ct % 2
                with b.group("pe", s_pe):
                    for dt in range(CT):
                        b.op("pe", lambda E, ct=ct, dt=dt, slot=slot, po=po: E.matmul(
                            out=pb[po][:, :], lhsT=watt_ap[slot][:, dt * 128:(dt + 1) * 128],
                            rhs=srr_v[:, dt * TCH:(dt + 1) * TCH],
                            start=(dt == 0), stop=(dt == CT - 1)),
                            reads=[bwatt[slot], bsrr], writes=[bpb[po]])
                with b.group("dve", s_dve):
                    b.op("dve", lambda E, ct=ct, po=po: E.tensor_tensor(
                        out=xch[:, ct * TCH:(ct + 1) * TCH],
                        in0=xch[:, ct * TCH:(ct + 1) * TCH],
                        in1=pb[po][:, :], op=OP.add),
                        reads=[bpb[po]], writes=[bx])
                if fillers:
                    fillers.pop(0)()
            for f in fillers:
                f()
            dma("act", s_st[q % 2], xbuf[q], xch[:, :], [bx], [bdram_x[q]])

        ssg_b = ssg.ap().bitcast(BF16)

        def gu_load01():
            dma("sync", s_wgu[0], wgu_s[0][:, :], wgu_full[0:128, :],
                [bdram_w[("gu", 0)]], [bwgu[0]])
            dma("sync", s_wgu[1], wgu_s[1][:, :], wgu_full[IT * 128:(IT + 1) * 128, :],
                [bdram_w[("gu", IT)]], [bwgu[1]])

        def ffn_hooks(q):
            if q + 1 >= NCH:
                return None, None, None
            nxt = xch_t[(q + 1) % 2]
            bnxt = bxch[(q + 1) % 2]

            def fills():
                def mk(ct):
                    def f():
                        with b.group("dve", s_dve):
                            b.op("dve", lambda E, ct=ct: E.tensor_scalar(
                                out=xnb[:, ct * TCH:(ct + 1) * TCH],
                                in0=nxt[:, ct * TCH:(ct + 1) * TCH],
                                scalar1=1.0, scalar2=None, op0=OP.mult),
                                reads=[bnxt], writes=[bxnb, bxnb4[ct // 4]])
                        with b.group("pe", s_pe):
                            b.op("pe", lambda E, ct=ct: E.matmul(
                                out=pb[0][0:1, :], lhsT=ones_b[:, :],
                                rhs=xnb[:, ct * TCH:(ct + 1) * TCH],
                                start=(ct == 0), stop=(ct == CT - 1)),
                                reads=[bxnb4[ct // 4], bconsts], writes=[bpb[0]])
                        with b.group("act", s_act):
                            b.op("act", lambda E, ct=ct: E.activation(
                                out=ssg_b[:, 0:TCH],
                                in_=nxt[:, ct * TCH:(ct + 1) * TCH], func=AF.Square),
                                reads=[bnxt], writes=[bssg])
                        with b.group("pe", s_pe):
                            b.op("pe", lambda E, ct=ct: E.matmul(
                                out=pb[1][0:1, :], lhsT=ones_b[:, :],
                                rhs=ssg_b[:, 0:TCH],
                                start=(ct == 0), stop=(ct == CT - 1)),
                                reads=[bssg, bconsts], writes=[bpb[1]])
                    return f
                return [mk(ct) for ct in range(CT)]

            def fin1():
                with b.group("dve", s_dve):
                    b.op("dve", lambda E: E.tensor_scalar(
                        out=m_row, in0=pb[0][0:1, :], scalar1=RECIP_C, scalar2=None,
                        op0=OP.mult), reads=[bpb[0]], writes=[bsrow])
                    b.op("dve", lambda E: E.tensor_scalar(
                        out=rstd_row, in0=pb[1][0:1, :], scalar1=RECIP_C, scalar2=None,
                        op0=OP.mult), reads=[bpb[1]], writes=[bsrow])
                    b.op("dve", lambda E: E.scalar_tensor_tensor(
                        out=mr_row, in0=m_row, scalar=1.0, in1=m_row,
                        op0=OP.mult, op1=OP.mult), reads=[bsrow], writes=[bsrow])
                    b.op("dve", lambda E: E.tensor_tensor(
                        out=rstd_row, in0=rstd_row, in1=mr_row, op=OP.subtract),
                        reads=[bsrow], writes=[bsrow])
                with b.group("act", s_act):
                    b.op("act", lambda E: E.activation(
                        out=rstd_row, in_=rstd_row, func=AF.Ln, bias=bias_eps5[0:1, :]),
                        reads=[bsrow, bconsts], writes=[bsrow])
                    b.op("act", lambda E: E.activation(
                        out=rstd_row, in_=rstd_row, func=AF.Exp, scale=-0.5),
                        reads=[bsrow], writes=[bsrow])
                with b.group("dve", s_dve):
                    b.op("dve", lambda E: E.tensor_tensor(
                        out=mr_row, in0=m_row, in1=rstd_row, op=OP.mult),
                        reads=[bsrow], writes=[bsrow])

            def fin2():
                with b.group("pe", s_pe):
                    b.op("pe", lambda E: E.matmul(
                        out=pb[2][:, :], lhsT=ones_f[:, :], rhs=rstd_row,
                        start=True, stop=True), reads=[bsrow, bconsts], writes=[bpb[2]])
                with b.group("pe", s_pe):
                    b.op("pe", lambda E: E.matmul(
                        out=pb[3][:, :], lhsT=ones_f[:, :], rhs=mr_row,
                        start=True, stop=True), reads=[bsrow, bconsts], writes=[bpb[3]])

            def xn(gi):
                def f():
                    with b.group("dve", s_dve):
                        for k in range(4):
                            ct = gi * 4 + k
                            b.op("dve", lambda E, ct=ct: E.tensor_tensor(
                                out=stats[:, :], in0=nxt[:, ct * TCH:(ct + 1) * TCH],
                                in1=pb[2][:, :], op=OP.mult),
                                reads=[bnxt, bpb[2]], writes=[bstats])
                            b.op("dve", lambda E, ct=ct: E.tensor_tensor(
                                out=xnb[:, ct * TCH:(ct + 1) * TCH], in0=stats[:, :],
                                in1=pb[3][:, :], op=OP.subtract),
                                reads=[bstats, bpb[3]], writes=[bxnb, bxnb4[ct // 4]])
                return f

            def all_fills():
                return fills() + [fin1, fin2] + [xn(gi) for gi in range(4)]

            return all_fills, gu_load01, None

        emit_x_load(0, x_in, bxch[0])
        for q in range(NCH):
            if q + 1 < NCH:
                emit_x_load(q + 1, x_in, bxch[(q + 1) % 2])
            if "f1" in phases:
                hf, ht_, hx = ffn_hooks(q)
                emit_ffn(q, True, xbuf, bdram_x, la_done=(q > 0),
                         hook_fills=hf, hook_tail=ht_, hook_xn=hx)
            else:
                dma("act", s_st[q % 2], xbuf[q], xch_t[q % 2][:, :],
                    [bxch[q % 2]], [bdram_x[q]])
        alias_handoff([bxk, bxv, bxr], [bh])
        alias_handoff([bsrr, bscan, b_ek, b_euk, b_sr, b_ekv, b_euv, b_U, b_D], [bysb])
        for i in range(3):
            alias_handoff([bwatt[i]], [bwgu[i]])
        alias_handoff([bwatt[3], bwatt[4]], [bwdn[0]])
        alias_handoff([bwatt[5]], [bwdn[1]])
        if "at" in phases:
            dma("sync", s_xch[0], xch_t[0][:, :], xbuf[0], [bdram_x[0]], [bxch[0]])
            emit_attn_pre(0)
            for q in range(NCH):
                emit_attn_main(q)
                fill = []
                if q + 1 < NCH:
                    dma("sync", s_xch[(q + 1) % 2], xch_t[(q + 1) % 2][:, :],
                        xbuf[q + 1], [bdram_x[q + 1]], [bxch[(q + 1) % 2]])
                    emit_attn_ln(q + 1)
                    fill = mix_ct_fns(q + 1)
                emit_attn_wo(q, fill)
        alias_handoff([bh], [bxk, bxv, bxr])
        alias_handoff([bysb], [bsrr, bscan, b_ek, b_euk, b_sr, b_ekv, b_euv, b_U, b_D])
        for i in range(3):
            alias_handoff([bwgu[i]], [bwatt[i]])
        alias_handoff([bwdn[0]], [bwatt[3], bwatt[4]])
        alias_handoff([bwdn[1]], [bwatt[5]])
        by_out = [Buf(f"yout{i}") for i in range(NCH)]
        dma("sync", s_xch[0], xch_t[0][:, :], xbuf[0], [bdram_x[0]], [bxch[0]])
        for q in range(NCH):
            if q + 1 < NCH:
                dma("sync", s_xch[(q + 1) % 2], xch_t[(q + 1) % 2][:, :],
                    xbuf[q + 1], [bdram_x[q + 1]], [bxch[(q + 1) % 2]])
            if "f2" in phases:
                hf, ht_, hx = ffn_hooks(q)
                emit_ffn(q, False, y_out, by_out, la_done=(q > 0),
                         hook_fills=hf, hook_tail=ht_, hook_xn=hx)
            else:
                dma("act", s_st[q % 2], y_out[q], xch_t[q % 2][:, :],
                    [bxch[q % 2]], [by_out[q]])

        for s in (s_st[0], s_st[1], s_dqst2[0], s_dqst2[1], s_dqpk[0], s_dqpk[1], s_init, s_cc):
            b.raw_wait("gp", s, s.planned)

        for s in b.sems.values():
            s.h = es.enter_context(nc.semaphore(s.name))
        blk = es.enter_context(nc.Block())
        blk.tensor(lambda E: _run_plan(b, "pe", E))
        blk.vector(lambda E: _run_plan(b, "dve", E))
        blk.scalar(lambda E: _run_plan(b, "act", E))
        blk.sync(lambda E: _run_plan(b, "sync", E))
        blk.gpsimd(lambda E: _run_plan(b, "gp", E))

    return nc



def build_in_maps(inp, TT=T):
    x = np.asarray(inp["L0_attn_x"])
    B = x.shape[0]
    wk_t = _tile_lhsT(np.ascontiguousarray(np.asarray(inp["L0_attn_Wk"]).T[ORDER, :]))
    wv_t = _tile_lhsT(np.ascontiguousarray(np.asarray(inp["L0_attn_Wv"]).T[ORDER, :]))
    wr_t = _tile_lhsT(np.ascontiguousarray(np.asarray(inp["L0_attn_Wr"]).T[ORDER, :]))
    wo_t = _tile_lhsT(np.ascontiguousarray(np.asarray(inp["L0_attn_Wo"]).T[:, ORDER]))
    _gupk = np.asarray(inp["L0_ffn_gate_up_packed"])
    gu_pk = np.ascontiguousarray(
        np.concatenate([_gupk[0:IDIM][ORDER_I], _gupk[IDIM:][ORDER_I]], axis=0))
    dn_pk = np.ascontiguousarray(np.asarray(inp["L0_ffn_down_packed"])[ORDER, :])
    gsc0 = np.asarray(inp["L0_ffn_gate_up_scales"])
    gsc = np.concatenate([gsc0[:, 0:IDIM][:, ORDER_I], gsc0[:, IDIM:][:, ORDER_I]], axis=1)
    gu_sc = np.ascontiguousarray(
        gsc.T.reshape(GU_UNITS, 128, 16).transpose(1, 0, 2).reshape(128, GU_UNITS * 16))
    dsc = np.asarray(inp["L0_ffn_down_scales"])[:, ORDER]
    dn_sc = np.ascontiguousarray(
        dsc.T.reshape(CT, 128, 44).transpose(1, 0, 2).reshape(128, CT * 44))
    vt = {"v_td": _vec_tile(np.asarray(inp["L0_attn_time_decay"])),
          "v_tf": _vec_tile(np.asarray(inp["L0_attn_time_first"])),
          "v_tmk": _vec_tile(np.asarray(inp["L0_attn_time_mix_k"])[ORDER]),
          "v_tmv": _vec_tile(np.asarray(inp["L0_attn_time_mix_v"])[ORDER]),
          "v_tmr": _vec_tile(np.asarray(inp["L0_attn_time_mix_r"])[ORDER])}
    ident = np.eye(128, dtype=ml_dtypes.bfloat16)
    in_maps = []
    for c in range(B):
        im = {"x_in": _chunks_fm(np.ascontiguousarray(x[c][:TT, ORDER])),
              "gu_pk": gu_pk, "gu_sc": gu_sc, "dn_pk": dn_pk, "dn_sc": dn_sc,
              "wk_t": wk_t, "wv_t": wv_t, "wr_t": wr_t, "wo_t": wo_t,
              "ident_in": ident}
        im.update(vt)
        in_maps.append(im)
    return in_maps


def kernel(L0_attn_x, L0_attn_time_decay, L0_attn_time_first, L0_attn_time_mix_k,
           L0_attn_time_mix_v, L0_attn_time_mix_r, L0_attn_Wk, L0_attn_Wv,
           L0_attn_Wr, L0_attn_Wo, L0_ffn_gate_up_packed, L0_ffn_gate_up_scales,
           L0_ffn_down_packed, L0_ffn_down_scales):
    inp = dict(L0_attn_x=L0_attn_x, L0_attn_time_decay=L0_attn_time_decay,
               L0_attn_time_first=L0_attn_time_first, L0_attn_time_mix_k=L0_attn_time_mix_k,
               L0_attn_time_mix_v=L0_attn_time_mix_v, L0_attn_time_mix_r=L0_attn_time_mix_r,
               L0_attn_Wk=L0_attn_Wk, L0_attn_Wv=L0_attn_Wv, L0_attn_Wr=L0_attn_Wr,
               L0_attn_Wo=L0_attn_Wo, L0_ffn_gate_up_packed=L0_ffn_gate_up_packed,
               L0_ffn_gate_up_scales=L0_ffn_gate_up_scales,
               L0_ffn_down_packed=L0_ffn_down_packed,
               L0_ffn_down_scales=L0_ffn_down_scales)
    B = np.asarray(L0_attn_x).shape[0]
    nc = build_nc(T, shard=False)
    in_maps = build_in_maps(inp, T)
    res = run_bass_kernel_spmd(nc, in_maps, core_ids=list(range(B)))
    out = np.empty((B, T, CDIM), dtype=np.float32)
    for c in range(B):
        out[c][:, ORDER] = _unchunk_fm(res.results[c]["y_out"])
    return out


# version 27
# speedup vs baseline: 1.0807x; 1.0071x over previous
import numpy as np
import ml_dtypes
import concourse.bass as bass
import concourse.mybir as mybir
from concourse.bass_utils import run_bass_kernel_spmd

F32 = mybir.dt.float32
F16 = mybir.dt.float16
BF16 = mybir.dt.bfloat16
I32 = mybir.dt.int32
AF = mybir.ActivationFunctionType
OP = mybir.AluOpType
LN2 = float(np.log(2.0))
LNHALF = float(np.log(0.5))

T = 2048
CDIM = 2048
IDIM = 5632
TCH = 512
CT = CDIM // 128
IT = IDIM // 128
GU_UNITS = 2 * IT
RECIP_C = 1.0 / 2048.0

ENGS = ("pe", "dve", "act", "sync", "gp")

_PERM = np.empty(CDIM, np.int64)
for _w in range(256):
    for _j in range(8):
        _PERM[8 * _w + _j] = _j * 256 + _w
ORDER = np.argsort(_PERM)
_PERM_I = np.empty(IDIM, np.int64)
for _w in range(704):
    for _j in range(8):
        _PERM_I[8 * _w + _j] = _j * 704 + _w
ORDER_I = np.argsort(_PERM_I)


class Sem:
    __slots__ = ("name", "planned", "h")

    def __init__(self, name):
        self.name = name
        self.planned = 0
        self.h = None


class Buf:
    __slots__ = ("name", "writer", "readers")

    def __init__(self, name):
        self.name = name
        self.writer = None
        self.readers = []


class PlOp:
    __slots__ = ("fn", "inc")

    def __init__(self, fn):
        self.fn = fn
        self.inc = None


class Builder:
    def __init__(self):
        self.plan = {e: [] for e in ENGS}
        self.sems = {}
        self.hw = {e: {} for e in ENGS}
        self.epoch = {e: 0 for e in ENGS}
        self.drain_at = {e: -1 for e in ENGS}
        self.cur_group = None

    def sem(self, name):
        s = Sem(name)
        self.sems[name] = s
        return s

    def _wait(self, eng, sem, val):
        if val <= 0:
            return
        if val <= self.hw[eng].get(sem.name, 0):
            return
        assert val <= sem.planned, (
            f"wait on {sem.name}={val} but only {sem.planned} planned (eng {eng})")
        self.hw[eng][sem.name] = val
        self.plan[eng].append(PlOp(lambda E, s=sem, v=val: E.wait_ge(s.h, v)))

    def _drain(self, eng):
        self.plan[eng].append(PlOp(lambda E: E.drain()))
        self.drain_at[eng] = self.epoch[eng]

    def group(self, eng, sem, n=1):
        assert self.cur_group is None
        self.cur_group = [eng, sem, n, None]
        return _GroupCtx(self)

    def _edge(self, eng, dep, this_is_dma):
        deng, dsem, dval, dep_, disdma = dep
        if deng == eng and not disdma and not this_is_dma:
            if dep_ > self.drain_at[eng] and eng in ("dve", "act", "gp"):
                self._drain(eng)
        else:
            assert dsem is not None, f"edge from {deng} has no milestone"
            self._wait(eng, dsem, dval)

    def op(self, eng, fn, reads=(), writes=()):
        g = self.cur_group
        assert g is not None and g[0] == eng, f"op on {eng} outside its group"
        sem, n = g[1], g[2]
        is_dma = (n == 16)
        for b in reads:
            if b.writer is not None:
                self._edge(eng, b.writer, is_dma)
        for b in writes:
            for r in b.readers:
                self._edge(eng, r, is_dma)
            if b.writer is not None:
                w = b.writer
                if w[0] == eng and not w[4] and not is_dma:
                    pass
                else:
                    self._edge(eng, w, is_dma)
        self.epoch[eng] += 1
        ep = self.epoch[eng]
        o = PlOp(fn)
        self.plan[eng].append(o)
        g[3] = o
        val_after = sem.planned + n
        for b in writes:
            b.writer = (eng, sem, val_after, ep, is_dma)
            b.readers = []
        for b in reads:
            b.readers.append((eng, sem, val_after, ep, is_dma))

    def close_group(self):
        eng, sem, n, last = self.cur_group
        assert last is not None, "empty group"
        sem.planned += n
        last.inc = (sem, n)
        self.cur_group = None

    def raw_wait(self, eng, sem, val):
        assert self.cur_group is None
        self._wait(eng, sem, val)

    def bare(self, eng, fn):
        assert self.cur_group is None
        self.plan[eng].append(PlOp(fn))


class _GroupCtx:
    def __init__(self, b):
        self.b = b

    def __enter__(self):
        return self.b

    def __exit__(self, *a):
        if a[0] is None:
            self.b.close_group()
        return False


def _run_plan(b, eng_name, E):
    for o in b.plan[eng_name]:
        inst = o.fn(E)
        if o.inc is not None:
            sem, n = o.inc
            inst.then_inc(sem.h, n)



def _chunks_fm(x2d):
    nq = x2d.shape[0] // TCH
    out = np.empty((nq, 128, CT * TCH), dtype=x2d.dtype)
    for q in range(nq):
        blk = x2d[q * TCH:(q + 1) * TCH, :]
        bt = np.ascontiguousarray(blk.T)
        out[q] = bt.reshape(CT, 128, TCH).transpose(1, 0, 2).reshape(128, CT * TCH)
    return out


def _unchunk_fm(ych):
    nq = ych.shape[0]
    out = np.empty((nq * TCH, CDIM), dtype=ych.dtype)
    for q in range(nq):
        blk = ych[q].reshape(128, CT, TCH).transpose(1, 0, 2).reshape(CDIM, TCH)
        out[q * TCH:(q + 1) * TCH, :] = blk.T
    return out


def _tile_lhsT(arr):
    K, M = arr.shape
    kt, mt = K // 128, M // 128
    return np.ascontiguousarray(
        arr.reshape(kt, 128, mt, 128).transpose(2, 1, 0, 3).reshape(mt, 128, kt * 128))


def _vec_tile(v):
    return np.ascontiguousarray(v.reshape(CT, 128).T)



def build_nc(T_=T, phases=("f1", "at", "f2"), dbg=None, shard=True):
    NCH = T_ // TCH
    GU_SH = GU_UNITS // 8 if shard else GU_UNITS
    DN_SH = CT // 8 if shard else CT
    nc = bass.Bass("TRN2", target_bir_lowering=False, debug=False)

    x_in = nc.dram_tensor("x_in", [NCH, 128, CT * TCH], F32, kind="ExternalInput").ap()
    gu_pk = nc.dram_tensor("gu_pk", [GU_SH * 128, 256], I32, kind="ExternalInput").ap()
    gu_sc = nc.dram_tensor("gu_sc", [128, GU_SH * 16], F16, kind="ExternalInput").ap()
    dn_pk = nc.dram_tensor("dn_pk", [DN_SH * 128, 704], I32, kind="ExternalInput").ap()
    dn_sc = nc.dram_tensor("dn_sc", [128, DN_SH * 44], F16, kind="ExternalInput").ap()
    wk_t = nc.dram_tensor("wk_t", [CT, 128, CDIM], F32, kind="ExternalInput").ap()
    wv_t = nc.dram_tensor("wv_t", [CT, 128, CDIM], F32, kind="ExternalInput").ap()
    wr_t = nc.dram_tensor("wr_t", [CT, 128, CDIM], F32, kind="ExternalInput").ap()
    wo_t = nc.dram_tensor("wo_t", [CT, 128, CDIM], F32, kind="ExternalInput").ap()
    v_td = nc.dram_tensor("v_td", [128, CT], F32, kind="ExternalInput").ap()
    v_tf = nc.dram_tensor("v_tf", [128, CT], F32, kind="ExternalInput").ap()
    v_tmk = nc.dram_tensor("v_tmk", [128, CT], F32, kind="ExternalInput").ap()
    v_tmv = nc.dram_tensor("v_tmv", [128, CT], F32, kind="ExternalInput").ap()
    v_tmr = nc.dram_tensor("v_tmr", [128, CT], F32, kind="ExternalInput").ap()
    ident_in = nc.dram_tensor("ident_in", [128, 128], BF16, kind="ExternalInput").ap()
    y_out = nc.dram_tensor("y_out", [NCH, 128, CT * TCH], F32, kind="ExternalOutput").ap()
    dbg_out = nc.dram_tensor("dbg_out", [128, CT * TCH], F32, kind="ExternalOutput").ap()
    xbuf = nc.dram_tensor("xbuf", [NCH, 128, CT * TCH], F32).ap()
    wgu_full_t = nc.dram_tensor("wgu_full", [GU_UNITS * 128, CT * 128], BF16)
    wdn_full_t = nc.dram_tensor("wdn_full", [CT * 128, IT * 128], BF16)
    wgu_full = wgu_full_t.ap()
    wdn_full = wdn_full_t.ap()
    if shard:
        cgu_t = nc.dram_tensor("cgu", [GU_SH * 128, CT * 128], BF16)
        cdn_t = nc.dram_tensor("cdn", [DN_SH * 128, IT * 128], BF16)
        cgu, cdn = cgu_t.ap(), cdn_t.ap()
    else:
        cgu, cdn = wgu_full, wdn_full

    from contextlib import ExitStack
    es = ExitStack()
    b = Builder()

    def sb(name, shape, dt):
        return es.enter_context(nc.sbuf_tensor(name, shape, dt))

    with es:
        xchA = sb("xchA", [128, CT * TCH], F32)
        xchB = sb("xchB", [128, CT * TCH], F32)
        ar2 = sb("ar2", [128, 3 * CT * TCH], BF16)
        ar3 = sb("ar3", [128, CT * TCH], F32)
        xnb = sb("xnb", [128, CT * TCH], BF16)
        wgu_s = [sb(f"wgu{i}", [128, CT * 128], BF16) for i in range(3)]
        wdn_s = [sb(f"wdn{i}", [128, IT * 128], BF16) for i in range(2)]
        stats = sb("stats", [128, 512], F32)
        srow = sb("srow", [128, 1536], F32)
        ssg = sb("ssg", [128, 512], F32)
        y2b = sb("y2b", [128, 1024], BF16)
        scf_gu = None
        scf_dn = None
        a_sb = sb("a_sb", [128, CT], F32)
        u_sb = sb("u_sb", [128, CT], F32)
        tmk_sb = sb("tmk_sb", [128, CT], F32)
        tmv_sb = sb("tmv_sb", [128, CT], F32)
        tmr_sb = sb("tmr_sb", [128, CT], F32)
        td_sb = sb("td_sb", [128, CT], F32)
        cu_sb = sb("cu_sb", [128, CT], F32)
        cd_sb = sb("cd_sb", [128, CT], F32)
        bnd_sb = sb("bnd_sb", [128, CT], BF16)
        ident = sb("ident", [128, 128], BF16)
        ones_b = sb("ones_b", [128, 1], BF16)
        ones_f = sb("ones_f", [1, 128], F32)
        bias_ln2n = sb("bias_ln2n", [128, 1], F32)
        bias_one = sb("bias_one", [128, 1], F32)
        bias_eps5 = sb("bias_eps5", [128, 1], F32)
        bias_eps6 = sb("bias_eps6", [128, 1], F32)
        bias_lnh = sb("bias_lnh", [128, 1], F32)
        bias_n6 = sb("bias_n6", [128, 1], F32)
        bias_n2 = sb("bias_n2", [128, 1], F32)
        WB = 768
        dq_pk = [xchA[:, 0:WB].bitcast(I32)[:, 0:WB],
                 xchA[:, WB:2 * WB].bitcast(I32)[:, 0:WB]]
        dq_m3 = [xchA[:, (2 + p) * WB:(3 + p) * WB].bitcast(I32)[:, 0:WB]
                 for p in range(4)]
        dq_si = [xchA[:, (6 + p) * WB:(7 + p) * WB].bitcast(I32)[:, 0:WB]
                 for p in range(4)]
        scf_gu = xchB[:, 0:GU_SH * 16]
        scf_dn = xnb[:, 2112:2112 + 2 * DN_SH * 44].bitcast(F32)[:, 0:DN_SH * 44]
        _o = GU_SH * 16
        def _bcarve(off_f32, n_bf16):
            return xchB[:, off_f32:off_f32 + (n_bf16 + 1) // 2].bitcast(BF16)[:, 0:n_bf16]
        def _bcarve3(off_f32, n_bf16):
            return ar3[:, off_f32:off_f32 + (n_bf16 + 1) // 2].bitcast(BF16)[:, 0:n_bf16]
        dq_h0, dq_r2 = [], []
        for _s in range(8):
            dq_h0.append(_bcarve(_o, WB)); _o += WB // 2
            dq_r2.append(_bcarve(_o, WB)); _o += WB // 2
        assert _o <= 8192, _o
        dq_r1 = [_bcarve3(_s * WB // 2, WB) for _s in range(8)]
        dq_s2m = [_bcarve3(8 * WB // 2, WB), _bcarve3(9 * WB // 2, WB),
                  _bcarve3(7682, WB),
                  xchA[:, 7680:7680 + WB // 2].bitcast(BF16)[:, 0:WB]] + \
                 [xchA[:, (2 + _p) * WB:(2 + _p) * WB + WB // 2]
                  .bitcast(BF16)[:, 0:WB] for _p in range(4)]
        dq_ub = [ar2[:, 0:IT * 128], ar2[:, IT * 128:2 * IT * 128]]
        dq_vb = [ar2[:, 2 * IT * 128:2 * IT * 128 + 6144],
                 ar2[:, 2 * IT * 128 + 6144:2 * IT * 128 + 12288]]

        pb = [es.enter_context(nc.psum_tensor(f"pb{i}", [128, 512], F32)) for i in range(7)]
        ptb = es.enter_context(nc.psum_tensor("ptb", [128, 1024], BF16))

        s_pe = b.sem("s_pe")
        s_dve = b.sem("s_dve")
        s_act = b.sem("s_act")
        s_init = b.sem("s_init")
        s_xch = [b.sem("s_xchA"), b.sem("s_xchB")]
        s_wgu = [b.sem(f"s_wgu{i}") for i in range(3)]
        s_wdn = [b.sem(f"s_wdn{i}") for i in range(2)]
        s_watt = [b.sem(f"s_watt{i}") for i in range(6)]
        s_dqpk = [b.sem("s_dqpkA"), b.sem("s_dqpkB")]
        s_dqst2 = [b.sem("s_dqstA"), b.sem("s_dqstB")]
        s_ptst = [b.sem("s_ptstA"), b.sem("s_ptstB")]
        s_st = [b.sem("s_stA"), b.sem("s_stB")]
        s_cc = b.sem("s_cc")

        B_ = Buf
        bxch = [B_("xchA"), B_("xchB")]
        bxnb = B_("xnb")
        bxnb4 = [B_(f"xnb{i}") for i in range(4)]
        bh = B_("h")
        bxk, bxv, bxr = B_("xk"), B_("xv"), B_("xr")
        bysb = B_("ysb")
        bsrr = B_("srr")
        bscan = B_("scan")
        b_ek, b_euk, b_sr = B_("ek"), B_("euk"), B_("sr")
        b_ekv, b_euv = B_("ekv"), B_("euv")
        b_U, b_D = B_("scU"), B_("scD")
        bstats = B_("stats")
        bsrow = B_("srow")
        bssg = B_("ssg")
        by2b = [B_("y2b0"), B_("y2b1")]
        bwgu = [B_(f"wgu{i}") for i in range(3)]
        bwdn = [B_(f"wdn{i}") for i in range(2)]
        bvecs = B_("vecs")
        bconsts = B_("consts")
        bident = B_("ident")
        bcarry = B_("carry")
        bbnd = B_("bnd")
        bpb = [B_(f"pb{i}") for i in range(7)]
        bptb = B_("ptb")
        bdq = B_("dq")
        bscf = B_("scf")
        bdram_w = {("gu", j): B_(f"dwgu{j}") for j in range(GU_UNITS)}
        bdram_w.update({("dn", j): B_(f"dwdn{j}") for j in range(CT)})
        bdram_x = [B_(f"dxbuf{q}") for q in range(NCH)]

        watt_ap = [wgu_s[0][:, :], wgu_s[1][:, :], wgu_s[2][:, :],
                   wdn_s[0][:, 0:CT * 128], wdn_s[0][:, CT * 128:2 * CT * 128],
                   wdn_s[1][:, 0:CT * 128]]
        bwatt = [B_(f"watt{i}") for i in range(6)]

        srr_v = ar3[:, 0:CT * TCH // 2].bitcast(BF16)[:, 0:CT * TCH]
        scf32 = ar3
        sc_ek = scf32[:, 4096:4608]
        sc_euk = scf32[:, 4608:5120]
        sc_sr = scf32[:, 5120:5632]
        sc_ekv = scf32[:, 5632:6144]
        sc_euv = scf32[:, 6144:6656]
        sc_U = scf32[:, 6656:7169]
        sc_D = scf32[:, 7169:7682]

        h_v = ar2[:, 0:IT * TCH]
        xk_v = ar2[:, 0:CT * TCH]
        xv_v = ar2[:, CT * TCH:2 * CT * TCH]
        xr_v = ar2[:, 2 * CT * TCH:3 * CT * TCH]

        xch_t = [xchA, xchB]

        with b.group("dve", s_dve):
            b.op("dve", lambda E: E.memset(ones_b[:], 1.0), writes=[bconsts])
            b.op("dve", lambda E: E.memset(ones_f[:], 1.0), writes=[bconsts])
            b.op("dve", lambda E: E.memset(bias_ln2n[:], -LN2), writes=[bconsts])
            b.op("dve", lambda E: E.memset(bias_one[:], 1.0), writes=[bconsts])
            b.op("dve", lambda E: E.memset(bias_eps5[:], 1e-5), writes=[bconsts])
            b.op("dve", lambda E: E.memset(bias_eps6[:], 1e-6), writes=[bconsts])
            b.op("dve", lambda E: E.memset(bias_lnh[:], LNHALF), writes=[bconsts])
            b.op("dve", lambda E: E.memset(bias_n6[:], -6.0), writes=[bconsts])
            b.op("dve", lambda E: E.memset(bias_n2[:], -2.0), writes=[bconsts])
            b.op("dve", lambda E: E.memset(cu_sb[:], 0.0), writes=[bcarry])
            b.op("dve", lambda E: E.memset(cd_sb[:], 0.0), writes=[bcarry])
            b.op("dve", lambda E: E.memset(bnd_sb[:], 0.0), writes=[bbnd])

        def dma(eng, sem, out_ap, in_ap, reads, writes):
            with b.group(eng, sem, 16):
                b.op(eng, lambda E, o=out_ap, i=in_ap: E.dma_start(out=o, in_=i),
                     reads=reads, writes=writes)

        dma("sync", s_init, ident[:], ident_in, [], [bident])
        for v_ap, tile in ((v_td, td_sb), (v_tf, u_sb), (v_tmk, tmk_sb),
                           (v_tmv, tmv_sb), (v_tmr, tmr_sb)):
            dma("sync", s_init, tile[:], v_ap, [], [bvecs])
        sc_gu16 = xnb[:, 0:GU_SH * 16].bitcast(F16)[:, 0:GU_SH * 16]
        sc_dn16 = xnb[:, GU_SH * 16:GU_SH * 16 + DN_SH * 44].bitcast(F16)[:, 0:DN_SH * 44]
        dma("sync", s_init, sc_gu16, gu_sc, [], [bxnb, bxnb4[0]])
        dma("sync", s_init, sc_dn16, dn_sc, [], [bxnb, bxnb4[0]])
        def bump_to_total(bufs, eng, sem):
            for bb_ in bufs:
                if bb_.writer is not None:
                    bb_.writer = (eng, sem, sem.planned, bb_.writer[3], True)
        bump_to_total([bident, bvecs, bxnb, bxnb4[0]], "sync", s_init)
        with b.group("act", s_act):
            b.op("act", lambda E: E.activation(out=scf_gu[:], in_=sc_gu16, func=AF.Copy),
                 reads=[bxnb], writes=[bscf])
            b.op("act", lambda E: E.activation(out=scf_dn[:], in_=sc_dn16, func=AF.Copy),
                 reads=[bxnb], writes=[bscf])
            b.op("act", lambda E: E.activation(out=a_sb[:], in_=td_sb[:], func=AF.Exp),
                 reads=[bvecs], writes=[bvecs])
            b.op("act", lambda E: E.activation(out=a_sb[:], in_=a_sb[:], func=AF.Exp,
                                               scale=-1.0),
                 reads=[bvecs], writes=[bvecs])

        def alias_handoff(dsts, srcs):
            rd, wr = [], None
            for s in srcs:
                rd.extend(s.readers)
                if s.writer is not None:
                    rd.append(s.writer)
                    wr = s.writer
            for d in dsts:
                d.writer = wr
                d.readers = list(rd)

        def emit_pk(kind, u0, nb, slot):
            W = 256 if kind == "gu" else 704
            pk_full = gu_pk if kind == "gu" else dn_pk
            pkt = dq_pk[slot]
            for k in range(nb):
                dma("sync", s_dqpk[slot], pkt[:, k * W:(k + 1) * W],
                    pk_full[(u0 + k) * 128:(u0 + k + 1) * 128, :], [], [bdq_pk[slot]])

        def emit_dq_planes(kind, u0, nb, slot, fill, barrier):
            W = 256 if kind == "gu" else 704
            NW = nb * W
            pkt = dq_pk[slot]
            vb = dq_vb[slot]
            def g1(jq):
                with b.group("dve", s_dve):
                    for p in range(4):
                        j = 4 * jq + p
                        b.op("dve", lambda E, j=j, p=p: E.tensor_scalar(
                            out=dq_m3[p][:, 0:NW], in0=pkt[:, 0:NW], scalar1=4 * j,
                            scalar2=7, op0=OP.logical_shift_right, op1=OP.bitwise_and),
                            reads=[bdq_pk[slot]], writes=[bdq_i[p]])
                        b.op("dve", lambda E, j=j, p=p: E.tensor_scalar(
                            out=dq_si[p][:, 0:NW], in0=pkt[:, 0:NW], scalar1=4 * j + 3,
                            scalar2=1, op0=OP.logical_shift_right, op1=OP.bitwise_and),
                            reads=[bdq_pk[slot]], writes=[bdq_i[p]])
                    for p in range(4):
                        s8 = 4 * jq + p
                        b.op("dve", lambda E, p=p, s8=s8: E.tensor_scalar(
                            out=dq_h0[s8][:, 0:NW], in0=dq_m3[p][:, 0:NW], scalar1=0.5,
                            scalar2=None, op0=OP.mult),
                            reads=[bdq_i[p]], writes=[b_dqh[s8]])
                    for p in range(4):
                        s8 = 4 * jq + p
                        ws = [b_dqs[s8]] + ([bdq_i[p]] if s8 >= 4 else [])
                        b.op("dve", lambda E, p=p, s8=s8: E.tensor_scalar(
                            out=dq_s2m[s8][:, 0:NW], in0=dq_si[p][:, 0:NW], scalar1=-2.0,
                            scalar2=1.0, op0=OP.mult, op1=OP.add),
                            reads=[bdq_i[p]], writes=ws)

            def gact(jq):
                with b.group("act", s_act):
                    for p in range(4):
                        s8 = 4 * jq + p
                        b.op("act", lambda E, s8=s8: E.activation(
                            out=dq_r2[s8][:, 0:NW], in_=dq_h0[s8][:, 0:NW],
                            func=AF.Relu, scale=2.0, bias=bias_n6[:]),
                            reads=[b_dqh[s8], bconsts], writes=[b_dqB[s8]])

            def g2(jq):
                with b.group("dve", s_dve):
                    for p in range(4):
                        s8 = 4 * jq + p
                        b.op("dve", lambda E, s8=s8: E.tensor_scalar(
                            out=dq_r1[s8][:, 0:NW], in0=dq_h0[s8][:, 0:NW],
                            scalar1=-2.0, scalar2=0.0, op0=OP.add, op1=OP.max),
                            reads=[b_dqh[s8]], writes=[b_dqr[s8]])
                    for p in range(4):
                        s8 = 4 * jq + p
                        b.op("dve", lambda E, s8=s8: E.tensor_tensor(
                            out=dq_r2[s8][:, 0:NW], in0=dq_h0[s8][:, 0:NW],
                            in1=dq_r2[s8][:, 0:NW], op=OP.add),
                            reads=[b_dqh[s8], b_dqB[s8]], writes=[b_dqB[s8]])
                    for p in range(4):
                        s8 = 4 * jq + p
                        b.op("dve", lambda E, s8=s8: E.tensor_tensor(
                            out=dq_r2[s8][:, 0:NW], in0=dq_r2[s8][:, 0:NW],
                            in1=dq_r1[s8][:, 0:NW], op=OP.add),
                            reads=[b_dqB[s8], b_dqr[s8]], writes=[b_dqB[s8]])
                    for p in range(4):
                        s8 = 4 * jq + p
                        j = 4 * jq + p
                        CW = W * 8
                        out_ap = vb[:, 0:nb * CW] \
                            .rearrange("p (k r) -> p k r", r=CW)[:, :, j * W:j * W + W] \
                            if nb > 1 else vb[:, j * W:j * W + W]
                        rd = [b_dqs[s8], b_dqB[s8]] + ([bdq_i[p]] if s8 >= 4 else [])
                        b.op("dve", lambda E, s8=s8, out_ap=out_ap: E.tensor_tensor(
                            out=out_ap, in0=dq_s2m[s8][:, 0:NW], in1=dq_r2[s8][:, 0:NW],
                            op=OP.mult),
                            reads=rd,
                            writes=[bdq_vbB[slot], bdq_vbS[slot]])

            g1(0)
            gact(0)
            g1(1)
            gact(1)
            fill()
            fill()
            barrier()
            g2(0)
            fill()
            g2(1)
            fill()

        def emit_dq_tail(kind, u0, nb, slot):
            W = 256 if kind == "gu" else 704
            nct = CT if kind == "gu" else IT
            scf = scf_gu if kind == "gu" else scf_dn
            ngrp = W * 8 // 128
            vb = dq_vb[slot]
            ub = dq_ub[slot]
            for k in range(nb):
                u = u0 + k
                scoff = u * (16 if kind == "gu" else 44)
                CW = W * 8
                with b.group("act", s_act):
                    for g in range(ngrp):
                        ap_g = vb[:, k * CW:(k + 1) * CW] \
                            .rearrange("p (j r) -> p j r", r=W)[:, :, 16 * g:16 * g + 16]
                        b.op("act", lambda E, ap_g=ap_g, g=g, scoff=scoff, scf=scf: E.activation(
                            out=ap_g, in_=ap_g,
                            func=AF.Copy, scale=scf[:, scoff + g:scoff + g + 1]),
                            reads=[bdq_vbB[slot], bscf], writes=[bdq_vbS[slot]])
                yield
                for t8 in range((ngrp + 7) // 8):
                    n_in = min(8, ngrp - t8 * 8)
                    with b.group("pe", s_pe):
                        for kk in range(n_in):
                            g = t8 * 8 + kk
                            c0 = k * ngrp * 128 + g * 128
                            b.op("pe", lambda E, c0=c0, kk=kk, vb=vb: E.transpose(
                                out=ptb[:, kk * 128:(kk + 1) * 128],
                                in_=vb[:, c0:c0 + 128], identity=ident[:]),
                                reads=[bdq_vbS[slot], bident], writes=[bptb])
                    with b.group("act", s_act):
                        b.op("act", lambda E, t8=t8, n_in=n_in, ub=ub: E.activation(
                            out=ub[:, t8 * 1024:t8 * 1024 + n_in * 128],
                            in_=ptb[:, 0:n_in * 128], func=AF.Copy),
                            reads=[bptb], writes=[bdq_ubB[slot]])
                    yield
                dst = (cgu if kind == "gu" else cdn)[u * 128:(u + 1) * 128, :]
                dma("gp", s_dqst2[slot], dst, ub[:, 0:nct * 128], [bdq_ubB[slot]],
                    [(b_cgu if kind == "gu" else b_cdn) if shard else bdram_w[(kind, u)]])

        b_cgu, b_cdn = Buf("cgu"), Buf("cdn")
        bdq_pk = [Buf("dqpk0"), Buf("dqpk1")]
        bdq_i = [Buf(f"dqi{p}") for p in range(4)]
        bdq_a = [Buf(f"dqa{p}") for p in range(4)]
        b_dqh = [Buf(f"dqh{p}") for p in range(8)]
        b_dqs = [Buf(f"dqs{p}") for p in range(8)]
        b_dqB = [Buf(f"dqB{p}") for p in range(8)]
        b_dqr = [Buf(f"dqr{p}") for p in range(8)]
        bdq_v = [Buf("dqv0"), Buf("dqv1")]
        bdq_vbB = [Buf("dqvb0"), Buf("dqvb1")]
        bdq_vbS = [Buf("dqvbs0"), Buf("dqvbs1")]
        bdq_ubB = [Buf("dqub0"), Buf("dqub1")]
        batches = []
        u = 0
        while u < GU_SH:
            nb = min(3, GU_SH - u)
            batches.append(("gu", u, nb))
            u += nb
        u = 0
        while u < DN_SH:
            batches.append(("dn", u, 1))
            u += 1
        pending = []
        _S = object()

        def fill():
            while pending:
                if next(pending[0], _S) is _S:
                    pending.pop(0)
                    continue
                return

        def barrier():
            while len(pending) > 1:
                for _ in pending.pop(0):
                    pass

        emit_pk(*batches[0], 0)
        for bi, (kind_, u0_, nb_) in enumerate(batches):
            if bi + 1 < len(batches):
                emit_pk(*batches[bi + 1], (bi + 1) % 2)
            if bi > 0:
                k2, u2, n2 = batches[bi - 1]
                pending.append(emit_dq_tail(k2, u2, n2, (bi - 1) % 2))
            emit_dq_planes(kind_, u0_, nb_, bi % 2, fill, barrier)
        for g in pending:
            for _ in g:
                pass
        k2, u2, n2 = batches[-1]
        for _ in emit_dq_tail(k2, u2, n2, (len(batches) - 1) % 2):
            pass
        alias_handoff([bxch[0], bxch[1], bh, bxk, bxv, bxr, bysb, bsrr, bscan,
                       b_ek, b_euk, b_sr, b_ekv, b_euv, b_U, b_D, bpb[6], bxnb,
                       bxnb4[0], bxnb4[1], bxnb4[2], bxnb4[3]],
                      [bdq, bscf, bdq_v[0], bdq_v[1], bdq_i[0], bdq_i[1],
                       bdq_i[2], bdq_i[3], bdq_a[0], bdq_a[1],
                       bdq_a[2], bdq_a[3], bdq_pk[0], bdq_pk[1],
                       b_dqh[0], b_dqh[1], b_dqh[2], b_dqh[3],
                       b_dqh[4], b_dqh[5], b_dqh[6], b_dqh[7],
                       b_dqs[0], b_dqs[1], b_dqs[2], b_dqs[3],
                       b_dqs[4], b_dqs[5], b_dqs[6], b_dqs[7],
                       b_dqB[0], b_dqB[1], b_dqB[2], b_dqB[3],
                       b_dqB[4], b_dqB[5], b_dqB[6], b_dqB[7],
                       b_dqr[0], b_dqr[1], b_dqr[2], b_dqr[3],
                       b_dqr[4], b_dqr[5], b_dqr[6], b_dqr[7],
                       bdq_vbB[0], bdq_vbB[1], bdq_vbS[0], bdq_vbS[1],
                       bdq_ubB[0], bdq_ubB[1], bptb])
        if shard:
            bump_to_total([b_cgu], "gp", s_dqst2[0])
            bump_to_total([b_cdn], "gp", s_dqst2[1])
            with b.group("gp", s_cc, 16):
                b.op("gp", lambda E: E.collective_compute(
                    "AllGather", OP.bypass, replica_groups=[list(range(8))],
                    ins=[cgu_t.ap().opt()], outs=[wgu_full_t.ap().opt()]),
                    reads=[b_cgu], writes=[bdram_w[("gu", 0)]])
            with b.group("gp", s_cc, 16):
                b.op("gp", lambda E: E.collective_compute(
                    "AllGather", OP.bypass, replica_groups=[list(range(8))],
                    ins=[cdn_t.ap().opt()], outs=[wdn_full_t.ap().opt()]),
                    reads=[b_cdn], writes=[bdram_w[("dn", 0)]])
            for (kk, uu), bb_ in bdram_w.items():
                bb_.writer = bdram_w[(kk, 0)].writer
        else:
            for (kk_, uu_), bb_ in bdram_w.items():
                pass
            done0 = ("gp", s_dqst2[0], s_dqst2[0].planned, 0, True)
            done1 = ("gp", s_dqst2[1], s_dqst2[1].planned, 0, True)
            for bb_ in bdram_w.values():
                bb_.writer = done0
                bb_.readers = [done1]

        m_row = srow[0:1, 0:512]
        rstd_row = srow[0:1, 512:1024]
        mr_row = srow[0:1, 1024:1536]

        def emit_ln_a1(xch, bx, xb16, bxb16, subbufs=None):
            with b.group("dve", s_dve):
                for ct in range(CT):
                    ws = [bxb16] + ([subbufs[ct // 4]] if subbufs else [])
                    b.op("dve", lambda E, ct=ct, xch=xch, xb16=xb16: E.tensor_scalar(
                        out=xb16[:, ct * TCH:(ct + 1) * TCH],
                        in0=xch[:, ct * TCH:(ct + 1) * TCH],
                        scalar1=1.0, scalar2=None, op0=OP.mult),
                        reads=[bx], writes=ws)

        def emit_ln_a2(xch, bx, xsq, bxsq):
            for gi in range(4):
                with b.group("act", s_act):
                    for k in range(4):
                        ct = gi * 4 + k
                        b.op("act", lambda E, ct=ct, xch=xch, xsq=xsq: E.activation(
                            out=xsq[:, ct * TCH:(ct + 1) * TCH],
                            in_=xch[:, ct * TCH:(ct + 1) * TCH], func=AF.Square),
                            reads=[bx], writes=[bxsq])

        def emit_ln(xch, bx, xb16, bxb16, xsq, bxsq, st0, st1, bc0, bc1,
                    prepped=False, subbufs=None):
            if not prepped:
                emit_ln_a1(xch, bx, xb16, bxb16, subbufs=subbufs)
                emit_ln_a2(xch, bx, xsq, bxsq)
            with b.group("pe", s_pe):
                for ct in range(CT):
                    b.op("pe", lambda E, ct=ct, xb16=xb16: E.matmul(
                        out=pb[st0][0:1, :], lhsT=ones_b[:, :],
                        rhs=xb16[:, ct * TCH:(ct + 1) * TCH],
                        start=(ct == 0), stop=(ct == CT - 1)),
                        reads=[bxb16, bconsts], writes=[bpb[st0]])
            with b.group("pe", s_pe):
                for ct in range(CT):
                    b.op("pe", lambda E, ct=ct, xsq=xsq: E.matmul(
                        out=pb[st1][0:1, :], lhsT=ones_b[:, :],
                        rhs=xsq[:, ct * TCH:(ct + 1) * TCH],
                        start=(ct == 0), stop=(ct == CT - 1)),
                        reads=[bxsq, bconsts], writes=[bpb[st1]])
            with b.group("dve", s_dve):
                b.op("dve", lambda E: E.tensor_scalar(
                    out=m_row, in0=pb[st0][0:1, :], scalar1=RECIP_C, scalar2=None,
                    op0=OP.mult), reads=[bpb[st0]], writes=[bsrow])
                b.op("dve", lambda E: E.tensor_scalar(
                    out=rstd_row, in0=pb[st1][0:1, :], scalar1=RECIP_C, scalar2=None,
                    op0=OP.mult), reads=[bpb[st1]], writes=[bsrow])
                b.op("dve", lambda E: E.scalar_tensor_tensor(
                    out=mr_row, in0=m_row, scalar=1.0, in1=m_row,
                    op0=OP.mult, op1=OP.mult), reads=[bsrow], writes=[bsrow])
                b.op("dve", lambda E: E.tensor_tensor(
                    out=rstd_row, in0=rstd_row, in1=mr_row, op=OP.subtract),
                    reads=[bsrow], writes=[bsrow])
            with b.group("act", s_act):
                b.op("act", lambda E: E.activation(
                    out=rstd_row, in_=rstd_row, func=AF.Ln, bias=bias_eps5[0:1, :]),
                    reads=[bsrow, bconsts], writes=[bsrow])
                b.op("act", lambda E: E.activation(
                    out=rstd_row, in_=rstd_row, func=AF.Exp, scale=-0.5),
                    reads=[bsrow], writes=[bsrow])
            with b.group("dve", s_dve):
                b.op("dve", lambda E: E.tensor_tensor(
                    out=mr_row, in0=m_row, in1=rstd_row, op=OP.mult),
                    reads=[bsrow], writes=[bsrow])
            with b.group("pe", s_pe):
                b.op("pe", lambda E: E.matmul(
                    out=pb[bc0][:, :], lhsT=ones_f[:, :], rhs=rstd_row,
                    start=True, stop=True), reads=[bsrow, bconsts], writes=[bpb[bc0]])
            with b.group("pe", s_pe):
                b.op("pe", lambda E: E.matmul(
                    out=pb[bc1][:, :], lhsT=ones_f[:, :], rhs=mr_row,
                    start=True, stop=True), reads=[bsrow, bconsts], writes=[bpb[bc1]])

        def emit_x_load(q, src_ap, bx):
            dma("sync", s_xch[q % 2], xch_t[q % 2][:, :], src_ap[q], [], [bx])

        def emit_ffn(q, src_is_input, dst_ap, bdst, la_done=False,
                     hook_fills=None, hook_tail=None, hook_xn=None,
                     hook_xload=None):
            xch = xch_t[q % 2]
            bx = bxch[q % 2]
            if not la_done:
                emit_ln(xch, bx, xnb, bxnb, h_v, bh, 4, 5, 4, 5,
                        prepped=False, subbufs=bxnb4)
                for gi in range(4):
                    with b.group("dve", s_dve):
                        for k in range(4):
                            ct = gi * 4 + k
                            b.op("dve", lambda E, ct=ct: E.tensor_tensor(
                                out=stats[:, :], in0=xch[:, ct * TCH:(ct + 1) * TCH],
                                in1=pb[4][:, :], op=OP.mult),
                                reads=[bx, bpb[4]], writes=[bstats])
                            b.op("dve", lambda E, ct=ct: E.tensor_tensor(
                                out=xnb[:, ct * TCH:(ct + 1) * TCH], in0=stats[:, :],
                                in1=pb[5][:, :], op=OP.subtract),
                                reads=[bstats, bpb[5]], writes=[bxnb, bxnb4[ct // 4]])
            if dbg == "lnonly":
                dma("act", s_st[q % 2], dst_ap[q], xch[:, :], [bx], [bdst[q]])
                return
            def gu_load(j, which, slot):
                u = j if which == 0 else j + IT
                dma("sync", s_wgu[slot], wgu_s[slot][:, :],
                    wgu_full[u * 128:(u + 1) * 128, :],
                    [bdram_w[("gu", u)]], [bwgu[slot]])
            if not la_done:
                gu_load(0, 0, 0)
                gu_load(0, 1, 1)
            for j in range(IT):
                sg_, su_ = (2 * j) % 3, (2 * j + 1) % 3
                pg, pu = 2 * (j % 2), 2 * (j % 2) + 1
                with b.group("pe", s_pe):
                    for ct in range(CT):
                        b.op("pe", lambda E, j=j, ct=ct, sg_=sg_, pg=pg: E.matmul(
                            out=pb[pg][:, :], lhsT=wgu_s[sg_][:, ct * 128:(ct + 1) * 128],
                            rhs=xnb[:, ct * TCH:(ct + 1) * TCH],
                            start=(ct == 0), stop=(ct == CT - 1)),
                            reads=[bwgu[sg_], bxnb4[ct // 4]], writes=[bpb[pg]])
                with b.group("pe", s_pe):
                    for ct in range(CT):
                        b.op("pe", lambda E, j=j, ct=ct, su_=su_, pu=pu: E.matmul(
                            out=pb[pu][:, :], lhsT=wgu_s[su_][:, ct * 128:(ct + 1) * 128],
                            rhs=xnb[:, ct * TCH:(ct + 1) * TCH],
                            start=(ct == 0), stop=(ct == CT - 1)),
                            reads=[bwgu[su_], bxnb4[ct // 4]], writes=[bpb[pu]])
                if j + 1 < IT:
                    gu_load(j + 1, 0, (2 * j + 2) % 3)
                    gu_load(j + 1, 1, (2 * j + 3) % 3)
                if j == 30 and hook_xload is not None:
                    hook_xload()
                with b.group("act", s_act):
                    b.op("act", lambda E, pg=pg: E.activation(
                        out=ssg[:, :], in_=pb[pg][:, :], func=AF.Silu),
                        reads=[bpb[pg]], writes=[bssg])
                with b.group("dve", s_dve):
                    b.op("dve", lambda E, j=j, pu=pu: E.tensor_tensor(
                        out=h_v[:, j * TCH:(j + 1) * TCH], in0=ssg[:, :],
                        in1=pb[pu][:, :], op=OP.mult),
                        reads=[bssg, bpb[pu]], writes=[bh])
            fill_a1 = list(hook_fills()) if hook_fills is not None else []
            if dbg == "xnb" and q == 0:
                dma("gp", s_dqst, dbg_out, xnb[:, :], [bxnb], [Buf("dbg")])
            if dbg == "srow" and q == 0:
                dma("gp", s_dqst, dbg_out[0:1, 0:1536], srow[0:1, :],
                    [bsrow], [Buf("dbg")])
            if dbg == "h" and q == 0:
                dma("gp", s_dqst, dbg_out, h_v[:, 0:CT * TCH], [bh], [Buf("dbg")])
            if dbg == "guonly":
                dma("act", s_st[q % 2], dst_ap[q], xch[:, :], [bx], [bdst[q]])
                return
            dma("sync", s_wdn[0], wdn_s[0][:, :], wdn_full[0:128, :],
                [bdram_w[("dn", 0)]], [bwdn[0]])
            dma("sync", s_wdn[1], wdn_s[1][:, :], wdn_full[128:256, :],
                [bdram_w[("dn", 1)]], [bwdn[1]])
            pend_sq = None
            for ht in range(CT):
                sl = ht % 2
                py = 4 + (ht % 2)
                with b.group("pe", s_pe):
                    for it in range(IT):
                        b.op("pe", lambda E, ht=ht, it=it, sl=sl, py=py: E.matmul(
                            out=pb[py][:, :], lhsT=wdn_s[sl][:, it * 128:(it + 1) * 128],
                            rhs=h_v[:, it * TCH:(it + 1) * TCH],
                            start=(it == 0), stop=(it == IT - 1)),
                            reads=[bwdn[sl], bh], writes=[bpb[py]])
                if ht + 2 < CT:
                    dma("sync", s_wdn[(ht + 2) % 2], wdn_s[(ht + 2) % 2][:, :],
                        wdn_full[(ht + 2) * 128:(ht + 3) * 128, :],
                        [bdram_w[("dn", ht + 2)]], [bwdn[(ht + 2) % 2]])
                if pend_sq is not None:
                    hp, slot = pend_sq
                    with b.group("pe", s_pe):
                        b.op("pe", lambda E, hp=hp, slot=slot: E.matmul(
                            out=pb[6][0:1, :], lhsT=ones_b[:, :],
                            rhs=y2b[:, slot * 512:(slot + 1) * 512],
                            start=(hp == 0), stop=(hp == CT - 1)),
                            reads=[by2b[slot], bconsts], writes=[bpb[6]])
                with b.group("act", s_act):
                    b.op("act", lambda E, ht=ht, py=py: E.activation(
                        out=y2b[:, (ht % 2) * 512:(ht % 2 + 1) * 512],
                        in_=pb[py][:, :], func=AF.Square),
                        reads=[bpb[py]], writes=[by2b[ht % 2]])
                with b.group("dve", s_dve):
                    b.op("dve", lambda E, ht=ht, py=py: E.tensor_scalar(
                        out=ar3[:, ht * TCH:(ht + 1) * TCH], in0=pb[py][:, :],
                        scalar1=1.0, scalar2=None, op0=OP.mult),
                        reads=[bpb[py], by2b[ht % 2]], writes=[bysb])
                if fill_a1:
                    fill_a1.pop(0)()
                if fill_a1:
                    fill_a1.pop(0)()
                pend_sq = (ht, ht % 2)
            hp, slot = pend_sq
            with b.group("pe", s_pe):
                b.op("pe", lambda E, hp=hp, slot=slot: E.matmul(
                    out=pb[6][0:1, :], lhsT=ones_b[:, :],
                    rhs=y2b[:, slot * 512:(slot + 1) * 512],
                    start=(hp == 0), stop=(hp == CT - 1)),
                    reads=[by2b[slot], bconsts], writes=[bpb[6]])
            if dbg == "ysb" and q == 0:
                dma("gp", s_dqst, dbg_out, ar3[:, 0:8192], [bysb], [Buf("dbg")])
            with b.group("dve", s_dve):
                b.op("dve", lambda E: E.tensor_scalar(
                    out=m_row, in0=pb[6][0:1, :], scalar1=RECIP_C, scalar2=None,
                    op0=OP.mult), reads=[bpb[6]], writes=[bsrow])
            with b.group("act", s_act):
                b.op("act", lambda E: E.activation(
                    out=m_row, in_=m_row, func=AF.Ln, bias=bias_eps6[0:1, :]),
                    reads=[bsrow, bconsts], writes=[bsrow])
                b.op("act", lambda E: E.activation(
                    out=m_row, in_=m_row, func=AF.Exp, scale=-0.5, bias=bias_lnh[0:1, :]),
                    reads=[bsrow, bconsts], writes=[bsrow])
            with b.group("pe", s_pe):
                b.op("pe", lambda E: E.matmul(
                    out=pb[4][:, :], lhsT=ones_f[:, :], rhs=m_row,
                    start=True, stop=True), reads=[bsrow, bconsts], writes=[bpb[4]])
            if hook_tail is not None:
                hook_tail()
            for gi in range(4):
                if hook_xn is not None:
                    hook_xn(gi)
                with b.group("dve", s_dve):
                    for k in range(4):
                        ct = gi * 4 + k
                        b.op("dve", lambda E, ct=ct: E.tensor_tensor(
                            out=stats[:, :], in0=ar3[:, ct * TCH:(ct + 1) * TCH],
                            in1=pb[4][:, :], op=OP.mult),
                            reads=[bysb, bpb[4]], writes=[bstats])
                        b.op("dve", lambda E, ct=ct: E.tensor_tensor(
                            out=xch[:, ct * TCH:(ct + 1) * TCH],
                            in0=xch[:, ct * TCH:(ct + 1) * TCH],
                            in1=stats[:, :], op=OP.add),
                            reads=[bstats], writes=[bx])
            for f in fill_a1:
                f()
            dma("act", s_st[q % 2], dst_ap[q], xch[:, :], [bx], [bdst[q]])

        def watt_load(mat_ap, unit, slot):
            dma("gp", s_watt[slot], watt_ap[slot], mat_ap[unit], [], [bwatt[slot]])

        def emit_attn_ln(q):
            xch = xch_t[q % 2]
            bx = bxch[q % 2]
            emit_ln(xch, bx, xk_v, bxk, xv_v, bxv, 0, 1, 0, 1)

        def mix_ct_fns(q):
            xch = xch_t[q % 2]
            bx = bxch[q % 2]
            xn_t = xnb[:, 0:513]
            bxn_t = bxnb
            def mk(ct):
                return lambda: emit_mix_ct(q, ct)
            return [mk(ct) for ct in range(CT)]

        def emit_mix_ct(q, ct):
            xch = xch_t[q % 2]
            bx = bxch[q % 2]
            xn_t = xnb[:, 0:513]
            bxn_t = bxnb
            if True:
                with b.group("dve", s_dve):
                    b.op("dve", lambda E, ct=ct: E.tensor_copy(
                        out=xn_t[:, 0:1], in_=bnd_sb[:, ct:ct + 1]),
                        reads=[bbnd], writes=[bxn_t, bxnb4[0]])
                    b.op("dve", lambda E, ct=ct: E.tensor_tensor(
                        out=stats[:, :], in0=xch[:, ct * TCH:(ct + 1) * TCH],
                        in1=pb[0][:, :], op=OP.mult),
                        reads=[bx, bpb[0]], writes=[bstats])
                    b.op("dve", lambda E, ct=ct: E.tensor_tensor(
                        out=xn_t[:, 1:513], in0=stats[:, :], in1=pb[1][:, :],
                        op=OP.subtract), reads=[bstats, bpb[1]], writes=[bxn_t, bxnb4[0]])
                    b.op("dve", lambda E, ct=ct: E.tensor_copy(
                        out=bnd_sb[:, ct:ct + 1], in_=xn_t[:, 512:513]),
                        reads=[bxn_t], writes=[bbnd])
                    b.op("dve", lambda E, ct=ct: E.tensor_tensor(
                        out=y2b[:, 0:512], in0=xn_t[:, 1:513], in1=xn_t[:, 0:512],
                        op=OP.subtract), reads=[bxn_t], writes=[by2b[0]])
                    for tm_sb, xdst, bdst_ in ((tmk_sb, xk_v, bxk), (tmv_sb, xv_v, bxv),
                                               (tmr_sb, xr_v, bxr)):
                        b.op("dve", lambda E, ct=ct, tm_sb=tm_sb, xdst=xdst: E.scalar_tensor_tensor(
                            out=xdst[:, ct * TCH:(ct + 1) * TCH], in0=y2b[:, 0:512],
                            scalar=tm_sb[:, ct:ct + 1], in1=xn_t[:, 0:512],
                            op0=OP.mult, op1=OP.add),
                            reads=[by2b[0], bxn_t, bvecs], writes=[bdst_])
        def emit_attn_pre(q):
            emit_attn_ln(q)
            for f in mix_ct_fns(q):
                f()

        def emit_attn_main(q):
            xch = xch_t[q % 2]
            bx = bxch[q % 2]
            watt_load(wk_t, 0, 0)
            watt_load(wv_t, 0, 1)
            watt_load(wr_t, 0, 2)
            for j in range(CT):
                s0, s1, s2 = (3 * j) % 6, (3 * j + 1) % 6, (3 * j + 2) % 6
                if j + 1 < CT:
                    watt_load(wk_t, j + 1, (3 * j + 3) % 6)
                    watt_load(wv_t, j + 1, (3 * j + 4) % 6)
                    watt_load(wr_t, j + 1, (3 * j + 5) % 6)
                pk_, pv_, pr_ = 3 * (j % 2), 3 * (j % 2) + 1, 3 * (j % 2) + 2
                for (slot, src, pdst) in ((s0, bxk, pk_), (s1, bxv, pv_), (s2, bxr, pr_)):
                    xsrc = {id(bxk): xk_v, id(bxv): xv_v, id(bxr): xr_v}[id(src)]
                    with b.group("pe", s_pe):
                        for ct in range(CT):
                            b.op("pe", lambda E, slot=slot, ct=ct, xsrc=xsrc, pdst=pdst: E.matmul(
                                out=pb[pdst][:, :],
                                lhsT=watt_ap[slot][:, ct * 128:(ct + 1) * 128],
                                rhs=xsrc[:, ct * TCH:(ct + 1) * TCH],
                                start=(ct == 0), stop=(ct == CT - 1)),
                                reads=[bwatt[slot], src], writes=[bpb[pdst]])
                with b.group("act", s_act):
                    b.op("act", lambda E, pk_=pk_: E.activation(
                        out=sc_ek, in_=pb[pk_][:, :], func=AF.Exp),
                        reads=[bpb[pk_]], writes=[b_ek])
                    b.op("act", lambda E, pk_=pk_, j=j: E.activation(
                        out=sc_euk, in_=pb[pk_][:, :], func=AF.Exp, bias=u_sb[:, j:j + 1]),
                        reads=[bpb[pk_], bvecs], writes=[b_euk])
                    b.op("act", lambda E, pr_=pr_: E.activation(
                        out=sc_sr, in_=pb[pr_][:, :], func=AF.Exp, scale=-1.0),
                        reads=[bpb[pr_]], writes=[b_sr])
                    b.op("act", lambda E: E.activation(
                        out=sc_sr, in_=sc_sr, func=AF.Ln, bias=bias_one[:]),
                        reads=[b_sr, bconsts], writes=[b_sr])
                    b.op("act", lambda E: E.activation(
                        out=sc_sr, in_=sc_sr, func=AF.Exp, scale=-1.0),
                        reads=[b_sr], writes=[b_sr])
                with b.group("dve", s_dve):
                    b.op("dve", lambda E, pv_=pv_: E.tensor_tensor(
                        out=sc_ekv, in0=sc_ek, in1=pb[pv_][:, :], op=OP.mult),
                        reads=[b_ek, bpb[pv_]], writes=[b_ekv])
                    b.op("dve", lambda E, pv_=pv_: E.tensor_tensor(
                        out=sc_euv, in0=sc_euk, in1=pb[pv_][:, :], op=OP.mult),
                        reads=[b_euk, bpb[pv_]], writes=[b_euv])
                    b.op("dve", lambda E, j=j: E.tensor_tensor_scan(
                        out=sc_U[:, 1:513], data0=a_sb[:, j:j + 1].to_broadcast([128, 512]),
                        data1=sc_ekv, initial=cu_sb[:, j:j + 1],
                        op0=OP.mult, op1=OP.add),
                        reads=[b_ekv, bvecs, bcarry], writes=[b_U])
                    b.op("dve", lambda E, j=j: E.tensor_copy(
                        out=sc_U[:, 0:1], in_=cu_sb[:, j:j + 1]),
                        reads=[bcarry], writes=[b_U])
                    b.op("dve", lambda E, j=j: E.tensor_copy(
                        out=cu_sb[:, j:j + 1], in_=sc_U[:, 512:513]),
                        reads=[b_U], writes=[bcarry])
                    b.op("dve", lambda E, j=j: E.tensor_tensor_scan(
                        out=sc_D[:, 1:513], data0=a_sb[:, j:j + 1].to_broadcast([128, 512]),
                        data1=sc_ek, initial=cd_sb[:, j:j + 1],
                        op0=OP.mult, op1=OP.add),
                        reads=[b_ek, bvecs, bcarry], writes=[b_D])
                    b.op("dve", lambda E, j=j: E.tensor_copy(
                        out=sc_D[:, 0:1], in_=cd_sb[:, j:j + 1]),
                        reads=[bcarry], writes=[b_D])
                    b.op("dve", lambda E, j=j: E.tensor_copy(
                        out=cd_sb[:, j:j + 1], in_=sc_D[:, 512:513]),
                        reads=[b_D], writes=[bcarry])
                    b.op("dve", lambda E: E.tensor_tensor(
                        out=sc_euv, in0=sc_U[:, 0:512], in1=sc_euv, op=OP.add),
                        reads=[b_U, b_euv], writes=[b_euv])
                    b.op("dve", lambda E: E.tensor_tensor(
                        out=sc_euk, in0=sc_D[:, 0:512], in1=sc_euk, op=OP.add),
                        reads=[b_D, b_euk], writes=[b_euk])
                with b.group("act", s_act):
                    b.op("act", lambda E: E.activation(
                        out=sc_ek, in_=sc_euk, func=AF.Ln),
                        reads=[b_euk], writes=[b_ek])
                    b.op("act", lambda E: E.activation(
                        out=sc_ek, in_=sc_ek, func=AF.Exp, scale=-1.0),
                        reads=[b_ek], writes=[b_ek])
                with b.group("dve", s_dve):
                    b.op("dve", lambda E: E.tensor_tensor(
                        out=sc_ekv, in0=sc_euv, in1=sc_ek, op=OP.mult),
                        reads=[b_euv, b_ek], writes=[b_ekv])
                    b.op("dve", lambda E, j=j: E.tensor_tensor(
                        out=srr_v[:, j * TCH:(j + 1) * TCH], in0=sc_ekv, in1=sc_sr,
                        op=OP.mult), reads=[b_ekv, b_sr], writes=[bsrr])
        def emit_attn_wo(q, fillers=()):
            fillers = list(fillers)
            xch = xch_t[q % 2]
            bx = bxch[q % 2]
            watt_load(wo_t, 0, 0)
            watt_load(wo_t, 1, 1)
            for ct in range(CT):
                slot = ct % 6
                if ct + 2 < CT:
                    watt_load(wo_t, ct + 2, (ct + 2) % 6)
                po = 2 + ct % 2
                with b.group("pe", s_pe):
                    for dt in range(CT):
                        b.op("pe", lambda E, ct=ct, dt=dt, slot=slot, po=po: E.matmul(
                            out=pb[po][:, :], lhsT=watt_ap[slot][:, dt * 128:(dt + 1) * 128],
                            rhs=srr_v[:, dt * TCH:(dt + 1) * TCH],
                            start=(dt == 0), stop=(dt == CT - 1)),
                            reads=[bwatt[slot], bsrr], writes=[bpb[po]])
                with b.group("dve", s_dve):
                    b.op("dve", lambda E, ct=ct, po=po: E.tensor_tensor(
                        out=xch[:, ct * TCH:(ct + 1) * TCH],
                        in0=xch[:, ct * TCH:(ct + 1) * TCH],
                        in1=pb[po][:, :], op=OP.add),
                        reads=[bpb[po]], writes=[bx])
                if fillers:
                    fillers.pop(0)()
            for f in fillers:
                f()
            dma("act", s_st[q % 2], xbuf[q], xch[:, :], [bx], [bdram_x[q]])

        ssg_b = ssg.ap().bitcast(BF16)

        def gu_load01():
            dma("sync", s_wgu[0], wgu_s[0][:, :], wgu_full[0:128, :],
                [bdram_w[("gu", 0)]], [bwgu[0]])
            dma("sync", s_wgu[1], wgu_s[1][:, :], wgu_full[IT * 128:(IT + 1) * 128, :],
                [bdram_w[("gu", IT)]], [bwgu[1]])

        def ffn_hooks(q):
            if q + 1 >= NCH:
                return None, None, None
            nxt = xch_t[(q + 1) % 2]
            bnxt = bxch[(q + 1) % 2]

            def fills():
                def mk(ct):
                    def f():
                        with b.group("dve", s_dve):
                            b.op("dve", lambda E, ct=ct: E.tensor_scalar(
                                out=xnb[:, ct * TCH:(ct + 1) * TCH],
                                in0=nxt[:, ct * TCH:(ct + 1) * TCH],
                                scalar1=1.0, scalar2=None, op0=OP.mult),
                                reads=[bnxt], writes=[bxnb, bxnb4[ct // 4]])
                        with b.group("pe", s_pe):
                            b.op("pe", lambda E, ct=ct: E.matmul(
                                out=pb[0][0:1, :], lhsT=ones_b[:, :],
                                rhs=xnb[:, ct * TCH:(ct + 1) * TCH],
                                start=(ct == 0), stop=(ct == CT - 1)),
                                reads=[bxnb4[ct // 4], bconsts], writes=[bpb[0]])
                        with b.group("act", s_act):
                            b.op("act", lambda E, ct=ct: E.activation(
                                out=ssg_b[:, 0:TCH],
                                in_=nxt[:, ct * TCH:(ct + 1) * TCH], func=AF.Square),
                                reads=[bnxt], writes=[bssg])
                        with b.group("pe", s_pe):
                            b.op("pe", lambda E, ct=ct: E.matmul(
                                out=pb[1][0:1, :], lhsT=ones_b[:, :],
                                rhs=ssg_b[:, 0:TCH],
                                start=(ct == 0), stop=(ct == CT - 1)),
                                reads=[bssg, bconsts], writes=[bpb[1]])
                    return f
                return [mk(ct) for ct in range(CT)]

            def fin1():
                with b.group("dve", s_dve):
                    b.op("dve", lambda E: E.tensor_scalar(
                        out=m_row, in0=pb[0][0:1, :], scalar1=RECIP_C, scalar2=None,
                        op0=OP.mult), reads=[bpb[0]], writes=[bsrow])
                    b.op("dve", lambda E: E.tensor_scalar(
                        out=rstd_row, in0=pb[1][0:1, :], scalar1=RECIP_C, scalar2=None,
                        op0=OP.mult), reads=[bpb[1]], writes=[bsrow])
                    b.op("dve", lambda E: E.scalar_tensor_tensor(
                        out=mr_row, in0=m_row, scalar=1.0, in1=m_row,
                        op0=OP.mult, op1=OP.mult), reads=[bsrow], writes=[bsrow])
                    b.op("dve", lambda E: E.tensor_tensor(
                        out=rstd_row, in0=rstd_row, in1=mr_row, op=OP.subtract),
                        reads=[bsrow], writes=[bsrow])
                with b.group("act", s_act):
                    b.op("act", lambda E: E.activation(
                        out=rstd_row, in_=rstd_row, func=AF.Ln, bias=bias_eps5[0:1, :]),
                        reads=[bsrow, bconsts], writes=[bsrow])
                    b.op("act", lambda E: E.activation(
                        out=rstd_row, in_=rstd_row, func=AF.Exp, scale=-0.5),
                        reads=[bsrow], writes=[bsrow])
                with b.group("dve", s_dve):
                    b.op("dve", lambda E: E.tensor_tensor(
                        out=mr_row, in0=m_row, in1=rstd_row, op=OP.mult),
                        reads=[bsrow], writes=[bsrow])

            def fin2():
                with b.group("pe", s_pe):
                    b.op("pe", lambda E: E.matmul(
                        out=pb[2][:, :], lhsT=ones_f[:, :], rhs=rstd_row,
                        start=True, stop=True), reads=[bsrow, bconsts], writes=[bpb[2]])
                with b.group("pe", s_pe):
                    b.op("pe", lambda E: E.matmul(
                        out=pb[3][:, :], lhsT=ones_f[:, :], rhs=mr_row,
                        start=True, stop=True), reads=[bsrow, bconsts], writes=[bpb[3]])

            def xn(gi):
                def f():
                    with b.group("dve", s_dve):
                        for k in range(4):
                            ct = gi * 4 + k
                            b.op("dve", lambda E, ct=ct: E.tensor_tensor(
                                out=stats[:, :], in0=nxt[:, ct * TCH:(ct + 1) * TCH],
                                in1=pb[2][:, :], op=OP.mult),
                                reads=[bnxt, bpb[2]], writes=[bstats])
                            b.op("dve", lambda E, ct=ct: E.tensor_tensor(
                                out=xnb[:, ct * TCH:(ct + 1) * TCH], in0=stats[:, :],
                                in1=pb[3][:, :], op=OP.subtract),
                                reads=[bstats, bpb[3]], writes=[bxnb, bxnb4[ct // 4]])
                return f

            def all_fills():
                return fills() + [fin1, fin2] + [xn(gi) for gi in range(4)]

            return all_fills, gu_load01, None

        emit_x_load(0, x_in, bxch[0])
        for q in range(NCH):
            if "f1" in phases:
                hf, ht_, hx = ffn_hooks(q)
                hxl = (lambda q=q: emit_x_load(q + 1, x_in, bxch[(q + 1) % 2])) \
                    if q + 1 < NCH else None
                emit_ffn(q, True, xbuf, bdram_x, la_done=(q > 0),
                         hook_fills=hf, hook_tail=ht_, hook_xn=hx,
                         hook_xload=hxl)
            else:
                if q + 1 < NCH:
                    emit_x_load(q + 1, x_in, bxch[(q + 1) % 2])
                dma("act", s_st[q % 2], xbuf[q], xch_t[q % 2][:, :],
                    [bxch[q % 2]], [bdram_x[q]])
        alias_handoff([bxk, bxv, bxr], [bh])
        alias_handoff([bsrr, bscan, b_ek, b_euk, b_sr, b_ekv, b_euv, b_U, b_D], [bysb])
        for i in range(3):
            alias_handoff([bwatt[i]], [bwgu[i]])
        alias_handoff([bwatt[3], bwatt[4]], [bwdn[0]])
        alias_handoff([bwatt[5]], [bwdn[1]])
        if "at" in phases:
            dma("sync", s_xch[0], xch_t[0][:, :], xbuf[0], [bdram_x[0]], [bxch[0]])
            emit_attn_pre(0)
            for q in range(NCH):
                emit_attn_main(q)
                fill = []
                if q + 1 < NCH:
                    dma("sync", s_xch[(q + 1) % 2], xch_t[(q + 1) % 2][:, :],
                        xbuf[q + 1], [bdram_x[q + 1]], [bxch[(q + 1) % 2]])
                    emit_attn_ln(q + 1)
                    fill = mix_ct_fns(q + 1)
                emit_attn_wo(q, fill)
        alias_handoff([bh], [bxk, bxv, bxr])
        alias_handoff([bysb], [bsrr, bscan, b_ek, b_euk, b_sr, b_ekv, b_euv, b_U, b_D])
        for i in range(3):
            alias_handoff([bwgu[i]], [bwatt[i]])
        alias_handoff([bwdn[0]], [bwatt[3], bwatt[4]])
        alias_handoff([bwdn[1]], [bwatt[5]])
        by_out = [Buf(f"yout{i}") for i in range(NCH)]
        dma("sync", s_xch[0], xch_t[0][:, :], xbuf[0], [bdram_x[0]], [bxch[0]])
        for q in range(NCH):
            if "f2" in phases:
                hf, ht_, hx = ffn_hooks(q)
                hxl = (lambda q=q: dma(
                    "sync", s_xch[(q + 1) % 2], xch_t[(q + 1) % 2][:, :],
                    xbuf[q + 1], [bdram_x[q + 1]], [bxch[(q + 1) % 2]])) \
                    if q + 1 < NCH else None
                emit_ffn(q, False, y_out, by_out, la_done=(q > 0),
                         hook_fills=hf, hook_tail=ht_, hook_xn=hx,
                         hook_xload=hxl)
            else:
                if q + 1 < NCH:
                    dma("sync", s_xch[(q + 1) % 2], xch_t[(q + 1) % 2][:, :],
                        xbuf[q + 1], [bdram_x[q + 1]], [bxch[(q + 1) % 2]])
                dma("act", s_st[q % 2], y_out[q], xch_t[q % 2][:, :],
                    [bxch[q % 2]], [by_out[q]])

        for s in (s_st[0], s_st[1], s_dqst2[0], s_dqst2[1], s_dqpk[0], s_dqpk[1], s_init, s_cc):
            b.raw_wait("gp", s, s.planned)

        for s in b.sems.values():
            s.h = es.enter_context(nc.semaphore(s.name))
        blk = es.enter_context(nc.Block())
        blk.tensor(lambda E: _run_plan(b, "pe", E))
        blk.vector(lambda E: _run_plan(b, "dve", E))
        blk.scalar(lambda E: _run_plan(b, "act", E))
        blk.sync(lambda E: _run_plan(b, "sync", E))
        blk.gpsimd(lambda E: _run_plan(b, "gp", E))

    return nc



def build_in_maps(inp, TT=T):
    x = np.asarray(inp["L0_attn_x"])
    B = x.shape[0]
    wk_t = _tile_lhsT(np.ascontiguousarray(np.asarray(inp["L0_attn_Wk"]).T[ORDER, :]))
    wv_t = _tile_lhsT(np.ascontiguousarray(np.asarray(inp["L0_attn_Wv"]).T[ORDER, :]))
    wr_t = _tile_lhsT(np.ascontiguousarray(np.asarray(inp["L0_attn_Wr"]).T[ORDER, :]))
    wo_t = _tile_lhsT(np.ascontiguousarray(np.asarray(inp["L0_attn_Wo"]).T[:, ORDER]))
    _gupk = np.asarray(inp["L0_ffn_gate_up_packed"])
    gu_pk = np.ascontiguousarray(
        np.concatenate([_gupk[0:IDIM][ORDER_I], _gupk[IDIM:][ORDER_I]], axis=0))
    dn_pk = np.ascontiguousarray(np.asarray(inp["L0_ffn_down_packed"])[ORDER, :])
    gsc0 = np.asarray(inp["L0_ffn_gate_up_scales"])
    gsc = np.concatenate([gsc0[:, 0:IDIM][:, ORDER_I], gsc0[:, IDIM:][:, ORDER_I]], axis=1)
    gu_sc = np.ascontiguousarray(
        gsc.T.reshape(GU_UNITS, 128, 16).transpose(1, 0, 2).reshape(128, GU_UNITS * 16))
    dsc = np.asarray(inp["L0_ffn_down_scales"])[:, ORDER]
    dn_sc = np.ascontiguousarray(
        dsc.T.reshape(CT, 128, 44).transpose(1, 0, 2).reshape(128, CT * 44))
    vt = {"v_td": _vec_tile(np.asarray(inp["L0_attn_time_decay"])),
          "v_tf": _vec_tile(np.asarray(inp["L0_attn_time_first"])),
          "v_tmk": _vec_tile(np.asarray(inp["L0_attn_time_mix_k"])[ORDER]),
          "v_tmv": _vec_tile(np.asarray(inp["L0_attn_time_mix_v"])[ORDER]),
          "v_tmr": _vec_tile(np.asarray(inp["L0_attn_time_mix_r"])[ORDER])}
    ident = np.eye(128, dtype=ml_dtypes.bfloat16)
    in_maps = []
    for c in range(B):
        im = {"x_in": _chunks_fm(np.ascontiguousarray(x[c][:TT, ORDER])),
              "gu_pk": gu_pk, "gu_sc": gu_sc, "dn_pk": dn_pk, "dn_sc": dn_sc,
              "wk_t": wk_t, "wv_t": wv_t, "wr_t": wr_t, "wo_t": wo_t,
              "ident_in": ident}
        im.update(vt)
        in_maps.append(im)
    return in_maps


def kernel(L0_attn_x, L0_attn_time_decay, L0_attn_time_first, L0_attn_time_mix_k,
           L0_attn_time_mix_v, L0_attn_time_mix_r, L0_attn_Wk, L0_attn_Wv,
           L0_attn_Wr, L0_attn_Wo, L0_ffn_gate_up_packed, L0_ffn_gate_up_scales,
           L0_ffn_down_packed, L0_ffn_down_scales):
    inp = dict(L0_attn_x=L0_attn_x, L0_attn_time_decay=L0_attn_time_decay,
               L0_attn_time_first=L0_attn_time_first, L0_attn_time_mix_k=L0_attn_time_mix_k,
               L0_attn_time_mix_v=L0_attn_time_mix_v, L0_attn_time_mix_r=L0_attn_time_mix_r,
               L0_attn_Wk=L0_attn_Wk, L0_attn_Wv=L0_attn_Wv, L0_attn_Wr=L0_attn_Wr,
               L0_attn_Wo=L0_attn_Wo, L0_ffn_gate_up_packed=L0_ffn_gate_up_packed,
               L0_ffn_gate_up_scales=L0_ffn_gate_up_scales,
               L0_ffn_down_packed=L0_ffn_down_packed,
               L0_ffn_down_scales=L0_ffn_down_scales)
    B = np.asarray(L0_attn_x).shape[0]
    nc = build_nc(T, shard=False)
    in_maps = build_in_maps(inp, T)
    res = run_bass_kernel_spmd(nc, in_maps, core_ids=list(range(B)))
    out = np.empty((B, T, CDIM), dtype=np.float32)
    for c in range(B):
        out[c][:, ORDER] = _unchunk_fm(res.results[c]["y_out"])
    return out


# version 28
# speedup vs baseline: 1.0819x; 1.0012x over previous
import numpy as np
import ml_dtypes
import concourse.bass as bass
import concourse.mybir as mybir
from concourse.bass_utils import run_bass_kernel_spmd

F32 = mybir.dt.float32
F16 = mybir.dt.float16
BF16 = mybir.dt.bfloat16
I32 = mybir.dt.int32
AF = mybir.ActivationFunctionType
OP = mybir.AluOpType
LN2 = float(np.log(2.0))
LNHALF = float(np.log(0.5))

T = 2048
CDIM = 2048
IDIM = 5632
TCH = 512
CT = CDIM // 128
IT = IDIM // 128
GU_UNITS = 2 * IT
RECIP_C = 1.0 / 2048.0

ENGS = ("pe", "dve", "act", "sync", "gp")

_PERM = np.empty(CDIM, np.int64)
for _w in range(256):
    for _j in range(8):
        _PERM[8 * _w + _j] = _j * 256 + _w
ORDER = np.argsort(_PERM)
_PERM_I = np.empty(IDIM, np.int64)
for _w in range(704):
    for _j in range(8):
        _PERM_I[8 * _w + _j] = _j * 704 + _w
ORDER_I = np.argsort(_PERM_I)


class Sem:
    __slots__ = ("name", "planned", "h")

    def __init__(self, name):
        self.name = name
        self.planned = 0
        self.h = None


class Buf:
    __slots__ = ("name", "writer", "readers")

    def __init__(self, name):
        self.name = name
        self.writer = None
        self.readers = []


class PlOp:
    __slots__ = ("fn", "inc")

    def __init__(self, fn):
        self.fn = fn
        self.inc = None


class Builder:
    def __init__(self):
        self.plan = {e: [] for e in ENGS}
        self.sems = {}
        self.hw = {e: {} for e in ENGS}
        self.epoch = {e: 0 for e in ENGS}
        self.drain_at = {e: -1 for e in ENGS}
        self.cur_group = None

    def sem(self, name):
        s = Sem(name)
        self.sems[name] = s
        return s

    def _wait(self, eng, sem, val):
        if val <= 0:
            return
        if val <= self.hw[eng].get(sem.name, 0):
            return
        assert val <= sem.planned, (
            f"wait on {sem.name}={val} but only {sem.planned} planned (eng {eng})")
        self.hw[eng][sem.name] = val
        self.plan[eng].append(PlOp(lambda E, s=sem, v=val: E.wait_ge(s.h, v)))

    def _drain(self, eng):
        self.plan[eng].append(PlOp(lambda E: E.drain()))
        self.drain_at[eng] = self.epoch[eng]

    def group(self, eng, sem, n=1):
        assert self.cur_group is None
        self.cur_group = [eng, sem, n, None]
        return _GroupCtx(self)

    def _edge(self, eng, dep, this_is_dma):
        deng, dsem, dval, dep_, disdma = dep
        if deng == eng and not disdma and not this_is_dma:
            if dep_ > self.drain_at[eng] and eng in ("dve", "act", "gp"):
                self._drain(eng)
        else:
            assert dsem is not None, f"edge from {deng} has no milestone"
            self._wait(eng, dsem, dval)

    def op(self, eng, fn, reads=(), writes=()):
        g = self.cur_group
        assert g is not None and g[0] == eng, f"op on {eng} outside its group"
        sem, n = g[1], g[2]
        is_dma = (n == 16)
        for b in reads:
            if b.writer is not None:
                self._edge(eng, b.writer, is_dma)
        for b in writes:
            for r in b.readers:
                self._edge(eng, r, is_dma)
            if b.writer is not None:
                w = b.writer
                if w[0] == eng and not w[4] and not is_dma:
                    pass
                else:
                    self._edge(eng, w, is_dma)
        self.epoch[eng] += 1
        ep = self.epoch[eng]
        o = PlOp(fn)
        self.plan[eng].append(o)
        g[3] = o
        val_after = sem.planned + n
        for b in writes:
            b.writer = (eng, sem, val_after, ep, is_dma)
            b.readers = []
        for b in reads:
            b.readers.append((eng, sem, val_after, ep, is_dma))

    def close_group(self):
        eng, sem, n, last = self.cur_group
        assert last is not None, "empty group"
        sem.planned += n
        last.inc = (sem, n)
        self.cur_group = None

    def raw_wait(self, eng, sem, val):
        assert self.cur_group is None
        self._wait(eng, sem, val)

    def bare(self, eng, fn):
        assert self.cur_group is None
        self.plan[eng].append(PlOp(fn))


class _GroupCtx:
    def __init__(self, b):
        self.b = b

    def __enter__(self):
        return self.b

    def __exit__(self, *a):
        if a[0] is None:
            self.b.close_group()
        return False


def _run_plan(b, eng_name, E):
    for o in b.plan[eng_name]:
        inst = o.fn(E)
        if o.inc is not None:
            sem, n = o.inc
            inst.then_inc(sem.h, n)



def _chunks_fm(x2d):
    nq = x2d.shape[0] // TCH
    out = np.empty((nq, 128, CT * TCH), dtype=x2d.dtype)
    for q in range(nq):
        blk = x2d[q * TCH:(q + 1) * TCH, :]
        bt = np.ascontiguousarray(blk.T)
        out[q] = bt.reshape(CT, 128, TCH).transpose(1, 0, 2).reshape(128, CT * TCH)
    return out


def _unchunk_fm(ych):
    nq = ych.shape[0]
    out = np.empty((nq * TCH, CDIM), dtype=ych.dtype)
    for q in range(nq):
        blk = ych[q].reshape(128, CT, TCH).transpose(1, 0, 2).reshape(CDIM, TCH)
        out[q * TCH:(q + 1) * TCH, :] = blk.T
    return out


def _tile_lhsT(arr):
    K, M = arr.shape
    kt, mt = K // 128, M // 128
    return np.ascontiguousarray(
        arr.reshape(kt, 128, mt, 128).transpose(2, 1, 0, 3).reshape(mt, 128, kt * 128))


def _vec_tile(v):
    return np.ascontiguousarray(v.reshape(CT, 128).T)



def build_nc(T_=T, phases=("f1", "at", "f2"), dbg=None, shard=True):
    NCH = T_ // TCH
    GU_SH = GU_UNITS // 8 if shard else GU_UNITS
    DN_SH = CT // 8 if shard else CT
    nc = bass.Bass("TRN2", target_bir_lowering=False, debug=False)

    x_in = nc.dram_tensor("x_in", [NCH, 128, CT * TCH], F32, kind="ExternalInput").ap()
    gu_pk = nc.dram_tensor("gu_pk", [GU_SH * 128, 256], I32, kind="ExternalInput").ap()
    gu_sc = nc.dram_tensor("gu_sc", [128, GU_SH * 16], F16, kind="ExternalInput").ap()
    dn_pk = nc.dram_tensor("dn_pk", [DN_SH * 128, 704], I32, kind="ExternalInput").ap()
    dn_sc = nc.dram_tensor("dn_sc", [128, DN_SH * 44], F16, kind="ExternalInput").ap()
    wk_t = nc.dram_tensor("wk_t", [CT, 128, CDIM], F32, kind="ExternalInput").ap()
    wv_t = nc.dram_tensor("wv_t", [CT, 128, CDIM], F32, kind="ExternalInput").ap()
    wr_t = nc.dram_tensor("wr_t", [CT, 128, CDIM], F32, kind="ExternalInput").ap()
    wo_t = nc.dram_tensor("wo_t", [CT, 128, CDIM], F32, kind="ExternalInput").ap()
    v_td = nc.dram_tensor("v_td", [128, CT], F32, kind="ExternalInput").ap()
    v_tf = nc.dram_tensor("v_tf", [128, CT], F32, kind="ExternalInput").ap()
    v_tmk = nc.dram_tensor("v_tmk", [128, CT], F32, kind="ExternalInput").ap()
    v_tmv = nc.dram_tensor("v_tmv", [128, CT], F32, kind="ExternalInput").ap()
    v_tmr = nc.dram_tensor("v_tmr", [128, CT], F32, kind="ExternalInput").ap()
    ident_in = nc.dram_tensor("ident_in", [128, 128], BF16, kind="ExternalInput").ap()
    y_out = nc.dram_tensor("y_out", [NCH, 128, CT * TCH], F32, kind="ExternalOutput").ap()
    dbg_out = nc.dram_tensor("dbg_out", [128, CT * TCH], F32, kind="ExternalOutput").ap()
    xbuf = nc.dram_tensor("xbuf", [NCH, 128, CT * TCH], F32).ap()
    wgu_full_t = nc.dram_tensor("wgu_full", [GU_UNITS * 128, CT * 128], BF16)
    wdn_full_t = nc.dram_tensor("wdn_full", [CT * 128, IT * 128], BF16)
    wgu_full = wgu_full_t.ap()
    wdn_full = wdn_full_t.ap()
    if shard:
        cgu_t = nc.dram_tensor("cgu", [GU_SH * 128, CT * 128], BF16)
        cdn_t = nc.dram_tensor("cdn", [DN_SH * 128, IT * 128], BF16)
        cgu, cdn = cgu_t.ap(), cdn_t.ap()
    else:
        cgu, cdn = wgu_full, wdn_full

    from contextlib import ExitStack
    es = ExitStack()
    b = Builder()

    def sb(name, shape, dt):
        return es.enter_context(nc.sbuf_tensor(name, shape, dt))

    with es:
        xchA = sb("xchA", [128, CT * TCH], F32)
        xchB = sb("xchB", [128, CT * TCH], F32)
        ar2 = sb("ar2", [128, 3 * CT * TCH], BF16)
        ar3 = sb("ar3", [128, CT * TCH], F32)
        xnb = sb("xnb", [128, CT * TCH], BF16)
        wgu_s = [sb(f"wgu{i}", [128, CT * 128], BF16) for i in range(3)]
        wdn_s = [sb(f"wdn{i}", [128, IT * 128], BF16) for i in range(2)]
        stats = sb("stats", [128, 512], F32)
        srow = sb("srow", [128, 1536], F32)
        ssg = sb("ssg", [128, 512], F32)
        y2b = sb("y2b", [128, 1024], BF16)
        scf_gu = None
        scf_dn = None
        a_sb = sb("a_sb", [128, CT], F32)
        u_sb = sb("u_sb", [128, CT], F32)
        tmk_sb = sb("tmk_sb", [128, CT], F32)
        tmv_sb = sb("tmv_sb", [128, CT], F32)
        tmr_sb = sb("tmr_sb", [128, CT], F32)
        td_sb = sb("td_sb", [128, CT], F32)
        cu_sb = sb("cu_sb", [128, CT], F32)
        cd_sb = sb("cd_sb", [128, CT], F32)
        bnd_sb = sb("bnd_sb", [128, CT], BF16)
        ident = sb("ident", [128, 128], BF16)
        ones_b = sb("ones_b", [128, 1], BF16)
        ones_f = sb("ones_f", [1, 128], F32)
        bias_ln2n = sb("bias_ln2n", [128, 1], F32)
        bias_one = sb("bias_one", [128, 1], F32)
        bias_eps5 = sb("bias_eps5", [128, 1], F32)
        bias_eps6 = sb("bias_eps6", [128, 1], F32)
        bias_lnh = sb("bias_lnh", [128, 1], F32)
        bias_n6 = sb("bias_n6", [128, 1], F32)
        bias_n2 = sb("bias_n2", [128, 1], F32)
        WB = 768
        dq_pk = [xchA[:, 0:WB].bitcast(I32)[:, 0:WB],
                 xchA[:, WB:2 * WB].bitcast(I32)[:, 0:WB]]
        dq_m3 = [xchA[:, (2 + p) * WB:(3 + p) * WB].bitcast(I32)[:, 0:WB]
                 for p in range(4)]
        dq_si = [xchA[:, (6 + p) * WB:(7 + p) * WB].bitcast(I32)[:, 0:WB]
                 for p in range(4)]
        scf_gu = xchB[:, 0:GU_SH * 16]
        scf_dn = xnb[:, 2112:2112 + 2 * DN_SH * 44].bitcast(F32)[:, 0:DN_SH * 44]
        _o = GU_SH * 16
        def _bcarve(off_f32, n_bf16):
            return xchB[:, off_f32:off_f32 + (n_bf16 + 1) // 2].bitcast(BF16)[:, 0:n_bf16]
        def _bcarve3(off_f32, n_bf16):
            return ar3[:, off_f32:off_f32 + (n_bf16 + 1) // 2].bitcast(BF16)[:, 0:n_bf16]
        dq_h0, dq_r2 = [], []
        for _s in range(8):
            dq_h0.append(_bcarve(_o, WB)); _o += WB // 2
            dq_r2.append(_bcarve(_o, WB)); _o += WB // 2
        assert _o <= 8192, _o
        dq_r1 = [_bcarve3(_s * WB // 2, WB) for _s in range(8)]
        dq_s2m = [_bcarve3(8 * WB // 2, WB), _bcarve3(9 * WB // 2, WB),
                  _bcarve3(7682, WB),
                  xchA[:, 7680:7680 + WB // 2].bitcast(BF16)[:, 0:WB]] + \
                 [xchA[:, (2 + _p) * WB:(2 + _p) * WB + WB // 2]
                  .bitcast(BF16)[:, 0:WB] for _p in range(4)]
        dq_ub = [ar2[:, 0:IT * 128], ar2[:, IT * 128:2 * IT * 128]]
        dq_vb = [ar2[:, 2 * IT * 128:2 * IT * 128 + 6144],
                 ar2[:, 2 * IT * 128 + 6144:2 * IT * 128 + 12288]]

        pb = [es.enter_context(nc.psum_tensor(f"pb{i}", [128, 512], F32)) for i in range(7)]
        ptb = es.enter_context(nc.psum_tensor("ptb", [128, 1024], BF16))

        s_pe = b.sem("s_pe")
        s_dve = b.sem("s_dve")
        s_act = b.sem("s_act")
        s_init = b.sem("s_init")
        s_xch = [b.sem("s_xchA"), b.sem("s_xchB")]
        s_wgu = [b.sem(f"s_wgu{i}") for i in range(3)]
        s_wdn = [b.sem(f"s_wdn{i}") for i in range(2)]
        s_watt = [b.sem(f"s_watt{i}") for i in range(6)]
        s_dqpk = [b.sem("s_dqpkA"), b.sem("s_dqpkB")]
        s_dqst2 = [b.sem("s_dqstA"), b.sem("s_dqstB")]
        s_ptst = [b.sem("s_ptstA"), b.sem("s_ptstB")]
        s_st = [b.sem("s_stA"), b.sem("s_stB")]
        s_cc = b.sem("s_cc")

        B_ = Buf
        bxch = [B_("xchA"), B_("xchB")]
        bxnb = B_("xnb")
        bxnb4 = [B_(f"xnb{i}") for i in range(4)]
        bh = B_("h")
        bxk, bxv, bxr = B_("xk"), B_("xv"), B_("xr")
        bysb = B_("ysb")
        bsrr = B_("srr")
        bscan = B_("scan")
        b_ek, b_euk, b_sr = B_("ek"), B_("euk"), B_("sr")
        b_ekv, b_euv = B_("ekv"), B_("euv")
        b_U, b_D = B_("scU"), B_("scD")
        bstats = B_("stats")
        bsrow = B_("srow")
        bssg = B_("ssg")
        by2b = [B_("y2b0"), B_("y2b1")]
        bwgu = [B_(f"wgu{i}") for i in range(3)]
        bwdn = [B_(f"wdn{i}") for i in range(2)]
        bvecs = B_("vecs")
        bconsts = B_("consts")
        bident = B_("ident")
        bcarry = B_("carry")
        bbnd = B_("bnd")
        bpb = [B_(f"pb{i}") for i in range(7)]
        bptb = B_("ptb")
        bdq = B_("dq")
        bscf = B_("scf")
        bdram_w = {("gu", j): B_(f"dwgu{j}") for j in range(GU_UNITS)}
        bdram_w.update({("dn", j): B_(f"dwdn{j}") for j in range(CT)})
        bdram_x = [B_(f"dxbuf{q}") for q in range(NCH)]

        watt_ap = [wgu_s[0][:, :], wgu_s[1][:, :], wgu_s[2][:, :],
                   wdn_s[0][:, 0:CT * 128], wdn_s[0][:, CT * 128:2 * CT * 128],
                   wdn_s[1][:, 0:CT * 128]]
        bwatt = [B_(f"watt{i}") for i in range(6)]

        srr_v = ar3[:, 0:CT * TCH // 2].bitcast(BF16)[:, 0:CT * TCH]
        scf32 = ar3
        sc_ek = scf32[:, 4096:4608]
        sc_euk = scf32[:, 4608:5120]
        sc_sr = scf32[:, 5120:5632]
        sc_ekv = scf32[:, 5632:6144]
        sc_euv = scf32[:, 6144:6656]
        sc_U = scf32[:, 6656:7169]
        sc_D = scf32[:, 7169:7682]

        h_v = ar2[:, 0:IT * TCH]
        xk_v = ar2[:, 0:CT * TCH]
        xv_v = ar2[:, CT * TCH:2 * CT * TCH]
        xr_v = ar2[:, 2 * CT * TCH:3 * CT * TCH]

        xch_t = [xchA, xchB]

        with b.group("dve", s_dve):
            b.op("dve", lambda E: E.memset(ones_b[:], 1.0), writes=[bconsts])
            b.op("dve", lambda E: E.memset(ones_f[:], 1.0), writes=[bconsts])
            b.op("dve", lambda E: E.memset(bias_ln2n[:], -LN2), writes=[bconsts])
            b.op("dve", lambda E: E.memset(bias_one[:], 1.0), writes=[bconsts])
            b.op("dve", lambda E: E.memset(bias_eps5[:], 1e-5), writes=[bconsts])
            b.op("dve", lambda E: E.memset(bias_eps6[:], 1e-6), writes=[bconsts])
            b.op("dve", lambda E: E.memset(bias_lnh[:], LNHALF), writes=[bconsts])
            b.op("dve", lambda E: E.memset(bias_n6[:], -6.0), writes=[bconsts])
            b.op("dve", lambda E: E.memset(bias_n2[:], -2.0), writes=[bconsts])
            b.op("dve", lambda E: E.memset(cu_sb[:], 0.0), writes=[bcarry])
            b.op("dve", lambda E: E.memset(cd_sb[:], 0.0), writes=[bcarry])
            b.op("dve", lambda E: E.memset(bnd_sb[:], 0.0), writes=[bbnd])

        def dma(eng, sem, out_ap, in_ap, reads, writes):
            with b.group(eng, sem, 16):
                b.op(eng, lambda E, o=out_ap, i=in_ap: E.dma_start(out=o, in_=i),
                     reads=reads, writes=writes)

        dma("sync", s_init, ident[:], ident_in, [], [bident])
        for v_ap, tile in ((v_td, td_sb), (v_tf, u_sb), (v_tmk, tmk_sb),
                           (v_tmv, tmv_sb), (v_tmr, tmr_sb)):
            dma("sync", s_init, tile[:], v_ap, [], [bvecs])
        sc_gu16 = xnb[:, 0:GU_SH * 16].bitcast(F16)[:, 0:GU_SH * 16]
        sc_dn16 = xnb[:, GU_SH * 16:GU_SH * 16 + DN_SH * 44].bitcast(F16)[:, 0:DN_SH * 44]
        dma("sync", s_init, sc_gu16, gu_sc, [], [bxnb, bxnb4[0]])
        dma("sync", s_init, sc_dn16, dn_sc, [], [bxnb, bxnb4[0]])
        def bump_to_total(bufs, eng, sem):
            for bb_ in bufs:
                if bb_.writer is not None:
                    bb_.writer = (eng, sem, sem.planned, bb_.writer[3], True)
        bump_to_total([bident, bvecs, bxnb, bxnb4[0]], "sync", s_init)
        with b.group("act", s_act):
            b.op("act", lambda E: E.activation(out=scf_gu[:], in_=sc_gu16, func=AF.Copy),
                 reads=[bxnb], writes=[bscf])
            b.op("act", lambda E: E.activation(out=scf_dn[:], in_=sc_dn16, func=AF.Copy),
                 reads=[bxnb], writes=[bscf])
            b.op("act", lambda E: E.activation(out=a_sb[:], in_=td_sb[:], func=AF.Exp),
                 reads=[bvecs], writes=[bvecs])
            b.op("act", lambda E: E.activation(out=a_sb[:], in_=a_sb[:], func=AF.Exp,
                                               scale=-1.0),
                 reads=[bvecs], writes=[bvecs])

        def alias_handoff(dsts, srcs):
            rd, wr = [], None
            for s in srcs:
                rd.extend(s.readers)
                if s.writer is not None:
                    rd.append(s.writer)
                    wr = s.writer
            for d in dsts:
                d.writer = wr
                d.readers = list(rd)

        def emit_pk(kind, u0, nb, slot):
            W = 256 if kind == "gu" else 704
            pk_full = gu_pk if kind == "gu" else dn_pk
            pkt = dq_pk[slot]
            for k in range(nb):
                dma("sync", s_dqpk[slot], pkt[:, k * W:(k + 1) * W],
                    pk_full[(u0 + k) * 128:(u0 + k + 1) * 128, :], [], [bdq_pk[slot]])

        def emit_dq_planes(kind, u0, nb, slot, fill, barrier):
            W = 256 if kind == "gu" else 704
            NW = nb * W
            pkt = dq_pk[slot]
            vb = dq_vb[slot]
            def g1(jq):
                with b.group("dve", s_dve):
                    for p in range(4):
                        j = 4 * jq + p
                        b.op("dve", lambda E, j=j, p=p: E.tensor_scalar(
                            out=dq_m3[p][:, 0:NW], in0=pkt[:, 0:NW], scalar1=4 * j,
                            scalar2=7, op0=OP.logical_shift_right, op1=OP.bitwise_and),
                            reads=[bdq_pk[slot]], writes=[bdq_i[p]])
                        b.op("dve", lambda E, j=j, p=p: E.tensor_scalar(
                            out=dq_si[p][:, 0:NW], in0=pkt[:, 0:NW], scalar1=4 * j + 3,
                            scalar2=1, op0=OP.logical_shift_right, op1=OP.bitwise_and),
                            reads=[bdq_pk[slot]], writes=[bdq_i[p]])
                    for p in range(4):
                        s8 = 4 * jq + p
                        b.op("dve", lambda E, p=p, s8=s8: E.tensor_scalar(
                            out=dq_h0[s8][:, 0:NW], in0=dq_m3[p][:, 0:NW], scalar1=0.5,
                            scalar2=None, op0=OP.mult),
                            reads=[bdq_i[p]], writes=[b_dqh[s8]])
                    for p in range(4):
                        s8 = 4 * jq + p
                        ws = [b_dqs[s8]] + ([bdq_i[p]] if s8 >= 4 else [])
                        b.op("dve", lambda E, p=p, s8=s8: E.tensor_scalar(
                            out=dq_s2m[s8][:, 0:NW], in0=dq_si[p][:, 0:NW], scalar1=-2.0,
                            scalar2=1.0, op0=OP.mult, op1=OP.add),
                            reads=[bdq_i[p]], writes=ws)

            def gact(jq):
                with b.group("act", s_act):
                    for p in range(4):
                        s8 = 4 * jq + p
                        b.op("act", lambda E, s8=s8: E.activation(
                            out=dq_r2[s8][:, 0:NW], in_=dq_h0[s8][:, 0:NW],
                            func=AF.Relu, scale=2.0, bias=bias_n6[:]),
                            reads=[b_dqh[s8], bconsts], writes=[b_dqB[s8]])

            def g2(jq):
                with b.group("dve", s_dve):
                    for p in range(4):
                        s8 = 4 * jq + p
                        b.op("dve", lambda E, s8=s8: E.tensor_scalar(
                            out=dq_r1[s8][:, 0:NW], in0=dq_h0[s8][:, 0:NW],
                            scalar1=-2.0, scalar2=0.0, op0=OP.add, op1=OP.max),
                            reads=[b_dqh[s8]], writes=[b_dqr[s8]])
                    for p in range(4):
                        s8 = 4 * jq + p
                        b.op("dve", lambda E, s8=s8: E.tensor_tensor(
                            out=dq_r2[s8][:, 0:NW], in0=dq_h0[s8][:, 0:NW],
                            in1=dq_r2[s8][:, 0:NW], op=OP.add),
                            reads=[b_dqh[s8], b_dqB[s8]], writes=[b_dqB[s8]])
                    for p in range(4):
                        s8 = 4 * jq + p
                        b.op("dve", lambda E, s8=s8: E.tensor_tensor(
                            out=dq_r2[s8][:, 0:NW], in0=dq_r2[s8][:, 0:NW],
                            in1=dq_r1[s8][:, 0:NW], op=OP.add),
                            reads=[b_dqB[s8], b_dqr[s8]], writes=[b_dqB[s8]])
                    for p in range(4):
                        s8 = 4 * jq + p
                        j = 4 * jq + p
                        CW = W * 8
                        out_ap = vb[:, 0:nb * CW] \
                            .rearrange("p (k r) -> p k r", r=CW)[:, :, j * W:j * W + W] \
                            if nb > 1 else vb[:, j * W:j * W + W]
                        rd = [b_dqs[s8], b_dqB[s8]] + ([bdq_i[p]] if s8 >= 4 else [])
                        b.op("dve", lambda E, s8=s8, out_ap=out_ap: E.tensor_tensor(
                            out=out_ap, in0=dq_s2m[s8][:, 0:NW], in1=dq_r2[s8][:, 0:NW],
                            op=OP.mult),
                            reads=rd,
                            writes=[bdq_vbB[slot], bdq_vbS[slot]])

            g1(0)
            gact(0)
            g1(1)
            gact(1)
            fill()
            fill()
            barrier()
            g2(0)
            fill()
            g2(1)
            fill()

        def emit_dq_tail(kind, u0, nb, slot):
            W = 256 if kind == "gu" else 704
            nct = CT if kind == "gu" else IT
            scf = scf_gu if kind == "gu" else scf_dn
            ngrp = W * 8 // 128
            vb = dq_vb[slot]
            ub = dq_ub[slot]
            for k in range(nb):
                u = u0 + k
                scoff = u * (16 if kind == "gu" else 44)
                CW = W * 8
                with b.group("act", s_act):
                    for g in range(ngrp):
                        ap_g = vb[:, k * CW:(k + 1) * CW] \
                            .rearrange("p (j r) -> p j r", r=W)[:, :, 16 * g:16 * g + 16]
                        b.op("act", lambda E, ap_g=ap_g, g=g, scoff=scoff, scf=scf: E.activation(
                            out=ap_g, in_=ap_g,
                            func=AF.Copy, scale=scf[:, scoff + g:scoff + g + 1]),
                            reads=[bdq_vbB[slot], bscf], writes=[bdq_vbS[slot]])
                yield
                for t8 in range((ngrp + 7) // 8):
                    n_in = min(8, ngrp - t8 * 8)
                    with b.group("pe", s_pe):
                        for kk in range(n_in):
                            g = t8 * 8 + kk
                            c0 = k * ngrp * 128 + g * 128
                            b.op("pe", lambda E, c0=c0, kk=kk, vb=vb: E.transpose(
                                out=ptb[:, kk * 128:(kk + 1) * 128],
                                in_=vb[:, c0:c0 + 128], identity=ident[:]),
                                reads=[bdq_vbS[slot], bident], writes=[bptb])
                    with b.group("act", s_act):
                        b.op("act", lambda E, t8=t8, n_in=n_in, ub=ub: E.activation(
                            out=ub[:, t8 * 1024:t8 * 1024 + n_in * 128],
                            in_=ptb[:, 0:n_in * 128], func=AF.Copy),
                            reads=[bptb], writes=[bdq_ubB[slot]])
                    yield
                dst = (cgu if kind == "gu" else cdn)[u * 128:(u + 1) * 128, :]
                dma("gp", s_dqst2[slot], dst, ub[:, 0:nct * 128], [bdq_ubB[slot]],
                    [(b_cgu if kind == "gu" else b_cdn) if shard else bdram_w[(kind, u)]])

        b_cgu, b_cdn = Buf("cgu"), Buf("cdn")
        bdq_pk = [Buf("dqpk0"), Buf("dqpk1")]
        bdq_i = [Buf(f"dqi{p}") for p in range(4)]
        bdq_a = [Buf(f"dqa{p}") for p in range(4)]
        b_dqh = [Buf(f"dqh{p}") for p in range(8)]
        b_dqs = [Buf(f"dqs{p}") for p in range(8)]
        b_dqB = [Buf(f"dqB{p}") for p in range(8)]
        b_dqr = [Buf(f"dqr{p}") for p in range(8)]
        bdq_v = [Buf("dqv0"), Buf("dqv1")]
        bdq_vbB = [Buf("dqvb0"), Buf("dqvb1")]
        bdq_vbS = [Buf("dqvbs0"), Buf("dqvbs1")]
        bdq_ubB = [Buf("dqub0"), Buf("dqub1")]
        batches = []
        u = 0
        while u < GU_SH:
            nb = min(3, GU_SH - u)
            batches.append(("gu", u, nb))
            u += nb
        u = 0
        while u < DN_SH:
            batches.append(("dn", u, 1))
            u += 1
        pending = []
        _S = object()

        def fill():
            while pending:
                if next(pending[0], _S) is _S:
                    pending.pop(0)
                    continue
                return

        def barrier():
            while len(pending) > 1:
                for _ in pending.pop(0):
                    pass

        emit_pk(*batches[0], 0)
        for bi, (kind_, u0_, nb_) in enumerate(batches):
            if bi + 1 < len(batches):
                emit_pk(*batches[bi + 1], (bi + 1) % 2)
            if bi > 0:
                k2, u2, n2 = batches[bi - 1]
                pending.append(emit_dq_tail(k2, u2, n2, (bi - 1) % 2))
            emit_dq_planes(kind_, u0_, nb_, bi % 2, fill, barrier)
        for g in pending:
            for _ in g:
                pass
        k2, u2, n2 = batches[-1]
        for _ in emit_dq_tail(k2, u2, n2, (len(batches) - 1) % 2):
            pass
        alias_handoff([bxch[0], bxch[1], bh, bxk, bxv, bxr, bysb, bsrr, bscan,
                       b_ek, b_euk, b_sr, b_ekv, b_euv, b_U, b_D, bpb[6], bxnb,
                       bxnb4[0], bxnb4[1], bxnb4[2], bxnb4[3]],
                      [bdq, bscf, bdq_v[0], bdq_v[1], bdq_i[0], bdq_i[1],
                       bdq_i[2], bdq_i[3], bdq_a[0], bdq_a[1],
                       bdq_a[2], bdq_a[3], bdq_pk[0], bdq_pk[1],
                       b_dqh[0], b_dqh[1], b_dqh[2], b_dqh[3],
                       b_dqh[4], b_dqh[5], b_dqh[6], b_dqh[7],
                       b_dqs[0], b_dqs[1], b_dqs[2], b_dqs[3],
                       b_dqs[4], b_dqs[5], b_dqs[6], b_dqs[7],
                       b_dqB[0], b_dqB[1], b_dqB[2], b_dqB[3],
                       b_dqB[4], b_dqB[5], b_dqB[6], b_dqB[7],
                       b_dqr[0], b_dqr[1], b_dqr[2], b_dqr[3],
                       b_dqr[4], b_dqr[5], b_dqr[6], b_dqr[7],
                       bdq_vbB[0], bdq_vbB[1], bdq_vbS[0], bdq_vbS[1],
                       bdq_ubB[0], bdq_ubB[1], bptb])
        if shard:
            bump_to_total([b_cgu], "gp", s_dqst2[0])
            bump_to_total([b_cdn], "gp", s_dqst2[1])
            with b.group("gp", s_cc, 16):
                b.op("gp", lambda E: E.collective_compute(
                    "AllGather", OP.bypass, replica_groups=[list(range(8))],
                    ins=[cgu_t.ap().opt()], outs=[wgu_full_t.ap().opt()]),
                    reads=[b_cgu], writes=[bdram_w[("gu", 0)]])
            with b.group("gp", s_cc, 16):
                b.op("gp", lambda E: E.collective_compute(
                    "AllGather", OP.bypass, replica_groups=[list(range(8))],
                    ins=[cdn_t.ap().opt()], outs=[wdn_full_t.ap().opt()]),
                    reads=[b_cdn], writes=[bdram_w[("dn", 0)]])
            for (kk, uu), bb_ in bdram_w.items():
                bb_.writer = bdram_w[(kk, 0)].writer
        else:
            for (kk_, uu_), bb_ in bdram_w.items():
                pass
            done0 = ("gp", s_dqst2[0], s_dqst2[0].planned, 0, True)
            done1 = ("gp", s_dqst2[1], s_dqst2[1].planned, 0, True)
            for bb_ in bdram_w.values():
                bb_.writer = done0
                bb_.readers = [done1]

        m_row = srow[0:1, 0:512]
        rstd_row = srow[0:1, 512:1024]
        mr_row = srow[0:1, 1024:1536]

        def emit_ln_a1(xch, bx, xb16, bxb16, subbufs=None):
            with b.group("dve", s_dve):
                for ct in range(CT):
                    ws = [bxb16] + ([subbufs[ct // 4]] if subbufs else [])
                    b.op("dve", lambda E, ct=ct, xch=xch, xb16=xb16: E.tensor_scalar(
                        out=xb16[:, ct * TCH:(ct + 1) * TCH],
                        in0=xch[:, ct * TCH:(ct + 1) * TCH],
                        scalar1=1.0, scalar2=None, op0=OP.mult),
                        reads=[bx], writes=ws)

        def emit_ln_a2(xch, bx, xsq, bxsq):
            for gi in range(4):
                with b.group("act", s_act):
                    for k in range(4):
                        ct = gi * 4 + k
                        b.op("act", lambda E, ct=ct, xch=xch, xsq=xsq: E.activation(
                            out=xsq[:, ct * TCH:(ct + 1) * TCH],
                            in_=xch[:, ct * TCH:(ct + 1) * TCH], func=AF.Square),
                            reads=[bx], writes=[bxsq])

        def emit_ln(xch, bx, xb16, bxb16, xsq, bxsq, st0, st1, bc0, bc1,
                    prepped=False, subbufs=None):
            if not prepped:
                emit_ln_a1(xch, bx, xb16, bxb16, subbufs=subbufs)
                emit_ln_a2(xch, bx, xsq, bxsq)
            with b.group("pe", s_pe):
                for ct in range(CT):
                    b.op("pe", lambda E, ct=ct, xb16=xb16: E.matmul(
                        out=pb[st0][0:1, :], lhsT=ones_b[:, :],
                        rhs=xb16[:, ct * TCH:(ct + 1) * TCH],
                        start=(ct == 0), stop=(ct == CT - 1)),
                        reads=[bxb16, bconsts], writes=[bpb[st0]])
            with b.group("pe", s_pe):
                for ct in range(CT):
                    b.op("pe", lambda E, ct=ct, xsq=xsq: E.matmul(
                        out=pb[st1][0:1, :], lhsT=ones_b[:, :],
                        rhs=xsq[:, ct * TCH:(ct + 1) * TCH],
                        start=(ct == 0), stop=(ct == CT - 1)),
                        reads=[bxsq, bconsts], writes=[bpb[st1]])
            with b.group("dve", s_dve):
                b.op("dve", lambda E: E.tensor_scalar(
                    out=m_row, in0=pb[st0][0:1, :], scalar1=RECIP_C, scalar2=None,
                    op0=OP.mult), reads=[bpb[st0]], writes=[bsrow])
                b.op("dve", lambda E: E.tensor_scalar(
                    out=rstd_row, in0=pb[st1][0:1, :], scalar1=RECIP_C, scalar2=None,
                    op0=OP.mult), reads=[bpb[st1]], writes=[bsrow])
                b.op("dve", lambda E: E.scalar_tensor_tensor(
                    out=mr_row, in0=m_row, scalar=1.0, in1=m_row,
                    op0=OP.mult, op1=OP.mult), reads=[bsrow], writes=[bsrow])
                b.op("dve", lambda E: E.tensor_tensor(
                    out=rstd_row, in0=rstd_row, in1=mr_row, op=OP.subtract),
                    reads=[bsrow], writes=[bsrow])
            with b.group("act", s_act):
                b.op("act", lambda E: E.activation(
                    out=rstd_row, in_=rstd_row, func=AF.Ln, bias=bias_eps5[0:1, :]),
                    reads=[bsrow, bconsts], writes=[bsrow])
                b.op("act", lambda E: E.activation(
                    out=rstd_row, in_=rstd_row, func=AF.Exp, scale=-0.5),
                    reads=[bsrow], writes=[bsrow])
            with b.group("dve", s_dve):
                b.op("dve", lambda E: E.tensor_tensor(
                    out=mr_row, in0=m_row, in1=rstd_row, op=OP.mult),
                    reads=[bsrow], writes=[bsrow])
            with b.group("pe", s_pe):
                b.op("pe", lambda E: E.matmul(
                    out=pb[bc0][:, :], lhsT=ones_f[:, :], rhs=rstd_row,
                    start=True, stop=True), reads=[bsrow, bconsts], writes=[bpb[bc0]])
            with b.group("pe", s_pe):
                b.op("pe", lambda E: E.matmul(
                    out=pb[bc1][:, :], lhsT=ones_f[:, :], rhs=mr_row,
                    start=True, stop=True), reads=[bsrow, bconsts], writes=[bpb[bc1]])

        def emit_x_load(q, src_ap, bx):
            dma("sync", s_xch[q % 2], xch_t[q % 2][:, :], src_ap[q], [], [bx])

        def emit_ffn(q, src_is_input, dst_ap, bdst, la_done=False,
                     hook_fills=None, hook_tail=None, hook_xn=None,
                     hook_xload=None):
            xch = xch_t[q % 2]
            bx = bxch[q % 2]
            if not la_done:
                emit_ln(xch, bx, xnb, bxnb, h_v, bh, 4, 5, 4, 5,
                        prepped=False, subbufs=bxnb4)
                for gi in range(4):
                    with b.group("dve", s_dve):
                        for k in range(4):
                            ct = gi * 4 + k
                            b.op("dve", lambda E, ct=ct: E.tensor_tensor(
                                out=stats[:, :], in0=xch[:, ct * TCH:(ct + 1) * TCH],
                                in1=pb[4][:, :], op=OP.mult),
                                reads=[bx, bpb[4]], writes=[bstats])
                            b.op("dve", lambda E, ct=ct: E.tensor_tensor(
                                out=xnb[:, ct * TCH:(ct + 1) * TCH], in0=stats[:, :],
                                in1=pb[5][:, :], op=OP.subtract),
                                reads=[bstats, bpb[5]], writes=[bxnb, bxnb4[ct // 4]])
            if dbg == "lnonly":
                dma("act", s_st[q % 2], dst_ap[q], xch[:, :], [bx], [bdst[q]])
                return
            def gu_load(j, which, slot):
                u = j if which == 0 else j + IT
                dma("sync", s_wgu[slot], wgu_s[slot][:, :],
                    wgu_full[u * 128:(u + 1) * 128, :],
                    [bdram_w[("gu", u)]], [bwgu[slot]])
            if not la_done:
                gu_load(0, 0, 0)
                gu_load(0, 1, 1)
            for j in range(IT):
                sg_, su_ = (2 * j) % 3, (2 * j + 1) % 3
                pg, pu = 2 * (j % 2), 2 * (j % 2) + 1
                with b.group("pe", s_pe):
                    for ct in range(CT):
                        b.op("pe", lambda E, j=j, ct=ct, sg_=sg_, pg=pg: E.matmul(
                            out=pb[pg][:, :], lhsT=wgu_s[sg_][:, ct * 128:(ct + 1) * 128],
                            rhs=xnb[:, ct * TCH:(ct + 1) * TCH],
                            start=(ct == 0), stop=(ct == CT - 1)),
                            reads=[bwgu[sg_], bxnb4[ct // 4]], writes=[bpb[pg]])
                with b.group("pe", s_pe):
                    for ct in range(CT):
                        b.op("pe", lambda E, j=j, ct=ct, su_=su_, pu=pu: E.matmul(
                            out=pb[pu][:, :], lhsT=wgu_s[su_][:, ct * 128:(ct + 1) * 128],
                            rhs=xnb[:, ct * TCH:(ct + 1) * TCH],
                            start=(ct == 0), stop=(ct == CT - 1)),
                            reads=[bwgu[su_], bxnb4[ct // 4]], writes=[bpb[pu]])
                if j + 1 < IT:
                    gu_load(j + 1, 0, (2 * j + 2) % 3)
                    gu_load(j + 1, 1, (2 * j + 3) % 3)
                if j == 30 and hook_xload is not None:
                    hook_xload()
                with b.group("act", s_act):
                    b.op("act", lambda E, pg=pg: E.activation(
                        out=ssg[:, :], in_=pb[pg][:, :], func=AF.Silu),
                        reads=[bpb[pg]], writes=[bssg])
                with b.group("dve", s_dve):
                    b.op("dve", lambda E, j=j, pu=pu: E.tensor_tensor(
                        out=h_v[:, j * TCH:(j + 1) * TCH], in0=ssg[:, :],
                        in1=pb[pu][:, :], op=OP.mult),
                        reads=[bssg, bpb[pu]], writes=[bh])
            fill_a1 = list(hook_fills()) if hook_fills is not None else []
            if dbg == "xnb" and q == 0:
                dma("gp", s_dqst, dbg_out, xnb[:, :], [bxnb], [Buf("dbg")])
            if dbg == "srow" and q == 0:
                dma("gp", s_dqst, dbg_out[0:1, 0:1536], srow[0:1, :],
                    [bsrow], [Buf("dbg")])
            if dbg == "h" and q == 0:
                dma("gp", s_dqst, dbg_out, h_v[:, 0:CT * TCH], [bh], [Buf("dbg")])
            if dbg == "guonly":
                dma("act", s_st[q % 2], dst_ap[q], xch[:, :], [bx], [bdst[q]])
                return
            dma("sync", s_wdn[0], wdn_s[0][:, :], wdn_full[0:128, :],
                [bdram_w[("dn", 0)]], [bwdn[0]])
            dma("sync", s_wdn[1], wdn_s[1][:, :], wdn_full[128:256, :],
                [bdram_w[("dn", 1)]], [bwdn[1]])
            pend_sq = None
            for ht in range(CT):
                sl = ht % 2
                py = 4 + (ht % 2)
                with b.group("pe", s_pe):
                    for it in range(IT):
                        b.op("pe", lambda E, ht=ht, it=it, sl=sl, py=py: E.matmul(
                            out=pb[py][:, :], lhsT=wdn_s[sl][:, it * 128:(it + 1) * 128],
                            rhs=h_v[:, it * TCH:(it + 1) * TCH],
                            start=(it == 0), stop=(it == IT - 1)),
                            reads=[bwdn[sl], bh], writes=[bpb[py]])
                if ht + 2 < CT:
                    dma("sync", s_wdn[(ht + 2) % 2], wdn_s[(ht + 2) % 2][:, :],
                        wdn_full[(ht + 2) * 128:(ht + 3) * 128, :],
                        [bdram_w[("dn", ht + 2)]], [bwdn[(ht + 2) % 2]])
                if pend_sq is not None:
                    hp, slot = pend_sq
                    with b.group("pe", s_pe):
                        b.op("pe", lambda E, hp=hp, slot=slot: E.matmul(
                            out=pb[6][0:1, :], lhsT=ones_b[:, :],
                            rhs=y2b[:, slot * 512:(slot + 1) * 512],
                            start=(hp == 0), stop=(hp == CT - 1)),
                            reads=[by2b[slot], bconsts], writes=[bpb[6]])
                with b.group("act", s_act):
                    b.op("act", lambda E, ht=ht, py=py: E.activation(
                        out=y2b[:, (ht % 2) * 512:(ht % 2 + 1) * 512],
                        in_=pb[py][:, :], func=AF.Square),
                        reads=[bpb[py]], writes=[by2b[ht % 2]])
                with b.group("dve", s_dve):
                    b.op("dve", lambda E, ht=ht, py=py: E.tensor_scalar(
                        out=ar3[:, ht * TCH:(ht + 1) * TCH], in0=pb[py][:, :],
                        scalar1=1.0, scalar2=None, op0=OP.mult),
                        reads=[bpb[py], by2b[ht % 2]], writes=[bysb])
                if fill_a1:
                    fill_a1.pop(0)()
                if fill_a1:
                    fill_a1.pop(0)()
                pend_sq = (ht, ht % 2)
            hp, slot = pend_sq
            with b.group("pe", s_pe):
                b.op("pe", lambda E, hp=hp, slot=slot: E.matmul(
                    out=pb[6][0:1, :], lhsT=ones_b[:, :],
                    rhs=y2b[:, slot * 512:(slot + 1) * 512],
                    start=(hp == 0), stop=(hp == CT - 1)),
                    reads=[by2b[slot], bconsts], writes=[bpb[6]])
            if dbg == "ysb" and q == 0:
                dma("gp", s_dqst, dbg_out, ar3[:, 0:8192], [bysb], [Buf("dbg")])
            with b.group("dve", s_dve):
                b.op("dve", lambda E: E.tensor_scalar(
                    out=m_row, in0=pb[6][0:1, :], scalar1=RECIP_C, scalar2=None,
                    op0=OP.mult), reads=[bpb[6]], writes=[bsrow])
            with b.group("act", s_act):
                b.op("act", lambda E: E.activation(
                    out=m_row, in_=m_row, func=AF.Ln, bias=bias_eps6[0:1, :]),
                    reads=[bsrow, bconsts], writes=[bsrow])
                b.op("act", lambda E: E.activation(
                    out=m_row, in_=m_row, func=AF.Exp, scale=-0.5, bias=bias_lnh[0:1, :]),
                    reads=[bsrow, bconsts], writes=[bsrow])
            with b.group("pe", s_pe):
                b.op("pe", lambda E: E.matmul(
                    out=pb[4][:, :], lhsT=ones_f[:, :], rhs=m_row,
                    start=True, stop=True), reads=[bsrow, bconsts], writes=[bpb[4]])
            if hook_tail is not None:
                hook_tail()
            for gi in range(4):
                if hook_xn is not None:
                    hook_xn(gi)
                with b.group("dve", s_dve):
                    for k in range(4):
                        ct = gi * 4 + k
                        b.op("dve", lambda E, ct=ct: E.tensor_tensor(
                            out=stats[:, :], in0=ar3[:, ct * TCH:(ct + 1) * TCH],
                            in1=pb[4][:, :], op=OP.mult),
                            reads=[bysb, bpb[4]], writes=[bstats])
                        b.op("dve", lambda E, ct=ct: E.tensor_tensor(
                            out=xch[:, ct * TCH:(ct + 1) * TCH],
                            in0=xch[:, ct * TCH:(ct + 1) * TCH],
                            in1=stats[:, :], op=OP.add),
                            reads=[bstats], writes=[bx])
            for f in fill_a1:
                f()
            dma("act", s_st[q % 2], dst_ap[q], xch[:, :], [bx], [bdst[q]])

        def watt_load(mat_ap, unit, slot):
            dma("gp", s_watt[slot], watt_ap[slot], mat_ap[unit], [], [bwatt[slot]])

        def emit_attn_ln(q):
            xch = xch_t[q % 2]
            bx = bxch[q % 2]
            emit_ln(xch, bx, xk_v, bxk, xv_v, bxv, 0, 1, 0, 1)

        def mix_ct_fns(q):
            xch = xch_t[q % 2]
            bx = bxch[q % 2]
            xn_t = xnb[:, 0:513]
            bxn_t = bxnb
            def mk(ct):
                return lambda: emit_mix_ct(q, ct)
            return [mk(ct) for ct in range(CT)]

        def emit_mix_ct(q, ct):
            xch = xch_t[q % 2]
            bx = bxch[q % 2]
            xn_t = xnb[:, 0:513]
            bxn_t = bxnb
            if True:
                with b.group("dve", s_dve):
                    b.op("dve", lambda E, ct=ct: E.tensor_copy(
                        out=xn_t[:, 0:1], in_=bnd_sb[:, ct:ct + 1]),
                        reads=[bbnd], writes=[bxn_t, bxnb4[0]])
                    b.op("dve", lambda E, ct=ct: E.tensor_tensor(
                        out=stats[:, :], in0=xch[:, ct * TCH:(ct + 1) * TCH],
                        in1=pb[0][:, :], op=OP.mult),
                        reads=[bx, bpb[0]], writes=[bstats])
                    b.op("dve", lambda E, ct=ct: E.tensor_tensor(
                        out=xn_t[:, 1:513], in0=stats[:, :], in1=pb[1][:, :],
                        op=OP.subtract), reads=[bstats, bpb[1]], writes=[bxn_t, bxnb4[0]])
                    b.op("dve", lambda E, ct=ct: E.tensor_copy(
                        out=bnd_sb[:, ct:ct + 1], in_=xn_t[:, 512:513]),
                        reads=[bxn_t], writes=[bbnd])
                    b.op("dve", lambda E, ct=ct: E.tensor_tensor(
                        out=y2b[:, 0:512], in0=xn_t[:, 1:513], in1=xn_t[:, 0:512],
                        op=OP.subtract), reads=[bxn_t], writes=[by2b[0]])
                    for tm_sb, xdst, bdst_ in ((tmk_sb, xk_v, bxk), (tmv_sb, xv_v, bxv),
                                               (tmr_sb, xr_v, bxr)):
                        b.op("dve", lambda E, ct=ct, tm_sb=tm_sb, xdst=xdst: E.scalar_tensor_tensor(
                            out=xdst[:, ct * TCH:(ct + 1) * TCH], in0=y2b[:, 0:512],
                            scalar=tm_sb[:, ct:ct + 1], in1=xn_t[:, 0:512],
                            op0=OP.mult, op1=OP.add),
                            reads=[by2b[0], bxn_t, bvecs], writes=[bdst_])
        def emit_attn_pre(q):
            emit_attn_ln(q)
            for f in mix_ct_fns(q):
                f()

        def emit_attn_main(q):
            xch = xch_t[q % 2]
            bx = bxch[q % 2]
            watt_load(wk_t, 0, 0)
            watt_load(wv_t, 0, 1)
            watt_load(wr_t, 0, 2)
            watt_load(wk_t, 1, 3)
            watt_load(wv_t, 1, 4)
            watt_load(wr_t, 1, 5)
            for j in range(CT):
                s0, s1, s2 = (3 * j) % 6, (3 * j + 1) % 6, (3 * j + 2) % 6
                pk_, pv_, pr_ = 3 * (j % 2), 3 * (j % 2) + 1, 3 * (j % 2) + 2
                for (slot, src, pdst) in ((s0, bxk, pk_), (s1, bxv, pv_), (s2, bxr, pr_)):
                    xsrc = {id(bxk): xk_v, id(bxv): xv_v, id(bxr): xr_v}[id(src)]
                    with b.group("pe", s_pe):
                        for ct in range(CT):
                            b.op("pe", lambda E, slot=slot, ct=ct, xsrc=xsrc, pdst=pdst: E.matmul(
                                out=pb[pdst][:, :],
                                lhsT=watt_ap[slot][:, ct * 128:(ct + 1) * 128],
                                rhs=xsrc[:, ct * TCH:(ct + 1) * TCH],
                                start=(ct == 0), stop=(ct == CT - 1)),
                                reads=[bwatt[slot], src], writes=[bpb[pdst]])
                if j + 2 < CT:
                    watt_load(wk_t, j + 2, s0)
                    watt_load(wv_t, j + 2, s1)
                    watt_load(wr_t, j + 2, s2)
                with b.group("act", s_act):
                    b.op("act", lambda E, pk_=pk_: E.activation(
                        out=sc_ek, in_=pb[pk_][:, :], func=AF.Exp),
                        reads=[bpb[pk_]], writes=[b_ek])
                    b.op("act", lambda E, pk_=pk_, j=j: E.activation(
                        out=sc_euk, in_=pb[pk_][:, :], func=AF.Exp, bias=u_sb[:, j:j + 1]),
                        reads=[bpb[pk_], bvecs], writes=[b_euk])
                    b.op("act", lambda E, pr_=pr_: E.activation(
                        out=sc_sr, in_=pb[pr_][:, :], func=AF.Exp, scale=-1.0),
                        reads=[bpb[pr_]], writes=[b_sr])
                    b.op("act", lambda E: E.activation(
                        out=sc_sr, in_=sc_sr, func=AF.Ln, bias=bias_one[:]),
                        reads=[b_sr, bconsts], writes=[b_sr])
                    b.op("act", lambda E: E.activation(
                        out=sc_sr, in_=sc_sr, func=AF.Exp, scale=-1.0),
                        reads=[b_sr], writes=[b_sr])
                with b.group("dve", s_dve):
                    b.op("dve", lambda E, pv_=pv_: E.tensor_tensor(
                        out=sc_ekv, in0=sc_ek, in1=pb[pv_][:, :], op=OP.mult),
                        reads=[b_ek, bpb[pv_]], writes=[b_ekv])
                    b.op("dve", lambda E, pv_=pv_: E.tensor_tensor(
                        out=sc_euv, in0=sc_euk, in1=pb[pv_][:, :], op=OP.mult),
                        reads=[b_euk, bpb[pv_]], writes=[b_euv])
                    b.op("dve", lambda E, j=j: E.tensor_tensor_scan(
                        out=sc_U[:, 1:513], data0=a_sb[:, j:j + 1].to_broadcast([128, 512]),
                        data1=sc_ekv, initial=cu_sb[:, j:j + 1],
                        op0=OP.mult, op1=OP.add),
                        reads=[b_ekv, bvecs, bcarry], writes=[b_U])
                    b.op("dve", lambda E, j=j: E.tensor_copy(
                        out=sc_U[:, 0:1], in_=cu_sb[:, j:j + 1]),
                        reads=[bcarry], writes=[b_U])
                    b.op("dve", lambda E, j=j: E.tensor_copy(
                        out=cu_sb[:, j:j + 1], in_=sc_U[:, 512:513]),
                        reads=[b_U], writes=[bcarry])
                    b.op("dve", lambda E, j=j: E.tensor_tensor_scan(
                        out=sc_D[:, 1:513], data0=a_sb[:, j:j + 1].to_broadcast([128, 512]),
                        data1=sc_ek, initial=cd_sb[:, j:j + 1],
                        op0=OP.mult, op1=OP.add),
                        reads=[b_ek, bvecs, bcarry], writes=[b_D])
                    b.op("dve", lambda E, j=j: E.tensor_copy(
                        out=sc_D[:, 0:1], in_=cd_sb[:, j:j + 1]),
                        reads=[bcarry], writes=[b_D])
                    b.op("dve", lambda E, j=j: E.tensor_copy(
                        out=cd_sb[:, j:j + 1], in_=sc_D[:, 512:513]),
                        reads=[b_D], writes=[bcarry])
                    b.op("dve", lambda E: E.tensor_tensor(
                        out=sc_euv, in0=sc_U[:, 0:512], in1=sc_euv, op=OP.add),
                        reads=[b_U, b_euv], writes=[b_euv])
                    b.op("dve", lambda E: E.tensor_tensor(
                        out=sc_euk, in0=sc_D[:, 0:512], in1=sc_euk, op=OP.add),
                        reads=[b_D, b_euk], writes=[b_euk])
                with b.group("act", s_act):
                    b.op("act", lambda E: E.activation(
                        out=sc_ek, in_=sc_euk, func=AF.Ln),
                        reads=[b_euk], writes=[b_ek])
                    b.op("act", lambda E: E.activation(
                        out=sc_ek, in_=sc_ek, func=AF.Exp, scale=-1.0),
                        reads=[b_ek], writes=[b_ek])
                with b.group("dve", s_dve):
                    b.op("dve", lambda E: E.tensor_tensor(
                        out=sc_ekv, in0=sc_euv, in1=sc_ek, op=OP.mult),
                        reads=[b_euv, b_ek], writes=[b_ekv])
                    b.op("dve", lambda E, j=j: E.tensor_tensor(
                        out=srr_v[:, j * TCH:(j + 1) * TCH], in0=sc_ekv, in1=sc_sr,
                        op=OP.mult), reads=[b_ekv, b_sr], writes=[bsrr])
        def emit_attn_wo(q, fillers=()):
            fillers = list(fillers)
            xch = xch_t[q % 2]
            bx = bxch[q % 2]
            watt_load(wo_t, 0, 0)
            watt_load(wo_t, 1, 1)
            for ct in range(CT):
                slot = ct % 6
                if ct + 2 < CT:
                    watt_load(wo_t, ct + 2, (ct + 2) % 6)
                po = 2 + ct % 2
                with b.group("pe", s_pe):
                    for dt in range(CT):
                        b.op("pe", lambda E, ct=ct, dt=dt, slot=slot, po=po: E.matmul(
                            out=pb[po][:, :], lhsT=watt_ap[slot][:, dt * 128:(dt + 1) * 128],
                            rhs=srr_v[:, dt * TCH:(dt + 1) * TCH],
                            start=(dt == 0), stop=(dt == CT - 1)),
                            reads=[bwatt[slot], bsrr], writes=[bpb[po]])
                with b.group("dve", s_dve):
                    b.op("dve", lambda E, ct=ct, po=po: E.tensor_tensor(
                        out=xch[:, ct * TCH:(ct + 1) * TCH],
                        in0=xch[:, ct * TCH:(ct + 1) * TCH],
                        in1=pb[po][:, :], op=OP.add),
                        reads=[bpb[po]], writes=[bx])
                if fillers:
                    fillers.pop(0)()
            for f in fillers:
                f()
            dma("act", s_st[q % 2], xbuf[q], xch[:, :], [bx], [bdram_x[q]])

        ssg_b = ssg.ap().bitcast(BF16)

        def gu_load01():
            dma("sync", s_wgu[0], wgu_s[0][:, :], wgu_full[0:128, :],
                [bdram_w[("gu", 0)]], [bwgu[0]])
            dma("sync", s_wgu[1], wgu_s[1][:, :], wgu_full[IT * 128:(IT + 1) * 128, :],
                [bdram_w[("gu", IT)]], [bwgu[1]])

        def ffn_hooks(q):
            if q + 1 >= NCH:
                return None, None, None
            nxt = xch_t[(q + 1) % 2]
            bnxt = bxch[(q + 1) % 2]

            def fills():
                def mk(ct):
                    def f():
                        with b.group("dve", s_dve):
                            b.op("dve", lambda E, ct=ct: E.tensor_scalar(
                                out=xnb[:, ct * TCH:(ct + 1) * TCH],
                                in0=nxt[:, ct * TCH:(ct + 1) * TCH],
                                scalar1=1.0, scalar2=None, op0=OP.mult),
                                reads=[bnxt], writes=[bxnb, bxnb4[ct // 4]])
                        with b.group("pe", s_pe):
                            b.op("pe", lambda E, ct=ct: E.matmul(
                                out=pb[0][0:1, :], lhsT=ones_b[:, :],
                                rhs=xnb[:, ct * TCH:(ct + 1) * TCH],
                                start=(ct == 0), stop=(ct == CT - 1)),
                                reads=[bxnb4[ct // 4], bconsts], writes=[bpb[0]])
                        with b.group("act", s_act):
                            b.op("act", lambda E, ct=ct: E.activation(
                                out=ssg_b[:, 0:TCH],
                                in_=nxt[:, ct * TCH:(ct + 1) * TCH], func=AF.Square),
                                reads=[bnxt], writes=[bssg])
                        with b.group("pe", s_pe):
                            b.op("pe", lambda E, ct=ct: E.matmul(
                                out=pb[1][0:1, :], lhsT=ones_b[:, :],
                                rhs=ssg_b[:, 0:TCH],
                                start=(ct == 0), stop=(ct == CT - 1)),
                                reads=[bssg, bconsts], writes=[bpb[1]])
                    return f
                return [mk(ct) for ct in range(CT)]

            def fin1():
                with b.group("dve", s_dve):
                    b.op("dve", lambda E: E.tensor_scalar(
                        out=m_row, in0=pb[0][0:1, :], scalar1=RECIP_C, scalar2=None,
                        op0=OP.mult), reads=[bpb[0]], writes=[bsrow])
                    b.op("dve", lambda E: E.tensor_scalar(
                        out=rstd_row, in0=pb[1][0:1, :], scalar1=RECIP_C, scalar2=None,
                        op0=OP.mult), reads=[bpb[1]], writes=[bsrow])
                    b.op("dve", lambda E: E.scalar_tensor_tensor(
                        out=mr_row, in0=m_row, scalar=1.0, in1=m_row,
                        op0=OP.mult, op1=OP.mult), reads=[bsrow], writes=[bsrow])
                    b.op("dve", lambda E: E.tensor_tensor(
                        out=rstd_row, in0=rstd_row, in1=mr_row, op=OP.subtract),
                        reads=[bsrow], writes=[bsrow])
                with b.group("act", s_act):
                    b.op("act", lambda E: E.activation(
                        out=rstd_row, in_=rstd_row, func=AF.Ln, bias=bias_eps5[0:1, :]),
                        reads=[bsrow, bconsts], writes=[bsrow])
                    b.op("act", lambda E: E.activation(
                        out=rstd_row, in_=rstd_row, func=AF.Exp, scale=-0.5),
                        reads=[bsrow], writes=[bsrow])
                with b.group("dve", s_dve):
                    b.op("dve", lambda E: E.tensor_tensor(
                        out=mr_row, in0=m_row, in1=rstd_row, op=OP.mult),
                        reads=[bsrow], writes=[bsrow])

            def fin2():
                with b.group("pe", s_pe):
                    b.op("pe", lambda E: E.matmul(
                        out=pb[2][:, :], lhsT=ones_f[:, :], rhs=rstd_row,
                        start=True, stop=True), reads=[bsrow, bconsts], writes=[bpb[2]])
                with b.group("pe", s_pe):
                    b.op("pe", lambda E: E.matmul(
                        out=pb[3][:, :], lhsT=ones_f[:, :], rhs=mr_row,
                        start=True, stop=True), reads=[bsrow, bconsts], writes=[bpb[3]])

            def xn(gi):
                def f():
                    with b.group("dve", s_dve):
                        for k in range(4):
                            ct = gi * 4 + k
                            b.op("dve", lambda E, ct=ct: E.tensor_tensor(
                                out=stats[:, :], in0=nxt[:, ct * TCH:(ct + 1) * TCH],
                                in1=pb[2][:, :], op=OP.mult),
                                reads=[bnxt, bpb[2]], writes=[bstats])
                            b.op("dve", lambda E, ct=ct: E.tensor_tensor(
                                out=xnb[:, ct * TCH:(ct + 1) * TCH], in0=stats[:, :],
                                in1=pb[3][:, :], op=OP.subtract),
                                reads=[bstats, bpb[3]], writes=[bxnb, bxnb4[ct // 4]])
                return f

            def all_fills():
                return fills() + [fin1, fin2] + [xn(gi) for gi in range(4)]

            return all_fills, gu_load01, None

        emit_x_load(0, x_in, bxch[0])
        for q in range(NCH):
            if "f1" in phases:
                hf, ht_, hx = ffn_hooks(q)
                hxl = (lambda q=q: emit_x_load(q + 1, x_in, bxch[(q + 1) % 2])) \
                    if q + 1 < NCH else None
                emit_ffn(q, True, xbuf, bdram_x, la_done=(q > 0),
                         hook_fills=hf, hook_tail=ht_, hook_xn=hx,
                         hook_xload=hxl)
            else:
                if q + 1 < NCH:
                    emit_x_load(q + 1, x_in, bxch[(q + 1) % 2])
                dma("act", s_st[q % 2], xbuf[q], xch_t[q % 2][:, :],
                    [bxch[q % 2]], [bdram_x[q]])
        alias_handoff([bxk, bxv, bxr], [bh])
        alias_handoff([bsrr, bscan, b_ek, b_euk, b_sr, b_ekv, b_euv, b_U, b_D], [bysb])
        for i in range(3):
            alias_handoff([bwatt[i]], [bwgu[i]])
        alias_handoff([bwatt[3], bwatt[4]], [bwdn[0]])
        alias_handoff([bwatt[5]], [bwdn[1]])
        if "at" in phases:
            dma("sync", s_xch[0], xch_t[0][:, :], xbuf[0], [bdram_x[0]], [bxch[0]])
            emit_attn_pre(0)
            for q in range(NCH):
                emit_attn_main(q)
                fill = []
                if q + 1 < NCH:
                    dma("sync", s_xch[(q + 1) % 2], xch_t[(q + 1) % 2][:, :],
                        xbuf[q + 1], [bdram_x[q + 1]], [bxch[(q + 1) % 2]])
                    emit_attn_ln(q + 1)
                    fill = mix_ct_fns(q + 1)
                emit_attn_wo(q, fill)
        alias_handoff([bh], [bxk, bxv, bxr])
        alias_handoff([bysb], [bsrr, bscan, b_ek, b_euk, b_sr, b_ekv, b_euv, b_U, b_D])
        for i in range(3):
            alias_handoff([bwgu[i]], [bwatt[i]])
        alias_handoff([bwdn[0]], [bwatt[3], bwatt[4]])
        alias_handoff([bwdn[1]], [bwatt[5]])
        by_out = [Buf(f"yout{i}") for i in range(NCH)]
        dma("sync", s_xch[0], xch_t[0][:, :], xbuf[0], [bdram_x[0]], [bxch[0]])
        for q in range(NCH):
            if "f2" in phases:
                hf, ht_, hx = ffn_hooks(q)
                hxl = (lambda q=q: dma(
                    "sync", s_xch[(q + 1) % 2], xch_t[(q + 1) % 2][:, :],
                    xbuf[q + 1], [bdram_x[q + 1]], [bxch[(q + 1) % 2]])) \
                    if q + 1 < NCH else None
                emit_ffn(q, False, y_out, by_out, la_done=(q > 0),
                         hook_fills=hf, hook_tail=ht_, hook_xn=hx,
                         hook_xload=hxl)
            else:
                if q + 1 < NCH:
                    dma("sync", s_xch[(q + 1) % 2], xch_t[(q + 1) % 2][:, :],
                        xbuf[q + 1], [bdram_x[q + 1]], [bxch[(q + 1) % 2]])
                dma("act", s_st[q % 2], y_out[q], xch_t[q % 2][:, :],
                    [bxch[q % 2]], [by_out[q]])

        for s in (s_st[0], s_st[1], s_dqst2[0], s_dqst2[1], s_dqpk[0], s_dqpk[1], s_init, s_cc):
            b.raw_wait("gp", s, s.planned)

        for s in b.sems.values():
            s.h = es.enter_context(nc.semaphore(s.name))
        blk = es.enter_context(nc.Block())
        blk.tensor(lambda E: _run_plan(b, "pe", E))
        blk.vector(lambda E: _run_plan(b, "dve", E))
        blk.scalar(lambda E: _run_plan(b, "act", E))
        blk.sync(lambda E: _run_plan(b, "sync", E))
        blk.gpsimd(lambda E: _run_plan(b, "gp", E))

    return nc



def build_in_maps(inp, TT=T):
    x = np.asarray(inp["L0_attn_x"])
    B = x.shape[0]
    wk_t = _tile_lhsT(np.ascontiguousarray(np.asarray(inp["L0_attn_Wk"]).T[ORDER, :]))
    wv_t = _tile_lhsT(np.ascontiguousarray(np.asarray(inp["L0_attn_Wv"]).T[ORDER, :]))
    wr_t = _tile_lhsT(np.ascontiguousarray(np.asarray(inp["L0_attn_Wr"]).T[ORDER, :]))
    wo_t = _tile_lhsT(np.ascontiguousarray(np.asarray(inp["L0_attn_Wo"]).T[:, ORDER]))
    _gupk = np.asarray(inp["L0_ffn_gate_up_packed"])
    gu_pk = np.ascontiguousarray(
        np.concatenate([_gupk[0:IDIM][ORDER_I], _gupk[IDIM:][ORDER_I]], axis=0))
    dn_pk = np.ascontiguousarray(np.asarray(inp["L0_ffn_down_packed"])[ORDER, :])
    gsc0 = np.asarray(inp["L0_ffn_gate_up_scales"])
    gsc = np.concatenate([gsc0[:, 0:IDIM][:, ORDER_I], gsc0[:, IDIM:][:, ORDER_I]], axis=1)
    gu_sc = np.ascontiguousarray(
        gsc.T.reshape(GU_UNITS, 128, 16).transpose(1, 0, 2).reshape(128, GU_UNITS * 16))
    dsc = np.asarray(inp["L0_ffn_down_scales"])[:, ORDER]
    dn_sc = np.ascontiguousarray(
        dsc.T.reshape(CT, 128, 44).transpose(1, 0, 2).reshape(128, CT * 44))
    vt = {"v_td": _vec_tile(np.asarray(inp["L0_attn_time_decay"])),
          "v_tf": _vec_tile(np.asarray(inp["L0_attn_time_first"])),
          "v_tmk": _vec_tile(np.asarray(inp["L0_attn_time_mix_k"])[ORDER]),
          "v_tmv": _vec_tile(np.asarray(inp["L0_attn_time_mix_v"])[ORDER]),
          "v_tmr": _vec_tile(np.asarray(inp["L0_attn_time_mix_r"])[ORDER])}
    ident = np.eye(128, dtype=ml_dtypes.bfloat16)
    in_maps = []
    for c in range(B):
        im = {"x_in": _chunks_fm(np.ascontiguousarray(x[c][:TT, ORDER])),
              "gu_pk": gu_pk, "gu_sc": gu_sc, "dn_pk": dn_pk, "dn_sc": dn_sc,
              "wk_t": wk_t, "wv_t": wv_t, "wr_t": wr_t, "wo_t": wo_t,
              "ident_in": ident}
        im.update(vt)
        in_maps.append(im)
    return in_maps


def kernel(L0_attn_x, L0_attn_time_decay, L0_attn_time_first, L0_attn_time_mix_k,
           L0_attn_time_mix_v, L0_attn_time_mix_r, L0_attn_Wk, L0_attn_Wv,
           L0_attn_Wr, L0_attn_Wo, L0_ffn_gate_up_packed, L0_ffn_gate_up_scales,
           L0_ffn_down_packed, L0_ffn_down_scales):
    inp = dict(L0_attn_x=L0_attn_x, L0_attn_time_decay=L0_attn_time_decay,
               L0_attn_time_first=L0_attn_time_first, L0_attn_time_mix_k=L0_attn_time_mix_k,
               L0_attn_time_mix_v=L0_attn_time_mix_v, L0_attn_time_mix_r=L0_attn_time_mix_r,
               L0_attn_Wk=L0_attn_Wk, L0_attn_Wv=L0_attn_Wv, L0_attn_Wr=L0_attn_Wr,
               L0_attn_Wo=L0_attn_Wo, L0_ffn_gate_up_packed=L0_ffn_gate_up_packed,
               L0_ffn_gate_up_scales=L0_ffn_gate_up_scales,
               L0_ffn_down_packed=L0_ffn_down_packed,
               L0_ffn_down_scales=L0_ffn_down_scales)
    B = np.asarray(L0_attn_x).shape[0]
    nc = build_nc(T, shard=False)
    in_maps = build_in_maps(inp, T)
    res = run_bass_kernel_spmd(nc, in_maps, core_ids=list(range(B)))
    out = np.empty((B, T, CDIM), dtype=np.float32)
    for c in range(B):
        out[c][:, ORDER] = _unchunk_fm(res.results[c]["y_out"])
    return out
